# version 02
# baseline (speedup 1.0000x reference)
import numpy as np
import concourse.bass as bass
import concourse.mybir as mybir
from concourse.bass_utils import run_bass_kernel_spmd

F32 = mybir.dt.float32
BF16 = mybir.dt.bfloat16
AF = mybir.ActivationFunctionType
ALU = mybir.AluOpType
AX = mybir.AxisListType

NCORE = 8
B, S, D = 4, 4096, 1024
SL = 2048
NH, HD = 16, 64
E, H = 8, 4096
NDC = D // 128
T = 512
NTT = SL // T
HC = 512
NHC = H // HC
SF = 4096
NKC = SF // 128
NQC = SL // 128
SCALE = 0.125
EPS = 1e-5


class Sem:
    def __init__(self, nc, ctx, name):
        self.h = ctx.enter_context(nc.semaphore(name))
        self.n = 0

    def inc(self, inst, k=1):
        inst.then_inc(self.h, k)
        self.n += k
        return self.n


def build_program(stop_after="all"):
    nc = bass.Bass("TRN2", target_bir_lowering=False, detect_race_conditions=False)

    xT = nc.dram_tensor("xT", [D, SL], F32, kind="ExternalInput").ap()
    wqT = nc.dram_tensor("wqT", [D, D], F32, kind="ExternalInput").ap()
    wkT = nc.dram_tensor("wkT", [D, D], F32, kind="ExternalInput").ap()
    wvT = nc.dram_tensor("wvT", [D, D], F32, kind="ExternalInput").ap()
    woT = nc.dram_tensor("woT", [D, D], F32, kind="ExternalInput").ap()
    bqd = nc.dram_tensor("bq", [D], F32, kind="ExternalInput").ap()
    bkd = nc.dram_tensor("bk", [D], F32, kind="ExternalInput").ap()
    bvd = nc.dram_tensor("bv", [D], F32, kind="ExternalInput").ap()
    bod = nc.dram_tensor("bo", [D], F32, kind="ExternalInput").ap()
    wgd = nc.dram_tensor("wg", [D, E], F32, kind="ExternalInput").ap()
    bgd = nc.dram_tensor("bg", [E], F32, kind="ExternalInput").ap()
    w1d = nc.dram_tensor("w1", [E, D, H], F32, kind="ExternalInput").ap()
    w2d = nc.dram_tensor("w2", [E, H, D], F32, kind="ExternalInput").ap()
    w3d = nc.dram_tensor("w3", [E, D, H], F32, kind="ExternalInput").ap()
    outT = nc.dram_tensor("outT", [D, SL], F32, kind="ExternalOutput").ap()

    kexp = nc.dram_tensor("kexp", [D, SL], BF16).ap()
    vexp = nc.dram_tensor("vexp", [SL, D], BF16).ap()
    kg = nc.dram_tensor("kg", [2 * D, SL], BF16).ap()
    vg = nc.dram_tensor("vg", [SF, D], BF16).ap()
    cdram = nc.dram_tensor("cdram", [E, SL], BF16).ap()

    from contextlib import ExitStack
    with ExitStack() as ctx:
        BASE = 16512
        off = [BASE]

        def At(name, shape, dt_, offset=None):
            nbytes = int(np.prod(shape[1:])) * mybir.dt.size(dt_)
            if offset is None:
                offset = off[0]
                off[0] += (nbytes + 31) // 32 * 32
            return nc.alloc_sbuf_tensor_at(name, list(shape), dt_, offset=offset).ap()

        xfm = At("xfm", [128, NDC * SL], F32)
        ub = At("ub", [128, NDC * SL], BF16)
        qoz = At("qoz", [128, NDC * SL], BF16)
        wreg = off[0]
        w1t = [At(f"w1t{p}", [128, NDC * HC], BF16, wreg + p * 24576) for p in range(2)]
        w3t = [At(f"w3t{p}", [128, NDC * HC], BF16, wreg + p * 24576 + 8192) for p in range(2)]
        w2t = [At(f"w2t{p}", [128, 4 * D], BF16, wreg + p * 24576 + 16384) for p in range(2)]
        wta = At("wta", [128, NDC * D], BF16, wreg)
        kh = At("kh", [128, SF], BF16, wreg + 16384)
        vh = [At(f"vh{p}", [128, NKC * 65], BF16, wreg + 24576 + p * 4288) for p in range(2)]
        kvx = At("kvx", [128, SL], BF16, wreg + 33152)
        att = At("att", [128, 2 * 512], BF16, wreg + 37248)
        bvrowb = At("bvrowb", [1, D], BF16, wreg + 39296)
        off[0] = wreg + 49152
        screg = off[0]
        sq = [At(f"sq{p}", [128, T], F32, screg + p * 2048) for p in range(2)]
        uf32 = [At(f"uf32{p}", [128, T], F32, screg + 4096 + p * 2048) for p in range(2)]
        hbuf = At("hbuf", [128, 4 * T], BF16, screg + 8192)
        s1 = [At(f"s1{p}", [128, T], F32, screg + 12288 + p * 2048) for p in range(2)]
        mrow = At("mrow", [1, SL], BF16, screg)
        mbc = At("mbc", [128, 512], BF16, screg + 4096)
        rsb = At("rsb", [128, 512], F32, screg + 5120)
        srow = At("srow", [128, 512], F32, screg + 7168)
        obuf = At("obuf", [64, 512], BF16, screg + 9216)
        off[0] = screg + 16384
        rowbuf = At("rowbuf", [1, 4 * T], F32)
        ctoff = off[0]
        cT = At("cT", [8, SL], BF16)
        crow = At("crow", [1, SL], BF16, offset=ctoff)
        battn = At("battn", [128, 4 * NDC], F32)
        onesf = At("onesf", [128, 128], F32)
        onesb = At("onesb", [1, 128], BF16)
        idf = At("idf", [128, 128], F32)
        idb = At("idb", [128, 128], BF16)
        wg_sb = At("wg_sb", [128, NDC * E], F32)
        g1b = At("g1b", [128, E], F32)
        bgb = At("bgb", [128, E], F32)
        grow = At("grow", [1, 2 * E], F32)
        gsc = At("gsc", [128, 4 * E], F32)
        gcol = At("gcol", [128, 8], F32)
        assert off[0] <= 229344, off[0]

        ROW_TMP, ROW_R, ROW_MR, ROW_NM = (rowbuf[0:1, k * T:(k + 1) * T] for k in range(4))

        pA = ctx.enter_context(nc.psum_tensor("pA", [128, 2048], F32))
        pB = ctx.enter_context(nc.psum_tensor("pB", [128, 2048], F32))
        pAa, pBa = pA[:, :], pB[:, :]
        pAb = pA[:, :].bitcast(BF16)

        def bank(ap_, i):
            return ap_[:, i * 512:(i + 1) * 512]

        names = ["qsem", "dsem", "tsem", "vsem", "asem", "ck", "cv", "esem",
                 "ws0", "ws1", "ks0", "ks1"]
        SM = {nm: Sem(nc, ctx, nm) for nm in names}
        qsem, dsem, tsem, vsem, asem, ck, cv, esem, ws0, ws1, ks0, ks1 = \
            (SM[n] for n in names)
        wsp = [ws0, ws1]
        ksp = [ks0, ks1]
        block = ctx.enter_context(nc.Block())

        @block.gpsimd
        def _(g):
            g.memset(onesf[:, :], 1.0)
            g.memset(onesb[:, :], 1.0)
            g.memset(idf[:, :], 1.0)
            esem.inc(g.memset(idb[:, :], 1.0))
            g.wait_ge(esem.h, 1)
            g.affine_select(idf[:, :], idf[:, :], [[1, 128]], ALU.is_equal, 0.0,
                            base=0, channel_multiplier=-1)
            esem.inc(g.affine_select(idb[:, :], idb[:, :], [[1, 128]], ALU.is_equal, 0.0,
                                     base=0, channel_multiplier=-1))
            with nc.allow_non_contiguous_dma("tiny one-time bias loads"):
                for j, bb in enumerate([bqd, bkd, bvd, bod]):
                    g.dma_start(out=battn[:, j * NDC:(j + 1) * NDC],
                                in_=bb.rearrange("(dc p) -> p dc", p=128),
                                allow_slow_non_contiguous=True).then_inc(dsem.h, 16)
            g.dma_start(out=wg_sb[:, :].rearrange("p (dc e) -> p dc e", e=E),
                        in_=wgd.rearrange("(dc p) e -> p dc e", p=128)).then_inc(dsem.h, 16)
            g.dma_start(out=grow[0:1, E:2 * E], in_=bgd[None, :]).then_inc(dsem.h, 16)
            dsem.n += 6 * 16

        @block.sync
        def _(sy):
            for dc in range(NDC):
                sy.dma_start(out=xfm[:, dc * SL:(dc + 1) * SL],
                             in_=xT[dc * 128:(dc + 1) * 128, :]).then_inc(qsem.h, 16)
            qsem.n += NDC * 16
        q_x = qsem.n

        @block.tensor
        def _(t):
            t.wait_ge(dsem.h, dsem.n)
            t.wait_ge(esem.h, 2)
            t.wait_ge(qsem.h, q_x)
            for dc in range(NDC):
                ins = t.matmul(pBa[0:1, 0:E], onesf[:, 0:1], wg_sb[:, dc * E:(dc + 1) * E],
                               start=(dc == 0), stop=(dc == NDC - 1))
            tsem.inc(ins)
        ev0 = tsem.n

        @block.vector
        def _(v):
            v.wait_ge(tsem.h, ev0)
            vsem.inc(v.tensor_copy(grow[0:1, 0:E], pBa[0:1, 0:E]))
        ev1 = vsem.n

        @block.tensor
        def _(t):
            t.wait_ge(vsem.h, ev1)
            tsem.inc(t.matmul(pAa[:, 0:2 * E], onesf[0:1, 0:128], grow[0:1, :],
                              start=True, stop=True))
        ev2 = tsem.n

        @block.vector
        def _(v):
            v.wait_ge(tsem.h, ev2)
            v.tensor_copy(g1b[:, :], pAa[:, 0:E])
            vsem.inc(v.tensor_copy(bgb[:, :], pAa[:, E:2 * E]))

        def gate(tt, v_rows):
            for tc in range(4):
                vg0 = vsem.n
                tok0 = tt * T + tc * 128

                @block.tensor
                def _(t, tt=tt, tc=tc, tok0=tok0, vg0=vg0, v_rows=v_rows):
                    if tc == 0:
                        t.wait_ge(vsem.h, v_rows)
                    else:
                        t.wait_ge(vsem.h, vg0)
                    for dc in range(NDC):
                        ins = t.matmul(pBa[0:128, 0:E], xfm[:, dc * SL + tok0:][:, 0:128],
                                       wg_sb[:, dc * E:(dc + 1) * E],
                                       start=(dc == 0), stop=(dc == NDC - 1))
                    tsem.inc(ins)
                    tsem.inc(t.transpose(pAa[0:128, 1536:1537],
                                         ROW_NM[0:1, tc * 128:(tc + 1) * 128], onesf[0:1, 0:1]))
                    tsem.inc(t.transpose(pAa[0:128, 1537:1538],
                                         ROW_R[0:1, tc * 128:(tc + 1) * 128], onesf[0:1, 0:1]))
                t_g = tsem.n

                @block.vector
                def _(v, t_g=t_g):
                    def vf(ins):
                        vsem.inc(ins)
                        v.wait_ge(vsem.h, vsem.n)
                    v.wait_ge(tsem.h, t_g)
                    vf(v.tensor_copy(gcol[:, 6:8], pAa[:, 1536:1538]))
                    vf(v.scalar_tensor_tensor(gsc[:, 0:E], g1b[:, :], gcol[:, 6:7],
                                              pBa[0:128, 0:E], ALU.mult, ALU.add))
                    vf(v.tensor_scalar(gsc[:, 0:E], gsc[:, 0:E], gcol[:, 7:8], None, ALU.mult))
                    vf(v.tensor_add(gsc[:, 0:E], gsc[:, 0:E], bgb[:, :]))
                    vf(v.reduce_max(gcol[:, 0:1], gsc[:, 0:E], axis=AX.X))
                    vf(v.tensor_scalar_mul(gcol[:, 1:2], gcol[:, 0:1], -1.0))
                    vf(v.tensor_scalar(gsc[:, E:2 * E], gsc[:, 0:E], gcol[:, 0:1], None, ALU.is_ge))
                    vf(v.scalar_tensor_tensor(gsc[:, 2 * E:3 * E], gsc[:, E:2 * E], -1e30,
                                              gsc[:, 0:E], ALU.mult, ALU.add))
                    vsem.inc(v.reduce_max(gcol[:, 2:3], gsc[:, 2 * E:3 * E], axis=AX.X))
                v_g1 = vsem.n

                @block.scalar
                def _(s, v_g1=v_g1):
                    s.wait_ge(vsem.h, v_g1)
                    s.activation(gsc[:, 2 * E:3 * E], gsc[:, 0:E], AF.Exp, bias=gcol[:, 1:2])
                    asem.inc(s.activation(gcol[:, 3:4], gcol[:, 2:3], AF.Exp, bias=gcol[:, 1:2]))
                a_g = asem.n

                @block.vector
                def _(v, a_g=a_g):
                    def vf(ins):
                        vsem.inc(ins)
                        v.wait_ge(vsem.h, vsem.n)
                    v.wait_ge(asem.h, a_g)
                    vf(v.tensor_scalar_add(gcol[:, 4:5], gcol[:, 3:4], 1.0))
                    vf(v.reciprocal(gcol[:, 5:6], gcol[:, 4:5]))
                    vf(v.tensor_scalar(gsc[:, E:2 * E], gsc[:, 0:E], gcol[:, 2:3], None, ALU.is_ge))
                    vf(v.tensor_mul(gsc[:, 2 * E:3 * E], gsc[:, 2 * E:3 * E],
                                    gsc[:, E:2 * E]))
                    vsem.inc(v.tensor_scalar(gsc[:, 3 * E:4 * E].bitcast(BF16)[:, 0:E],
                                             gsc[:, 2 * E:3 * E], gcol[:, 5:6], None, ALU.mult))
                v_g2 = vsem.n

                @block.tensor
                def _(t, v_g2=v_g2):
                    t.wait_ge(vsem.h, v_g2)
                    tsem.inc(t.transpose(pAb[0:8, 3072:3200],
                                         gsc[:, 3 * E:4 * E].bitcast(BF16)[:, 0:E], idb[:, :]))
                t_c = tsem.n

                @block.vector
                def _(v, t_c=t_c, tok0=tok0):
                    v.wait_ge(tsem.h, t_c)
                    vsem.inc(v.tensor_copy(cT[0:8, tok0:tok0 + 128], pAb[0:8, 3072:3200]))

        def layernorm(fold_x, with_gate):
            v_enter = vsem.n
            prev_vu = [v_enter]
            for tt in range(NTT):
                av0 = asem.n
                tv0 = tsem.n

                @block.scalar
                def _(s, tt=tt, tv0=tv0, v_enter=v_enter):
                    if tt == 0:
                        s.wait_ge(vsem.h, v_enter)
                        s.wait_ge(qsem.h, q_x)
                    for dc in range(NDC):
                        if dc >= 2:
                            s.wait_ge(tsem.h, tv0 + dc - 1)
                        asem.inc(s.activation(sq[dc % 2][:, :],
                                              xfm[:, dc * SL + tt * T:][:, 0:T], AF.Square))

                @block.tensor
                def _(t, tt=tt, av0=av0, wait_v=prev_vu[0]):
                    t.wait_ge(vsem.h, wait_v)
                    for dc in range(NDC):
                        x_sl = xfm[:, dc * SL + tt * T:][:, 0:T]
                        t.matmul(pBa[0:1, 1024:1536], onesf[:, 0:1], x_sl,
                                 start=(dc == 0), stop=(dc == NDC - 1))
                        t.wait_ge(asem.h, av0 + dc + 1)
                        tsem.inc(t.matmul(pBa[0:1, 1536:2048], onesf[:, 0:1], sq[dc % 2][:, :],
                                          start=(dc == 0), stop=(dc == NDC - 1)))
                t_sums = tsem.n

                @block.vector
                def _(v, t_sums=t_sums):
                    v.wait_ge(tsem.h, t_sums)
                    v.tensor_scalar_mul(pBa[0:1, 1024:1536], pBa[0:1, 1024:1536], 1.0 / D)
                    vsem.inc(v.tensor_scalar_mul(pBa[0:1, 1536:2048], pBa[0:1, 1536:2048], 1.0 / D))
                v_mean = vsem.n

                @block.scalar
                def _(s, v_mean=v_mean):
                    s.wait_ge(vsem.h, v_mean)
                    asem.inc(s.activation(ROW_TMP, pBa[0:1, 1024:1536], AF.Square))
                a_msq = asem.n

                @block.vector
                def _(v, a_msq=a_msq):
                    v.wait_ge(asem.h, a_msq)
                    v.tensor_sub(pBa[0:1, 1536:2048], pBa[0:1, 1536:2048], ROW_TMP)
                    vsem.inc(v.tensor_scalar_add(pBa[0:1, 1536:2048], pBa[0:1, 1536:2048], EPS))
                v_var = vsem.n

                @block.scalar
                def _(s, v_var=v_var):
                    s.wait_ge(vsem.h, v_var)
                    asem.inc(s.activation(ROW_TMP, pBa[0:1, 1536:2048], AF.Ln))
                a_ln = asem.n

                @block.vector
                def _(v, a_ln=a_ln):
                    v.wait_ge(asem.h, a_ln)
                    vsem.inc(v.tensor_scalar_mul(ROW_TMP, ROW_TMP, -0.5))
                v_ln = vsem.n

                @block.scalar
                def _(s, v_ln=v_ln):
                    s.wait_ge(vsem.h, v_ln)
                    asem.inc(s.activation(ROW_R, ROW_TMP, AF.Exp))
                a_r = asem.n

                @block.vector
                def _(v, a_r=a_r, with_gate=with_gate):
                    v.wait_ge(asem.h, a_r)
                    v.tensor_mul(ROW_MR, pBa[0:1, 1024:1536], ROW_R)
                    if with_gate:
                        vsem.inc(v.tensor_scalar_mul(ROW_NM, pBa[0:1, 1024:1536], -1.0))
                    else:
                        vsem.inc(v.engine_nop())
                v_rows = vsem.n

                if with_gate:
                    gate(tt, v_rows)

                @block.tensor
                def _(t, v_rows=v_rows):
                    t.wait_ge(vsem.h, v_rows)
                    tsem.inc(t.matmul(pBa[:, 1024:1536], onesf[0:1, 0:128], ROW_R,
                                      start=True, stop=True))
                    tsem.inc(t.matmul(pBa[:, 1536:2048], onesf[0:1, 0:128], ROW_MR,
                                      start=True, stop=True))
                t_bc = tsem.n
                vu0 = vsem.n
                au0 = asem.n

                @block.vector
                def _(v, tt=tt, t_bc=t_bc, fold_x=fold_x, au0=au0):
                    v.wait_ge(tsem.h, t_bc)
                    for dc in range(NDC):
                        x_sl = xfm[:, dc * SL + tt * T:][:, 0:T]
                        p = dc % 2
                        if dc >= 2:
                            v.wait_ge(asem.h, au0 + dc - 1)
                        v.tensor_mul(uf32[p][:, :], x_sl, pBa[:, 1024:1536])
                        vsem.inc(v.tensor_sub(uf32[p][:, :], uf32[p][:, :], pBa[:, 1536:2048]))
                        if fold_x:
                            vsem.inc(v.scalar_tensor_tensor(x_sl, uf32[p][:, :], 0.5, x_sl,
                                                            ALU.mult, ALU.add))
                        else:
                            vsem.inc(v.engine_nop())

                @block.scalar
                def _(s, tt=tt, vu0=vu0):
                    for dc in range(NDC):
                        u_sl = ub[:, dc * SL + tt * T:][:, 0:T]
                        s.wait_ge(vsem.h, vu0 + 2 * dc + 1)
                        asem.inc(s.copy(u_sl, uf32[dc % 2][:, :]))
                prev_vu[0] = vsem.n

        def ffn_experts(call):
            t0 = tsem.n
            v0 = vsem.n
            a0 = asem.n
            a_enter = asem.n
            v_pre = vsem.n
            d0w = [wsp[0].n, wsp[1].n]

            @block.sync
            def _(sy, v_pre=v_pre):
                sy.wait_ge(vsem.h, v_pre)
                sy.dma_start(out=cdram[:, :], in_=cT[0:8, :]).then_inc(qsem.h, 16)
                qsem.n += 16
            q_cflat = qsem.n

            def t_base(e):
                return t0 + e * (4 + 32 * 16)

            def t_cb(e, tt):
                return t_base(e) + tt + 1

            def t_blk(e, hc, tt):
                return t_base(e) + 4 + (hc * NTT + tt) * 16

            def t_h1(e, hc, tt, i):
                return t_blk(e, hc, tt) + 2 * i + 1

            def t_h3(e, hc, tt, i):
                return t_blk(e, hc, tt) + 2 * i + 2

            def t_y(e, hc, tt, dc):
                return t_blk(e, hc, tt) + 8 + dc + 1

            def v_base(e):
                return v0 + e * (4 + 32 * 12)

            def v_z(e, tt):
                return v_base(e) + tt + 1

            def v_blkbase(e, hc, tt):
                return v_base(e) + 4 + (hc * NTT + tt) * 12

            def v_hmul(e, hc, tt, i):
                return v_blkbase(e, hc, tt) + i + 1

            def v_yadd(e, hc, tt, dc):
                return v_blkbase(e, hc, tt) + 4 + dc + 1

            def lin_h(e, hc, tt, i):
                return ((e * NHC + hc) * NTT + tt) * 4 + i

            def lin_y(e, hc, tt, dc):
                return ((e * NHC + hc) * NTT + tt) * 8 + dc

            def unlin_h(k):
                return (k // (4 * NTT * NHC), (k // (4 * NTT)) % NHC,
                        (k // 4) % NTT, k % 4)

            def unlin_y(k):
                return (k // (8 * NTT * NHC), (k // (8 * NTT)) % NHC,
                        (k // 8) % NTT, k % 8)

            def a_silu(e, hc, tt, i):
                return a0 + ((e * NHC + hc) * NTT + tt) * 4 + i + 1

            @block.gpsimd
            def _(g, call=call, t0=t0):
                for c in range(E * NHC):
                    e, hc = divmod(c, NHC)
                    p = c % 2
                    if c < 2:
                        g.wait_ge(tsem.h, t0)
                    else:
                        ee, hh = divmod(c - 2, NHC)
                        g.wait_ge(tsem.h, t_y(ee, hh, NTT - 1, 7))
                    g.dma_start(out=w1t[p].rearrange("p (dc h) -> p dc h", h=HC),
                                in_=w1d[e][:, hc * HC:(hc + 1) * HC]
                                .rearrange("(dc p) h -> p dc h", p=128)).then_inc(wsp[p].h, 16)
                    g.dma_start(out=w3t[p].rearrange("p (dc h) -> p dc h", h=HC),
                                in_=w3d[e][:, hc * HC:(hc + 1) * HC]
                                .rearrange("(dc p) h -> p dc h", p=128)).then_inc(wsp[p].h, 16)
                    g.dma_start(out=w2t[p].rearrange("p (hh d) -> p hh d", d=D),
                                in_=w2d[e][hc * HC:(hc + 1) * HC, :]
                                .rearrange("(hh p) d -> p hh d", p=128)).then_inc(wsp[p].h, 16)
                    wsp[p].n += 48

            @block.sync
            def _(sy, q_cflat=q_cflat):
                for e in range(E):
                    if e == 0:
                        sy.wait_ge(qsem.h, q_cflat)
                    else:
                        sy.wait_ge(tsem.h, t_cb(e - 1, NTT - 1))
                    sy.dma_start(out=crow[0:1, :], in_=cdram[e:e + 1, :]).then_inc(qsem.h, 16)
                    qsem.n += 16
            q_crow0 = q_cflat

            @block.tensor
            def _(t, call=call, q_crow0=q_crow0, d0w=d0w):
                for e in range(E):
                    for tt in range(NTT):
                        t.wait_ge(qsem.h, q_crow0 + 16 * (e + 1))
                        if tt > 0:
                            t.wait_ge(vsem.h, v_z(e, tt - 1))
                        if e > 0 and tt == 0:
                            t.wait_ge(vsem.h, v_z(e - 1, NTT - 1))
                        tsem.inc(t.matmul(pBa[:, 1024:1536], onesb[:, 0:128],
                                          crow[0:1, tt * T:(tt + 1) * T], start=True, stop=True))
                        assert tsem.n == t_cb(e, tt)
                    for hc in range(NHC):
                        c = e * NHC + hc
                        p = c % 2
                        t.wait_ge(wsp[p].h, d0w[p] + 48 * (c // 2 + 1))
                        for tt in range(NTT):
                            t.wait_ge(vsem.h, v_z(e, tt))
                            for i in range(4):
                                kk = lin_h(e, hc, tt, i)
                                if kk >= 2:
                                    t.wait_ge(vsem.h, v_hmul(*unlin_h(kk - 2)))
                                pp = kk % 2
                                for dc in range(NDC):
                                    ins = t.matmul(bank(pAa, pp),
                                                   w1t[p][:, dc * HC + i * 128:][:, 0:128],
                                                   qoz[:, dc * SL + tt * T:][:, 0:T],
                                                   start=(dc == 0), stop=(dc == NDC - 1))
                                tsem.inc(ins)
                                for dc in range(NDC):
                                    ins = t.matmul(bank(pAa, 2 + pp),
                                                   w3t[p][:, dc * HC + i * 128:][:, 0:128],
                                                   qoz[:, dc * SL + tt * T:][:, 0:T],
                                                   start=(dc == 0), stop=(dc == NDC - 1))
                                tsem.inc(ins)
                                assert tsem.n == t_h3(e, hc, tt, i)
                            t.wait_ge(vsem.h, v_hmul(e, hc, tt, 3))
                            for dc in range(NDC):
                                kk = lin_y(e, hc, tt, dc)
                                if kk >= 2:
                                    t.wait_ge(vsem.h, v_yadd(*unlin_y(kk - 2)))
                                pp = kk % 2
                                for i in range(4):
                                    ins = t.matmul(bank(pBa, pp),
                                                   w2t[p][:, i * D + dc * 128:][:, 0:128],
                                                   hbuf[:, i * T:(i + 1) * T],
                                                   start=(i == 0), stop=(i == 3))
                                tsem.inc(ins)
                                assert tsem.n == t_y(e, hc, tt, dc)

            @block.vector
            def _(v, call=call, a_enter=a_enter):
                for e in range(E):
                    for tt in range(NTT):
                        if e == 0 and tt == 0:
                            v.wait_ge(asem.h, a_enter)
                        v.wait_ge(tsem.h, t_cb(e, tt))
                        if e > 0:
                            v.wait_ge(tsem.h, t_h3(e - 1, NHC - 1, tt, 3))
                        ins = None
                        for dc in range(NDC):
                            ins = v.tensor_mul(qoz[:, dc * SL + tt * T:][:, 0:T],
                                               ub[:, dc * SL + tt * T:][:, 0:T],
                                               pBa[:, 1024:1536])
                        vsem.inc(ins)
                        assert vsem.n == v_z(e, tt)
                    for hc in range(NHC):
                        for tt in range(NTT):
                            for i in range(4):
                                kk = lin_h(e, hc, tt, i)
                                pp = kk % 2
                                v.wait_ge(asem.h, a_silu(e, hc, tt, i))
                                v.wait_ge(tsem.h, t_h3(e, hc, tt, i))
                                vsem.inc(v.tensor_mul(hbuf[:, i * T:(i + 1) * T],
                                                      s1[pp][:, :], bank(pAa, 2 + pp)))
                                assert vsem.n == v_hmul(e, hc, tt, i)
                            for dc in range(NDC):
                                kk = lin_y(e, hc, tt, dc)
                                pp = kk % 2
                                v.wait_ge(tsem.h, t_y(e, hc, tt, dc))
                                x_sl = xfm[:, dc * SL + tt * T:][:, 0:T]
                                vsem.inc(v.scalar_tensor_tensor(x_sl, bank(pBa, pp), 0.5,
                                                                x_sl, ALU.mult, ALU.add))
                                assert vsem.n == v_yadd(e, hc, tt, dc)

            @block.scalar
            def _(s, call=call):
                for e in range(E):
                    for hc in range(NHC):
                        for tt in range(NTT):
                            for i in range(4):
                                kk = lin_h(e, hc, tt, i)
                                pp = kk % 2
                                s.wait_ge(tsem.h, t_h1(e, hc, tt, i))
                                if kk >= 2:
                                    s.wait_ge(vsem.h, v_hmul(*unlin_h(kk - 2)))
                                asem.inc(s.activation(s1[pp][:, :], bank(pAa, pp), AF.Silu))
                                assert asem.n == a_silu(e, hc, tt, i)

        def attention(stop_after="all"):
            a_enter = asem.n
            d_w_wo = [0]
            for wi, wsrc_ in enumerate([wqT, wkT, wvT, woT]):
                t_pre = tsem.n

                @block.gpsimd
                def _(g, wsrc_=wsrc_, t_pre=t_pre, wi=wi):
                    g.wait_ge(tsem.h, t_pre)
                    if wi == 0:
                        for p in range(2):
                            vv = vh[p].rearrange("p (k c) -> p k c", c=65)
                            esem.inc(g.memset(vv[:, :, 64:65], 1.0))
                    if wi == 2:
                        g.dma_start(out=bvrowb[:, :], in_=bvd[None, :]).then_inc(dsem.h, 16)
                        dsem.n += 16
                    g.dma_start(out=wta[:, :].rearrange("p (dc f) -> p dc f", f=D),
                                in_=wsrc_.rearrange("(dc p) f -> p dc f", p=128)
                                ).then_inc(dsem.h, 16)
                    dsem.n += 16
                d_w = dsem.n
                if wi == 3:
                    d_w_wo[0] = d_w

                if wi == 0:
                    a_pre = asem.n

                    @block.tensor
                    def _(t, d_w=d_w, a_pre=a_pre, a_enter=a_enter):
                        t.wait_ge(dsem.h, d_w)
                        t.wait_ge(asem.h, a_enter)
                        for fc in range(NDC):
                            for tt in range(NTT):
                                k = fc * NTT + tt
                                if k >= 2:
                                    t.wait_ge(asem.h, a_pre + k - 1)
                                for dc in range(NDC):
                                    ins = t.matmul(bank(pAa, k % 2),
                                                   wta[:, dc * D + fc * 128:][:, 0:128],
                                                   ub[:, dc * SL + tt * T:][:, 0:T],
                                                   start=(dc == 0), stop=(dc == NDC - 1))
                                tsem.inc(ins)

                    @block.scalar
                    def _(s, t_pre=t_pre):
                        for fc in range(NDC):
                            for tt in range(NTT):
                                k = fc * NTT + tt
                                s.wait_ge(tsem.h, t_pre + k + 1)
                                asem.inc(s.activation(qoz[:, fc * SL + tt * T:][:, 0:T],
                                                      bank(pAa, k % 2), AF.Identity,
                                                      bias=battn[:, 0 * NDC + fc:][:, 0:1]))
                elif wi == 1:
                    a_pre = asem.n
                    q_pre = qsem.n

                    @block.tensor
                    def _(t, d_w=d_w, a_pre=a_pre):
                        t.wait_ge(dsem.h, d_w)
                        t.wait_ge(asem.h, a_pre)
                        for fc in range(NDC):
                            for tt in range(NTT):
                                k = fc * NTT + tt
                                if k >= 2:
                                    t.wait_ge(asem.h, a_pre + k - 1)
                                for dc in range(NDC):
                                    ins = t.matmul(bank(pAa, k % 2),
                                                   wta[:, dc * D + fc * 128:][:, 0:128],
                                                   ub[:, dc * SL + tt * T:][:, 0:T],
                                                   start=(dc == 0), stop=(dc == NDC - 1))
                                tsem.inc(ins)

                    @block.scalar
                    def _(s, t_pre=t_pre, q_pre=q_pre, a_pre=a_pre):
                        for fc in range(NDC):
                            for tt in range(NTT):
                                k = fc * NTT + tt
                                s.wait_ge(tsem.h, t_pre + k + 1)
                                if fc >= 1 and tt == 0:
                                    s.wait_ge(qsem.h, q_pre + 16 * fc)
                                asem.inc(s.activation(kvx[:, tt * T:(tt + 1) * T],
                                                      bank(pAa, k % 2), AF.Identity,
                                                      bias=battn[:, 1 * NDC + fc:][:, 0:1]))

                    @block.sync
                    def _(sy, a_pre=a_pre):
                        for fc in range(NDC):
                            sy.wait_ge(asem.h, a_pre + NTT * (fc + 1))
                            sy.dma_start(out=kexp[fc * 128:(fc + 1) * 128, :],
                                         in_=kvx[:, :]).then_inc(qsem.h, 16)
                            qsem.n += 16
                    q_kexp = qsem.n

                    @block.gpsimd
                    def _(g, q_kexp=q_kexp):
                        g.wait_ge(qsem.h, q_kexp)
                        g.collective_compute(
                            "AllGather", ALU.bypass,
                            replica_groups=[[0, 1], [2, 3], [4, 5], [6, 7]],
                            ins=[kexp.opt()], outs=[kg.opt()]).then_inc(ck.h, 1)
                        ck.n = 1
                elif wi == 2:
                    v_pre = vsem.n
                    q_pre = qsem.n

                    @block.tensor
                    def _(t, d_w=d_w, v_pre=v_pre):
                        t.wait_ge(dsem.h, d_w)
                        t.wait_ge(vsem.h, v_pre)
                        for tc in range(NQC):
                            for fb in range(2):
                                k = tc * 2 + fb
                                if k >= 4:
                                    t.wait_ge(vsem.h, v_pre + k - 3)
                                for dc in range(NDC):
                                    t.matmul(bank(pBa, k % 4),
                                             ub[:, dc * SL + tc * 128:][:, 0:128],
                                             wta[:, dc * D + fb * T:][:, 0:T],
                                             start=(dc == 0), stop=False)
                                ins = t.matmul(bank(pBa, k % 4), onesb[:, 0:128],
                                               bvrowb[0:1, fb * T:(fb + 1) * T],
                                               start=False, stop=True)
                                tsem.inc(ins)

                    @block.vector
                    def _(v, t_pre=t_pre, q_pre=q_pre):
                        for tc in range(NQC):
                            for fb in range(2):
                                k = tc * 2 + fb
                                v.wait_ge(tsem.h, t_pre + k + 1)
                                if tc >= 1 and fb == 0:
                                    v.wait_ge(qsem.h, q_pre + 16 * tc)
                                vsem.inc(v.tensor_copy(kvx[:, fb * T:(fb + 1) * T],
                                                       bank(pBa, k % 4)))

                    @block.sync
                    def _(sy, v_pre=v_pre):
                        for tc in range(NQC):
                            sy.wait_ge(vsem.h, v_pre + 2 * (tc + 1))
                            sy.dma_start(out=vexp[tc * 128:(tc + 1) * 128, :],
                                         in_=kvx[:, 0:1024]).then_inc(qsem.h, 16)
                            qsem.n += 16
                    q_vexp = qsem.n

                    @block.gpsimd
                    def _(g, q_vexp=q_vexp):
                        g.wait_ge(qsem.h, q_vexp)
                        g.collective_compute(
                            "AllGather", ALU.bypass,
                            replica_groups=[[0, 1], [2, 3], [4, 5], [6, 7]],
                            ins=[vexp.opt()], outs=[vg.opt()]).then_inc(cv.h, 1)
                        cv.n = 1

            if stop_after == "qkv":
                return
            head_t_end = {}
            for h in range(NH):
                lo = 64 * (h % 2)
                fcq = h // 2
                v_h0, a_h0, q_h0 = vsem.n, asem.n, qsem.n
                kse = ksp[h % 2]

                @block.sync
                def _(sy, h=h, lo=lo, kse=kse):
                    if h == 0:
                        sy.wait_ge(ck.h, 1)
                        sy.wait_ge(cv.h, 1)
                    if h >= 2:
                        sy.wait_ge(tsem.h, head_t_end[h - 2])
                    sy.dma_start(out=kh[lo:lo + 64, 0:SL],
                                 in_=kg[64 * h:64 * h + 64, :]).then_inc(kse.h, 16)
                    sy.dma_start(out=kh[lo:lo + 64, SL:2 * SL],
                                 in_=kg[D + 64 * h:D + 64 * h + 64, :]).then_inc(kse.h, 16)
                    vv = vh[h % 2].rearrange("p (k c) -> p k c", c=65)
                    sy.dma_start(out=vv[:, :, 0:64],
                                 in_=vg.rearrange("(k p) f -> p k f", p=128)
                                 [:, :, 64 * h:64 * h + 64]).then_inc(kse.h, 16)
                    kse.n += 48
                q_kv = kse.n

                tA0, vA0 = tsem.n, vsem.n

                @block.tensor
                def _(t, h=h, lo=lo, fcq=fcq, q_kv=q_kv, tA0=tA0, vA0=vA0,
                      v_h0=v_h0, a_h0=a_h0, kse=kse):
                    t.wait_ge(kse.h, q_kv)
                    t.wait_ge(esem.h, 4)
                    t.wait_ge(vsem.h, v_h0)
                    t.wait_ge(asem.h, a_h0)
                    for qc in range(NQC):
                        if qc >= 1:
                            t.wait_ge(vsem.h, vA0 + 4 * qc)
                        q_sl = qoz[lo:lo + 64, fcq * SL + qc * 128:][:, 0:128]
                        for g4 in range(4):
                            t.matmul(bank(pAa, g4), q_sl, kh[lo:lo + 64, g4 * T:(g4 + 1) * T],
                                     start=True, stop=True)
                        for g4 in range(4):
                            ins = t.matmul(bank(pBa, g4), q_sl,
                                           kh[lo:lo + 64, 2048 + g4 * T:][:, 0:T],
                                           start=True, stop=True)
                        tsem.inc(ins)
                        t.wait_ge(vsem.h, vA0 + 4 * qc + 3)
                        tsem.inc(t.transpose(pAa[0:1, 0:128], gcol[:, 3:4], idf[:, :]))

                @block.vector
                def _(v, h=h, tA0=tA0):
                    for qc in range(NQC):
                        v.wait_ge(tsem.h, tA0 + 2 * qc + 1)
                        v.reduce_max(gcol[:, 0:1], pAa[:, :], axis=AX.X)
                        vsem.inc(v.reduce_max(gcol[:, 1:2], pBa[:, :], axis=AX.X))
                        v.wait_ge(vsem.h, vsem.n)
                        vsem.inc(v.tensor_max(gcol[:, 2:3], gcol[:, 0:1], gcol[:, 1:2]))
                        v.wait_ge(vsem.h, vsem.n)
                        vsem.inc(v.tensor_scalar_mul(gcol[:, 3:4], gcol[:, 2:3], SCALE))
                        v.wait_ge(tsem.h, tA0 + 2 * qc + 2)
                        vsem.inc(v.tensor_copy(mrow[0:1, qc * 128:(qc + 1) * 128],
                                               pAa[0:1, 0:128]))

                for qb in range(NTT):
                    tB0, vB0, aB0 = tsem.n, vsem.n, asem.n

                    def t_sT(kc, tB0=tB0):
                        return tB0 + 2 + kc if kc < 2 else tB0 + 2 * kc + 1

                    def t_AV(kc, tB0=tB0):
                        if kc <= NKC - 3:
                            return tB0 + 4 + 2 * kc
                        return tB0 + 2 * NKC + (kc - NKC + 2)

                    t_rsb = tB0 + 2 * NKC + 2

                    def s_exp(kc, aB0=aB0):
                        return aB0 + kc + 2

                    def v_stt(kc, vB0=vB0):
                        return vB0 + kc + 1

                    @block.tensor
                    def _(t, h=h, lo=lo, fcq=fcq, qb=qb, tB0=tB0, vB0=vB0, aB0=aB0,
                          t_sT=t_sT, t_AV=t_AV, s_exp=s_exp, t_rsb=t_rsb):
                        t.wait_ge(vsem.h, vB0)
                        if qb > 0:
                            t.wait_ge(asem.h, aB0)
                        tsem.inc(t.matmul(pAa[:, 1536:2048], onesb[:, 0:128],
                                          mrow[0:1, qb * T:(qb + 1) * T], start=True, stop=True))
                        q_all = qoz[lo:lo + 64, fcq * SL + qb * T:][:, 0:T]

                        def sT(kc):
                            if kc >= 2:
                                t.wait_ge(asem.h, s_exp(kc - 2))
                            tsem.inc(t.matmul(bank(pAa, kc % 2),
                                              kh[lo:lo + 64, kc * 128:(kc + 1) * 128],
                                              q_all, start=True, stop=True))
                        sT(0)
                        sT(1)
                        for kc in range(NKC):
                            t.wait_ge(asem.h, s_exp(kc))
                            tsem.inc(t.matmul(pAa[0:65, 1024:1536],
                                              vh[h % 2][:, kc * 65:(kc + 1) * 65],
                                              att[:, (kc % 2) * 512:(kc % 2 + 1) * 512],
                                              start=(kc == 0), stop=(kc == NKC - 1)))
                            if kc + 2 < NKC:
                                sT(kc + 2)
                        t.wait_ge(vsem.h, vB0 + NKC + 1)
                        tsem.inc(t.matmul(pAa[:, 1536:2048], onesf[64:65, 0:128],
                                          srow[64:65, :], start=True, stop=True))
                        assert tsem.n == t_rsb

                    @block.scalar
                    def _(s, qb=qb, tB0=tB0, vB0=vB0, t_AV=t_AV, v_stt=v_stt, t_rsb=t_rsb):
                        s.wait_ge(tsem.h, tB0 + 1)
                        asem.inc(s.copy(mbc[:, :], pAa[:, 1536:2048]))
                        for kc in range(NKC):
                            s.wait_ge(vsem.h, v_stt(kc))
                            if kc >= 2:
                                s.wait_ge(tsem.h, t_AV(kc - 2))
                            asem.inc(s.activation(att[:, (kc % 2) * 512:(kc % 2 + 1) * 512],
                                                  bank(pAa, kc % 2), AF.Exp))
                        s.wait_ge(tsem.h, t_rsb)
                        asem.inc(s.copy(rsb[0:64, :], pAa[0:64, 1536:2048]))

                    @block.vector
                    def _(v, lo=lo, fcq=fcq, qb=qb, tB0=tB0, aB0=aB0,
                          t_sT=t_sT, t_AV=t_AV, q_h0=q_h0):
                        for kc in range(NKC):
                            v.wait_ge(tsem.h, t_sT(kc))
                            v.wait_ge(asem.h, aB0 + 1)
                            vsem.inc(v.scalar_tensor_tensor(bank(pAa, kc % 2),
                                                            bank(pAa, kc % 2),
                                                            SCALE, mbc[:, :],
                                                            ALU.mult, ALU.subtract))
                        v.wait_ge(tsem.h, t_AV(NKC - 1))
                        vsem.inc(v.reciprocal(srow[64:65, :], pAa[64:65, 1024:1536]))
                        v.wait_ge(asem.h, aB0 + NKC + 2)
                        if lo == 0:
                            vsem.inc(v.tensor_mul(qoz[0:64, fcq * SL + qb * T:][:, 0:T],
                                                  pAa[0:64, 1024:1536], rsb[0:64, :]))
                        else:
                            if qb > 0:
                                v.wait_ge(qsem.h, q_h0 + 16 * qb)
                            vsem.inc(v.tensor_mul(obuf[:, :], pAa[0:64, 1024:1536],
                                                  rsb[0:64, :]))
                    v_oscale = vsem.n

                    if lo == 64:
                        @block.sync
                        def _(sy, fcq=fcq, qb=qb, v_oscale=v_oscale):
                            sy.wait_ge(vsem.h, v_oscale)
                            sy.dma_start(out=qoz[64:128, fcq * SL + qb * T:][:, 0:T],
                                         in_=obuf[:, :]).then_inc(qsem.h, 16)
                            qsem.n += 16
                head_t_end[h] = tsem.n

            v_wo0, q_wo0 = vsem.n, qsem.n

            @block.tensor
            def _(t, v_wo0=v_wo0, q_wo0=q_wo0, d_w_wo=d_w_wo):
                t.wait_ge(vsem.h, v_wo0)
                t.wait_ge(qsem.h, q_wo0)
                t.wait_ge(dsem.h, d_w_wo[0])
                for dc in range(NDC):
                    for tt in range(NTT):
                        k = dc * NTT + tt
                        if k >= 2:
                            t.wait_ge(vsem.h, v_wo0 + k - 1)
                        for fc in range(NDC):
                            ins = t.matmul(bank(pAa, k % 2),
                                           wta[:, fc * D + dc * 128:][:, 0:128],
                                           qoz[:, fc * SL + tt * T:][:, 0:T],
                                           start=(fc == 0), stop=(fc == NDC - 1))
                        tsem.inc(ins)
            t_wo0 = tsem.n - NDC * NTT

            @block.vector
            def _(v, t_wo0=t_wo0):
                for dc in range(NDC):
                    for tt in range(NTT):
                        k = dc * NTT + tt
                        v.wait_ge(tsem.h, t_wo0 + k + 1)
                        x_sl = xfm[:, dc * SL + tt * T:][:, 0:T]
                        vsem.inc(v.scalar_tensor_tensor(
                            x_sl, bank(pAa, k % 2), battn[:, 3 * NDC + dc:][:, 0:1],
                            x_sl, ALU.add, ALU.add))

        if stop_after != "p0":
            layernorm(fold_x=True, with_gate=True)
        if stop_after in ("u1", "c1"):
            t_d, v_d, a_d = tsem.n, vsem.n, asem.n

            @block.gpsimd
            def _(g):
                g.wait_ge(tsem.h, t_d)
                g.wait_ge(vsem.h, v_d)
                g.wait_ge(asem.h, a_d)
                if stop_after == "u1":
                    for dc in range(NDC):
                        g.dma_start(out=outT[dc * 128:(dc + 1) * 128, :],
                                    in_=ub[:, dc * SL:(dc + 1) * SL]).then_inc(dsem.h, 16)
                        dsem.n += 16
                else:
                    g.dma_start(out=outT[0:8, :], in_=cT[0:8, :]).then_inc(dsem.h, 16)
                    dsem.n += 16
                g.wait_ge(dsem.h, dsem.n)
        if stop_after not in ("p0", "ln1", "u1", "c1"):
            ffn_experts(0)
        if stop_after not in ("p0", "ln1", "u1", "c1", "ffn1"):
            layernorm(fold_x=False, with_gate=False)
            attention("qkv" if stop_after == "qkv" else "all")
        if stop_after not in ("p0", "ln1", "u1", "c1", "ffn1", "qkv", "attn"):
            layernorm(fold_x=True, with_gate=True)
            ffn_experts(1)

        t_fin, v_fin, a_fin = tsem.n, vsem.n, asem.n
        skip_out = stop_after in ("u1", "c1")

        @block.sync
        def _(sy):
            if skip_out:
                sy.wait_ge(dsem.h, dsem.n)
                return
            sy.wait_ge(vsem.h, v_fin)
            sy.wait_ge(tsem.h, t_fin)
            sy.wait_ge(asem.h, a_fin)
            for dc in range(NDC):
                sy.dma_start(out=outT[dc * 128:(dc + 1) * 128, :],
                             in_=xfm[:, dc * SL:(dc + 1) * SL]).then_inc(qsem.h, 16)
            qsem.n += NDC * 16
            sy.wait_ge(qsem.h, qsem.n)

    return nc


def kernel(L0_attn_x, L0_attn_Wq, L0_attn_bq, L0_attn_Wk, L0_attn_bk,
           L0_attn_Wv, L0_attn_bv, L0_attn_Wo, L0_attn_bo,
           L0_ffn_Wg, L0_ffn_bg, L0_ffn_W1, L0_ffn_W2, L0_ffn_W3):
    x = np.ascontiguousarray(np.asarray(L0_attn_x, dtype=np.float32)).reshape(B * S, D)
    com = dict(
        wqT=np.ascontiguousarray(np.asarray(L0_attn_Wq, np.float32).T),
        wkT=np.ascontiguousarray(np.asarray(L0_attn_Wk, np.float32).T),
        wvT=np.ascontiguousarray(np.asarray(L0_attn_Wv, np.float32).T),
        woT=np.ascontiguousarray(np.asarray(L0_attn_Wo, np.float32).T),
        bq=np.asarray(L0_attn_bq, np.float32), bk=np.asarray(L0_attn_bk, np.float32),
        bv=np.asarray(L0_attn_bv, np.float32), bo=np.asarray(L0_attn_bo, np.float32),
        wg=np.asarray(L0_ffn_Wg, np.float32), bg=np.asarray(L0_ffn_bg, np.float32),
        w1=np.ascontiguousarray(np.asarray(L0_ffn_W1, np.float32)),
        w2=np.ascontiguousarray(np.asarray(L0_ffn_W2, np.float32)),
        w3=np.ascontiguousarray(np.asarray(L0_ffn_W3, np.float32)),
    )
    in_maps = []
    for c in range(NCORE):
        xs = np.ascontiguousarray(x[c * SL:(c + 1) * SL, :].T)
        in_maps.append(dict(xT=xs, **com))
    nc = build_program()
    res = run_bass_kernel_spmd(nc, in_maps, core_ids=list(range(NCORE)))
    outs = [np.asarray(r["outT"]).T for r in res.results]
    return np.concatenate(outs, axis=0).reshape(B, S, D)


# version 03
# speedup vs baseline: 1.0132x; 1.0132x over previous
import numpy as np
import concourse.bass as bass
import concourse.mybir as mybir
from concourse.bass_utils import run_bass_kernel_spmd

F32 = mybir.dt.float32
BF16 = mybir.dt.bfloat16
AF = mybir.ActivationFunctionType
ALU = mybir.AluOpType
AX = mybir.AxisListType

NCORE = 8
B, S, D = 4, 4096, 1024
SL = 2048
NH, HD = 16, 64
E, H = 8, 4096
NDC = D // 128
T = 512
NTT = SL // T
HC = 512
NHC = H // HC
SF = 4096
NKC = SF // 128
NQC = SL // 128
SCALE = 0.125
EPS = 1e-5


class Sem:
    def __init__(self, nc, ctx, name):
        self.h = ctx.enter_context(nc.semaphore(name))
        self.n = 0

    def inc(self, inst, k=1):
        inst.then_inc(self.h, k)
        self.n += k
        return self.n


def build_program(stop_after="all"):
    nc = bass.Bass("TRN2", target_bir_lowering=False, detect_race_conditions=False)

    xT = nc.dram_tensor("xT", [D, SL], F32, kind="ExternalInput").ap()
    wqT = nc.dram_tensor("wqT", [D, D], F32, kind="ExternalInput").ap()
    wkT = nc.dram_tensor("wkT", [D, D], F32, kind="ExternalInput").ap()
    wvT = nc.dram_tensor("wvT", [D, D], F32, kind="ExternalInput").ap()
    woT = nc.dram_tensor("woT", [D, D], F32, kind="ExternalInput").ap()
    bqd = nc.dram_tensor("bq", [D], F32, kind="ExternalInput").ap()
    bkd = nc.dram_tensor("bk", [D], F32, kind="ExternalInput").ap()
    bvd = nc.dram_tensor("bv", [D], F32, kind="ExternalInput").ap()
    bod = nc.dram_tensor("bo", [D], F32, kind="ExternalInput").ap()
    wgd = nc.dram_tensor("wg", [D, E], F32, kind="ExternalInput").ap()
    bgd = nc.dram_tensor("bg", [E], F32, kind="ExternalInput").ap()
    w1d = nc.dram_tensor("w1", [E, D, H], F32, kind="ExternalInput").ap()
    w2d = nc.dram_tensor("w2", [E, H, D], F32, kind="ExternalInput").ap()
    w3d = nc.dram_tensor("w3", [E, D, H], F32, kind="ExternalInput").ap()
    outT = nc.dram_tensor("outT", [D, SL], F32, kind="ExternalOutput").ap()

    kexp = nc.dram_tensor("kexp", [D, SL], BF16).ap()
    vexp = nc.dram_tensor("vexp", [SL, D], BF16).ap()
    kg = nc.dram_tensor("kg", [2 * D, SL], BF16).ap()
    vg = nc.dram_tensor("vg", [SF, D], BF16).ap()
    cdram = nc.dram_tensor("cdram", [E, SL], BF16).ap()

    from contextlib import ExitStack
    with ExitStack() as ctx:
        BASE = 16512
        off = [BASE]

        def At(name, shape, dt_, offset=None):
            nbytes = int(np.prod(shape[1:])) * mybir.dt.size(dt_)
            if offset is None:
                offset = off[0]
                off[0] += (nbytes + 31) // 32 * 32
            return nc.alloc_sbuf_tensor_at(name, list(shape), dt_, offset=offset).ap()

        xfm = At("xfm", [128, NDC * SL], F32)
        ub = At("ub", [128, NDC * SL], BF16)
        qoz = At("qoz", [128, NDC * SL], BF16)
        wreg = off[0]
        w1t = [At(f"w1t{p}", [128, NDC * HC], BF16, wreg + p * 24576) for p in range(2)]
        w3t = [At(f"w3t{p}", [128, NDC * HC], BF16, wreg + p * 24576 + 8192) for p in range(2)]
        w2t = [At(f"w2t{p}", [128, 4 * D], BF16, wreg + p * 24576 + 16384) for p in range(2)]
        wta = At("wta", [128, NDC * D], BF16, wreg)
        qtile = [At(f"qtile{p}", [65, SL], BF16, wreg + p * 4096) for p in range(2)]
        khb = [At(f"khb{p}", [65, SF], BF16, wreg + 16384 - p * 8192) for p in range(2)]
        vh = [At(f"vh{p}", [128, NKC * 65], BF16, wreg + 24576 + p * 4288) for p in range(2)]
        kvx = At("kvx", [128, SL], BF16, wreg + 33152)
        att = [At(f"att{p}", [128, 4 * 512], BF16, wreg + 37248 + p * 4096) for p in range(2)]
        bvrowb = At("bvrowb", [1, D], BF16, wreg + 45440)
        off[0] = wreg + 49152
        screg = off[0]
        sq = [At(f"sq{p}", [128, T], F32, screg + p * 2048) for p in range(2)]
        uf32 = [At(f"uf32{p}", [128, T], F32, screg + 4096 + p * 2048) for p in range(2)]
        hbuf = At("hbuf", [128, 4 * T], BF16, screg + 8192)
        s1 = [At(f"s1{p}", [128, T], F32, screg + 12288 + p * 2048) for p in range(2)]
        mrow = At("mrow", [1, SL], BF16, screg)
        rsb = At("rsb", [128, 512], F32, screg + 5120)
        srow = At("srow", [128, 512], F32, screg + 7168)
        obuf = At("obuf", [64, 512], BF16, screg + 9216)
        off[0] = screg + 16384
        rowbuf = At("rowbuf", [1, 4 * T], F32)
        ctoff = off[0]
        cT = At("cT", [8, SL], BF16)
        crow = At("crow", [1, SL], BF16, offset=ctoff)
        battn = At("battn", [128, 4 * NDC], F32)
        onesf = At("onesf", [128, 128], F32)
        onesb = At("onesb", [1, 128], BF16)
        idf = At("idf", [128, 128], F32)
        idb = At("idb", [128, 128], BF16)
        wg_sb = At("wg_sb", [128, NDC * E], F32)
        g1b = At("g1b", [128, E], F32)
        bgb = At("bgb", [128, E], F32)
        grow = At("grow", [1, 2 * E], F32)
        gsc = At("gsc", [128, 4 * E], F32)
        gcol = At("gcol", [128, 8], F32)
        assert off[0] <= 229344, off[0]

        ROW_TMP, ROW_R, ROW_MR, ROW_NM = (rowbuf[0:1, k * T:(k + 1) * T] for k in range(4))

        pA = ctx.enter_context(nc.psum_tensor("pA", [128, 2048], F32))
        pB = ctx.enter_context(nc.psum_tensor("pB", [128, 2048], F32))
        pAa, pBa = pA[:, :], pB[:, :]
        pAb = pA[:, :].bitcast(BF16)

        def bank(ap_, i):
            return ap_[:, i * 512:(i + 1) * 512]

        names = ["qsem", "dsem", "tsem", "vsem", "asem", "ck", "cv", "esem",
                 "ws0", "ws1", "ks0", "ks1"]
        SM = {nm: Sem(nc, ctx, nm) for nm in names}
        qsem, dsem, tsem, vsem, asem, ck, cv, esem, ws0, ws1, ks0, ks1 = \
            (SM[n] for n in names)
        wsp = [ws0, ws1]
        ksp = [ks0, ks1]
        block = ctx.enter_context(nc.Block())

        @block.gpsimd
        def _(g):
            g.memset(onesf[:, :], 1.0)
            g.memset(onesb[:, :], 1.0)
            g.memset(idf[:, :], 1.0)
            esem.inc(g.memset(idb[:, :], 1.0))
            g.wait_ge(esem.h, 1)
            g.affine_select(idf[:, :], idf[:, :], [[1, 128]], ALU.is_equal, 0.0,
                            base=0, channel_multiplier=-1)
            esem.inc(g.affine_select(idb[:, :], idb[:, :], [[1, 128]], ALU.is_equal, 0.0,
                                     base=0, channel_multiplier=-1))
            with nc.allow_non_contiguous_dma("tiny one-time bias loads"):
                for j, bb in enumerate([bqd, bkd, bvd, bod]):
                    g.dma_start(out=battn[:, j * NDC:(j + 1) * NDC],
                                in_=bb.rearrange("(dc p) -> p dc", p=128),
                                allow_slow_non_contiguous=True).then_inc(dsem.h, 16)
            g.dma_start(out=wg_sb[:, :].rearrange("p (dc e) -> p dc e", e=E),
                        in_=wgd.rearrange("(dc p) e -> p dc e", p=128)).then_inc(dsem.h, 16)
            g.dma_start(out=grow[0:1, E:2 * E], in_=bgd[None, :]).then_inc(dsem.h, 16)
            dsem.n += 6 * 16

        @block.sync
        def _(sy):
            for dc in range(NDC):
                sy.dma_start(out=xfm[:, dc * SL:(dc + 1) * SL],
                             in_=xT[dc * 128:(dc + 1) * 128, :]).then_inc(qsem.h, 16)
            qsem.n += NDC * 16
        q_x = qsem.n

        @block.tensor
        def _(t):
            t.wait_ge(dsem.h, dsem.n)
            t.wait_ge(esem.h, 2)
            t.wait_ge(qsem.h, q_x)
            for dc in range(NDC):
                ins = t.matmul(pBa[0:1, 0:E], onesf[:, 0:1], wg_sb[:, dc * E:(dc + 1) * E],
                               start=(dc == 0), stop=(dc == NDC - 1))
            tsem.inc(ins)
        ev0 = tsem.n

        @block.vector
        def _(v):
            v.wait_ge(tsem.h, ev0)
            vsem.inc(v.tensor_copy(grow[0:1, 0:E], pBa[0:1, 0:E]))
        ev1 = vsem.n

        @block.tensor
        def _(t):
            t.wait_ge(vsem.h, ev1)
            tsem.inc(t.matmul(pAa[:, 0:2 * E], onesf[0:1, 0:128], grow[0:1, :],
                              start=True, stop=True))
        ev2 = tsem.n

        @block.vector
        def _(v):
            v.wait_ge(tsem.h, ev2)
            v.tensor_copy(g1b[:, :], pAa[:, 0:E])
            v.tensor_scalar_mul(battn[:, 0:NDC], battn[:, 0:NDC], SCALE)
            vsem.inc(v.tensor_copy(bgb[:, :], pAa[:, E:2 * E]))

        def gate(tt, v_rows):
            for tc in range(4):
                vg0 = vsem.n
                tok0 = tt * T + tc * 128

                @block.tensor
                def _(t, tt=tt, tc=tc, tok0=tok0, vg0=vg0, v_rows=v_rows):
                    if tc == 0:
                        t.wait_ge(vsem.h, v_rows)
                    else:
                        t.wait_ge(vsem.h, vg0)
                    for dc in range(NDC):
                        ins = t.matmul(pBa[0:128, 0:E], xfm[:, dc * SL + tok0:][:, 0:128],
                                       wg_sb[:, dc * E:(dc + 1) * E],
                                       start=(dc == 0), stop=(dc == NDC - 1))
                    tsem.inc(ins)
                    tsem.inc(t.transpose(pAa[0:128, 1536:1537],
                                         ROW_NM[0:1, tc * 128:(tc + 1) * 128], onesf[0:1, 0:1]))
                    tsem.inc(t.transpose(pAa[0:128, 1537:1538],
                                         ROW_R[0:1, tc * 128:(tc + 1) * 128], onesf[0:1, 0:1]))
                t_g = tsem.n

                @block.vector
                def _(v, t_g=t_g):
                    def vf(ins):
                        vsem.inc(ins)
                        v.wait_ge(vsem.h, vsem.n)
                    v.wait_ge(tsem.h, t_g)
                    vf(v.tensor_copy(gcol[:, 6:8], pAa[:, 1536:1538]))
                    vf(v.scalar_tensor_tensor(gsc[:, 0:E], g1b[:, :], gcol[:, 6:7],
                                              pBa[0:128, 0:E], ALU.mult, ALU.add))
                    vf(v.tensor_scalar(gsc[:, 0:E], gsc[:, 0:E], gcol[:, 7:8], None, ALU.mult))
                    vf(v.tensor_add(gsc[:, 0:E], gsc[:, 0:E], bgb[:, :]))
                    vf(v.reduce_max(gcol[:, 0:1], gsc[:, 0:E], axis=AX.X))
                    vf(v.tensor_scalar_mul(gcol[:, 1:2], gcol[:, 0:1], -1.0))
                    vf(v.tensor_scalar(gsc[:, E:2 * E], gsc[:, 0:E], gcol[:, 0:1], None, ALU.is_ge))
                    vf(v.scalar_tensor_tensor(gsc[:, 2 * E:3 * E], gsc[:, E:2 * E], -1e30,
                                              gsc[:, 0:E], ALU.mult, ALU.add))
                    vsem.inc(v.reduce_max(gcol[:, 2:3], gsc[:, 2 * E:3 * E], axis=AX.X))
                v_g1 = vsem.n

                @block.scalar
                def _(s, v_g1=v_g1):
                    s.wait_ge(vsem.h, v_g1)
                    s.activation(gsc[:, 2 * E:3 * E], gsc[:, 0:E], AF.Exp, bias=gcol[:, 1:2])
                    asem.inc(s.activation(gcol[:, 3:4], gcol[:, 2:3], AF.Exp, bias=gcol[:, 1:2]))
                a_g = asem.n

                @block.vector
                def _(v, a_g=a_g):
                    def vf(ins):
                        vsem.inc(ins)
                        v.wait_ge(vsem.h, vsem.n)
                    v.wait_ge(asem.h, a_g)
                    vf(v.tensor_scalar_add(gcol[:, 4:5], gcol[:, 3:4], 1.0))
                    vf(v.reciprocal(gcol[:, 5:6], gcol[:, 4:5]))
                    vf(v.tensor_scalar(gsc[:, E:2 * E], gsc[:, 0:E], gcol[:, 2:3], None, ALU.is_ge))
                    vf(v.tensor_mul(gsc[:, 2 * E:3 * E], gsc[:, 2 * E:3 * E],
                                    gsc[:, E:2 * E]))
                    vsem.inc(v.tensor_scalar(gsc[:, 3 * E:4 * E].bitcast(BF16)[:, 0:E],
                                             gsc[:, 2 * E:3 * E], gcol[:, 5:6], None, ALU.mult))
                v_g2 = vsem.n

                @block.tensor
                def _(t, v_g2=v_g2):
                    t.wait_ge(vsem.h, v_g2)
                    tsem.inc(t.transpose(pAb[0:8, 3072:3200],
                                         gsc[:, 3 * E:4 * E].bitcast(BF16)[:, 0:E], idb[:, :]))
                t_c = tsem.n

                @block.vector
                def _(v, t_c=t_c, tok0=tok0):
                    v.wait_ge(tsem.h, t_c)
                    vsem.inc(v.tensor_copy(cT[0:8, tok0:tok0 + 128], pAb[0:8, 3072:3200]))

        def layernorm(fold_x, with_gate):
            v_enter = vsem.n
            prev_vu = [v_enter]
            for tt in range(NTT):
                av0 = asem.n
                tv0 = tsem.n

                @block.scalar
                def _(s, tt=tt, tv0=tv0, v_enter=v_enter):
                    if tt == 0:
                        s.wait_ge(vsem.h, v_enter)
                        s.wait_ge(qsem.h, q_x)
                    for dc in range(NDC):
                        if dc >= 2:
                            s.wait_ge(tsem.h, tv0 + dc - 1)
                        asem.inc(s.activation(sq[dc % 2][:, :],
                                              xfm[:, dc * SL + tt * T:][:, 0:T], AF.Square))

                @block.tensor
                def _(t, tt=tt, av0=av0, wait_v=prev_vu[0]):
                    t.wait_ge(vsem.h, wait_v)
                    for dc in range(NDC):
                        x_sl = xfm[:, dc * SL + tt * T:][:, 0:T]
                        t.matmul(pBa[0:1, 1024:1536], onesf[:, 0:1], x_sl,
                                 start=(dc == 0), stop=(dc == NDC - 1))
                        t.wait_ge(asem.h, av0 + dc + 1)
                        tsem.inc(t.matmul(pBa[0:1, 1536:2048], onesf[:, 0:1], sq[dc % 2][:, :],
                                          start=(dc == 0), stop=(dc == NDC - 1)))
                t_sums = tsem.n

                @block.vector
                def _(v, t_sums=t_sums):
                    v.wait_ge(tsem.h, t_sums)
                    v.tensor_scalar_mul(pBa[0:1, 1024:1536], pBa[0:1, 1024:1536], 1.0 / D)
                    vsem.inc(v.tensor_scalar_mul(pBa[0:1, 1536:2048], pBa[0:1, 1536:2048], 1.0 / D))
                v_mean = vsem.n

                @block.scalar
                def _(s, v_mean=v_mean):
                    s.wait_ge(vsem.h, v_mean)
                    asem.inc(s.activation(ROW_TMP, pBa[0:1, 1024:1536], AF.Square))
                a_msq = asem.n

                @block.vector
                def _(v, a_msq=a_msq):
                    v.wait_ge(asem.h, a_msq)
                    v.tensor_sub(pBa[0:1, 1536:2048], pBa[0:1, 1536:2048], ROW_TMP)
                    vsem.inc(v.tensor_scalar_add(pBa[0:1, 1536:2048], pBa[0:1, 1536:2048], EPS))
                v_var = vsem.n

                @block.scalar
                def _(s, v_var=v_var):
                    s.wait_ge(vsem.h, v_var)
                    asem.inc(s.activation(ROW_TMP, pBa[0:1, 1536:2048], AF.Ln))
                a_ln = asem.n

                @block.vector
                def _(v, a_ln=a_ln):
                    v.wait_ge(asem.h, a_ln)
                    vsem.inc(v.tensor_scalar_mul(ROW_TMP, ROW_TMP, -0.5))
                v_ln = vsem.n

                @block.scalar
                def _(s, v_ln=v_ln):
                    s.wait_ge(vsem.h, v_ln)
                    asem.inc(s.activation(ROW_R, ROW_TMP, AF.Exp))
                a_r = asem.n

                @block.vector
                def _(v, a_r=a_r, with_gate=with_gate):
                    v.wait_ge(asem.h, a_r)
                    v.tensor_mul(ROW_MR, pBa[0:1, 1024:1536], ROW_R)
                    if with_gate:
                        vsem.inc(v.tensor_scalar_mul(ROW_NM, pBa[0:1, 1024:1536], -1.0))
                    else:
                        vsem.inc(v.engine_nop())
                v_rows = vsem.n

                if with_gate:
                    gate(tt, v_rows)

                @block.tensor
                def _(t, v_rows=v_rows):
                    t.wait_ge(vsem.h, v_rows)
                    tsem.inc(t.matmul(pBa[:, 1024:1536], onesf[0:1, 0:128], ROW_R,
                                      start=True, stop=True))
                    tsem.inc(t.matmul(pBa[:, 1536:2048], onesf[0:1, 0:128], ROW_MR,
                                      start=True, stop=True))
                t_bc = tsem.n
                vu0 = vsem.n
                au0 = asem.n

                @block.vector
                def _(v, tt=tt, t_bc=t_bc, fold_x=fold_x, au0=au0):
                    v.wait_ge(tsem.h, t_bc)
                    for dc in range(NDC):
                        x_sl = xfm[:, dc * SL + tt * T:][:, 0:T]
                        p = dc % 2
                        if dc >= 2:
                            v.wait_ge(asem.h, au0 + dc - 1)
                        v.tensor_mul(uf32[p][:, :], x_sl, pBa[:, 1024:1536])
                        vsem.inc(v.tensor_sub(uf32[p][:, :], uf32[p][:, :], pBa[:, 1536:2048]))
                        if fold_x:
                            vsem.inc(v.scalar_tensor_tensor(x_sl, uf32[p][:, :], 0.5, x_sl,
                                                            ALU.mult, ALU.add))
                        else:
                            vsem.inc(v.engine_nop())

                @block.scalar
                def _(s, tt=tt, vu0=vu0):
                    for dc in range(NDC):
                        u_sl = ub[:, dc * SL + tt * T:][:, 0:T]
                        s.wait_ge(vsem.h, vu0 + 2 * dc + 1)
                        asem.inc(s.copy(u_sl, uf32[dc % 2][:, :]))
                prev_vu[0] = vsem.n

        def ffn_experts(call):
            t0 = tsem.n
            v0 = vsem.n
            a0 = asem.n
            a_enter = asem.n
            v_pre = vsem.n
            d0w = [wsp[0].n, wsp[1].n]

            @block.sync
            def _(sy, v_pre=v_pre):
                sy.wait_ge(vsem.h, v_pre)
                sy.dma_start(out=cdram[:, :], in_=cT[0:8, :]).then_inc(qsem.h, 16)
                qsem.n += 16
            q_cflat = qsem.n

            def t_base(e):
                return t0 + e * (4 + 32 * 16)

            def t_cb(e, tt):
                return t_base(e) + tt + 1

            def t_blk(e, hc, tt):
                return t_base(e) + 4 + (hc * NTT + tt) * 16

            def t_h1(e, hc, tt, i):
                return t_blk(e, hc, tt) + 2 * i + 1

            def t_h3(e, hc, tt, i):
                return t_blk(e, hc, tt) + 2 * i + 2

            def t_y(e, hc, tt, dc):
                return t_blk(e, hc, tt) + 8 + dc + 1

            def v_base(e):
                return v0 + e * (4 + 32 * 12)

            def v_z(e, tt):
                return v_base(e) + tt + 1

            def v_blkbase(e, hc, tt):
                return v_base(e) + 4 + (hc * NTT + tt) * 12

            def v_hmul(e, hc, tt, i):
                return v_blkbase(e, hc, tt) + i + 1

            def v_yadd(e, hc, tt, dc):
                return v_blkbase(e, hc, tt) + 4 + dc + 1

            def lin_h(e, hc, tt, i):
                return ((e * NHC + hc) * NTT + tt) * 4 + i

            def lin_y(e, hc, tt, dc):
                return ((e * NHC + hc) * NTT + tt) * 8 + dc

            def unlin_h(k):
                return (k // (4 * NTT * NHC), (k // (4 * NTT)) % NHC,
                        (k // 4) % NTT, k % 4)

            def unlin_y(k):
                return (k // (8 * NTT * NHC), (k // (8 * NTT)) % NHC,
                        (k // 8) % NTT, k % 8)

            def a_silu(e, hc, tt, i):
                return a0 + ((e * NHC + hc) * NTT + tt) * 4 + i + 1

            @block.gpsimd
            def _(g, call=call, t0=t0):
                for c in range(E * NHC):
                    e, hc = divmod(c, NHC)
                    p = c % 2
                    if c < 2:
                        g.wait_ge(tsem.h, t0)
                    else:
                        ee, hh = divmod(c - 2, NHC)
                        g.wait_ge(tsem.h, t_y(ee, hh, NTT - 1, 7))
                    g.dma_start(out=w1t[p].rearrange("p (dc h) -> p dc h", h=HC),
                                in_=w1d[e][:, hc * HC:(hc + 1) * HC]
                                .rearrange("(dc p) h -> p dc h", p=128)).then_inc(wsp[p].h, 16)
                    g.dma_start(out=w3t[p].rearrange("p (dc h) -> p dc h", h=HC),
                                in_=w3d[e][:, hc * HC:(hc + 1) * HC]
                                .rearrange("(dc p) h -> p dc h", p=128)).then_inc(wsp[p].h, 16)
                    g.dma_start(out=w2t[p].rearrange("p (hh d) -> p hh d", d=D),
                                in_=w2d[e][hc * HC:(hc + 1) * HC, :]
                                .rearrange("(hh p) d -> p hh d", p=128)).then_inc(wsp[p].h, 16)
                    wsp[p].n += 48

            @block.sync
            def _(sy, q_cflat=q_cflat):
                for e in range(E):
                    if e == 0:
                        sy.wait_ge(qsem.h, q_cflat)
                    else:
                        sy.wait_ge(tsem.h, t_cb(e - 1, NTT - 1))
                    sy.dma_start(out=crow[0:1, :], in_=cdram[e:e + 1, :]).then_inc(qsem.h, 16)
                    qsem.n += 16
            q_crow0 = q_cflat

            @block.tensor
            def _(t, call=call, q_crow0=q_crow0, d0w=d0w):
                for e in range(E):
                    for tt in range(NTT):
                        t.wait_ge(qsem.h, q_crow0 + 16 * (e + 1))
                        if tt > 0:
                            t.wait_ge(vsem.h, v_z(e, tt - 1))
                        if e > 0 and tt == 0:
                            t.wait_ge(vsem.h, v_z(e - 1, NTT - 1))
                        tsem.inc(t.matmul(pBa[:, 1024:1536], onesb[:, 0:128],
                                          crow[0:1, tt * T:(tt + 1) * T], start=True, stop=True))
                        assert tsem.n == t_cb(e, tt)
                    for hc in range(NHC):
                        c = e * NHC + hc
                        p = c % 2
                        t.wait_ge(wsp[p].h, d0w[p] + 48 * (c // 2 + 1))
                        for tt in range(NTT):
                            t.wait_ge(vsem.h, v_z(e, tt))
                            for i in range(4):
                                kk = lin_h(e, hc, tt, i)
                                if kk >= 2:
                                    t.wait_ge(vsem.h, v_hmul(*unlin_h(kk - 2)))
                                pp = kk % 2
                                for dc in range(NDC):
                                    ins = t.matmul(bank(pAa, pp),
                                                   w1t[p][:, dc * HC + i * 128:][:, 0:128],
                                                   qoz[:, dc * SL + tt * T:][:, 0:T],
                                                   start=(dc == 0), stop=(dc == NDC - 1))
                                tsem.inc(ins)
                                for dc in range(NDC):
                                    ins = t.matmul(bank(pAa, 2 + pp),
                                                   w3t[p][:, dc * HC + i * 128:][:, 0:128],
                                                   qoz[:, dc * SL + tt * T:][:, 0:T],
                                                   start=(dc == 0), stop=(dc == NDC - 1))
                                tsem.inc(ins)
                                assert tsem.n == t_h3(e, hc, tt, i)
                            t.wait_ge(vsem.h, v_hmul(e, hc, tt, 3))
                            for dc in range(NDC):
                                kk = lin_y(e, hc, tt, dc)
                                if kk >= 2:
                                    t.wait_ge(vsem.h, v_yadd(*unlin_y(kk - 2)))
                                pp = kk % 2
                                for i in range(4):
                                    ins = t.matmul(bank(pBa, pp),
                                                   w2t[p][:, i * D + dc * 128:][:, 0:128],
                                                   hbuf[:, i * T:(i + 1) * T],
                                                   start=(i == 0), stop=(i == 3))
                                tsem.inc(ins)
                                assert tsem.n == t_y(e, hc, tt, dc)

            @block.vector
            def _(v, call=call, a_enter=a_enter):
                for e in range(E):
                    for tt in range(NTT):
                        if e == 0 and tt == 0:
                            v.wait_ge(asem.h, a_enter)
                        v.wait_ge(tsem.h, t_cb(e, tt))
                        if e > 0:
                            v.wait_ge(tsem.h, t_h3(e - 1, NHC - 1, tt, 3))
                        ins = None
                        for dc in range(NDC):
                            ins = v.tensor_mul(qoz[:, dc * SL + tt * T:][:, 0:T],
                                               ub[:, dc * SL + tt * T:][:, 0:T],
                                               pBa[:, 1024:1536])
                        vsem.inc(ins)
                        assert vsem.n == v_z(e, tt)
                    for hc in range(NHC):
                        for tt in range(NTT):
                            for i in range(4):
                                kk = lin_h(e, hc, tt, i)
                                pp = kk % 2
                                v.wait_ge(asem.h, a_silu(e, hc, tt, i))
                                v.wait_ge(tsem.h, t_h3(e, hc, tt, i))
                                vsem.inc(v.tensor_mul(hbuf[:, i * T:(i + 1) * T],
                                                      s1[pp][:, :], bank(pAa, 2 + pp)))
                                assert vsem.n == v_hmul(e, hc, tt, i)
                            for dc in range(NDC):
                                kk = lin_y(e, hc, tt, dc)
                                pp = kk % 2
                                v.wait_ge(tsem.h, t_y(e, hc, tt, dc))
                                x_sl = xfm[:, dc * SL + tt * T:][:, 0:T]
                                vsem.inc(v.scalar_tensor_tensor(x_sl, bank(pBa, pp), 0.5,
                                                                x_sl, ALU.mult, ALU.add))
                                assert vsem.n == v_yadd(e, hc, tt, dc)

            @block.scalar
            def _(s, call=call):
                for e in range(E):
                    for hc in range(NHC):
                        for tt in range(NTT):
                            for i in range(4):
                                kk = lin_h(e, hc, tt, i)
                                pp = kk % 2
                                s.wait_ge(tsem.h, t_h1(e, hc, tt, i))
                                if kk >= 2:
                                    s.wait_ge(vsem.h, v_hmul(*unlin_h(kk - 2)))
                                asem.inc(s.activation(s1[pp][:, :], bank(pAa, pp), AF.Silu))
                                assert asem.n == a_silu(e, hc, tt, i)

        def attention(stop_after="all"):
            a_enter = asem.n
            for wi, wsrc_ in enumerate([wqT, wkT, wvT]):
                t_pre = tsem.n

                @block.gpsimd
                def _(g, wsrc_=wsrc_, t_pre=t_pre, wi=wi):
                    g.wait_ge(tsem.h, t_pre)
                    if wi == 2:
                        g.dma_start(out=bvrowb[:, :], in_=bvd[None, :]).then_inc(dsem.h, 16)
                        dsem.n += 16
                    g.dma_start(out=wta[:, :].rearrange("p (dc f) -> p dc f", f=D),
                                in_=wsrc_.rearrange("(dc p) f -> p dc f", p=128)
                                ).then_inc(dsem.h, 16)
                    dsem.n += 16
                d_w = dsem.n

                if wi == 0:
                    a_pre = asem.n

                    @block.tensor
                    def _(t, d_w=d_w, a_pre=a_pre, a_enter=a_enter):
                        t.wait_ge(dsem.h, d_w)
                        t.wait_ge(asem.h, a_enter)
                        for fc in range(NDC):
                            for tt in range(NTT):
                                k = fc * NTT + tt
                                if k >= 2:
                                    t.wait_ge(asem.h, a_pre + k - 1)
                                for dc in range(NDC):
                                    ins = t.matmul(bank(pAa, k % 2),
                                                   wta[:, dc * D + fc * 128:][:, 0:128],
                                                   ub[:, dc * SL + tt * T:][:, 0:T],
                                                   start=(dc == 0), stop=(dc == NDC - 1))
                                tsem.inc(ins)

                    @block.scalar
                    def _(s, t_pre=t_pre):
                        for fc in range(NDC):
                            for tt in range(NTT):
                                k = fc * NTT + tt
                                s.wait_ge(tsem.h, t_pre + k + 1)
                                asem.inc(s.activation(qoz[:, fc * SL + tt * T:][:, 0:T],
                                                      bank(pAa, k % 2), AF.Identity,
                                                      bias=battn[:, 0 * NDC + fc:][:, 0:1],
                                                      scale=SCALE))
                elif wi == 1:
                    a_pre = asem.n
                    q_pre = qsem.n

                    @block.tensor
                    def _(t, d_w=d_w, a_pre=a_pre):
                        t.wait_ge(dsem.h, d_w)
                        t.wait_ge(asem.h, a_pre)
                        for fc in range(NDC):
                            for tt in range(NTT):
                                k = fc * NTT + tt
                                if k >= 2:
                                    t.wait_ge(asem.h, a_pre + k - 1)
                                for dc in range(NDC):
                                    ins = t.matmul(bank(pAa, k % 2),
                                                   wta[:, dc * D + fc * 128:][:, 0:128],
                                                   ub[:, dc * SL + tt * T:][:, 0:T],
                                                   start=(dc == 0), stop=(dc == NDC - 1))
                                tsem.inc(ins)

                    @block.scalar
                    def _(s, t_pre=t_pre, q_pre=q_pre, a_pre=a_pre):
                        for fc in range(NDC):
                            for tt in range(NTT):
                                k = fc * NTT + tt
                                s.wait_ge(tsem.h, t_pre + k + 1)
                                if fc >= 1 and tt == 0:
                                    s.wait_ge(qsem.h, q_pre + 16 * fc)
                                asem.inc(s.activation(kvx[:, tt * T:(tt + 1) * T],
                                                      bank(pAa, k % 2), AF.Identity,
                                                      bias=battn[:, 1 * NDC + fc:][:, 0:1]))

                    @block.sync
                    def _(sy, a_pre=a_pre):
                        for fc in range(NDC):
                            sy.wait_ge(asem.h, a_pre + NTT * (fc + 1))
                            sy.dma_start(out=kexp[fc * 128:(fc + 1) * 128, :],
                                         in_=kvx[:, :]).then_inc(qsem.h, 16)
                            qsem.n += 16
                    q_kexp = qsem.n

                    @block.gpsimd
                    def _(g, q_kexp=q_kexp):
                        g.wait_ge(qsem.h, q_kexp)
                        g.collective_compute(
                            "AllGather", ALU.bypass,
                            replica_groups=[[0, 1], [2, 3], [4, 5], [6, 7]],
                            ins=[kexp.opt()], outs=[kg.opt()]).then_inc(ck.h, 1)
                        ck.n = 1
                elif wi == 2:
                    v_pre = vsem.n
                    q_pre = qsem.n

                    @block.tensor
                    def _(t, d_w=d_w, v_pre=v_pre):
                        t.wait_ge(dsem.h, d_w)
                        t.wait_ge(vsem.h, v_pre)
                        for tc in range(NQC):
                            for fb in range(2):
                                k = tc * 2 + fb
                                if k >= 4:
                                    t.wait_ge(vsem.h, v_pre + k - 3)
                                for dc in range(NDC):
                                    t.matmul(bank(pBa, k % 4),
                                             ub[:, dc * SL + tc * 128:][:, 0:128],
                                             wta[:, dc * D + fb * T:][:, 0:T],
                                             start=(dc == 0), stop=False)
                                ins = t.matmul(bank(pBa, k % 4), onesb[:, 0:128],
                                               bvrowb[0:1, fb * T:(fb + 1) * T],
                                               start=False, stop=True)
                                tsem.inc(ins)

                    @block.vector
                    def _(v, t_pre=t_pre, q_pre=q_pre):
                        for tc in range(NQC):
                            for fb in range(2):
                                k = tc * 2 + fb
                                v.wait_ge(tsem.h, t_pre + k + 1)
                                if tc >= 1 and fb == 0:
                                    v.wait_ge(qsem.h, q_pre + 16 * tc)
                                vsem.inc(v.tensor_copy(kvx[:, fb * T:(fb + 1) * T],
                                                       bank(pBa, k % 4)))

                    @block.sync
                    def _(sy, v_pre=v_pre):
                        for tc in range(NQC):
                            sy.wait_ge(vsem.h, v_pre + 2 * (tc + 1))
                            sy.dma_start(out=vexp[tc * 128:(tc + 1) * 128, :],
                                         in_=kvx[:, 0:1024]).then_inc(qsem.h, 16)
                            qsem.n += 16
                    q_vexp = qsem.n

                    @block.gpsimd
                    def _(g, q_vexp=q_vexp):
                        g.wait_ge(qsem.h, q_vexp)
                        g.collective_compute(
                            "AllGather", ALU.bypass,
                            replica_groups=[[0, 1], [2, 3], [4, 5], [6, 7]],
                            ins=[vexp.opt()], outs=[vg.opt()]).then_inc(cv.h, 1)
                        cv.n = 1

            t_postv = tsem.n

            @block.gpsimd
            def _(g, t_postv=t_postv):
                g.wait_ge(tsem.h, t_postv)
                for p in range(2):
                    vv = vh[p].rearrange("p (k c) -> p k c", c=65)
                    g.memset(vv[:, :, 64:65], 1.0)
                    g.memset(khb[p][64:65, :], 1.0)
                g.memset(srow[:, :], 1.0).then_inc(esem.h, 1)
                esem.n += 1

            if stop_after == "qkv":
                return
            head_t_end = {}
            for h in range(NH):
                lo = 64 * (h % 2)
                fcq = h // 2
                p = h % 2
                v_h0, a_h0, q_h0 = vsem.n, asem.n, qsem.n
                kse = ksp[p]

                @block.sync
                def _(sy, h=h, lo=lo, fcq=fcq, p=p, kse=kse):
                    if h == 0:
                        sy.wait_ge(ck.h, 1)
                        sy.wait_ge(cv.h, 1)
                        sy.wait_ge(esem.h, esem.n)
                    if h >= 2:
                        sy.wait_ge(tsem.h, head_t_end[h - 2])
                    sy.dma_start(out=khb[p][0:64, 0:SL],
                                 in_=kg[64 * h:64 * h + 64, :]).then_inc(kse.h, 16)
                    sy.dma_start(out=khb[p][0:64, SL:2 * SL],
                                 in_=kg[D + 64 * h:D + 64 * h + 64, :]).then_inc(kse.h, 16)
                    vv = vh[p].rearrange("p (k c) -> p k c", c=65)
                    sy.dma_start(out=vv[:, :, 0:64],
                                 in_=vg.rearrange("(k p) f -> p k f", p=128)
                                 [:, :, 64 * h:64 * h + 64]).then_inc(kse.h, 16)
                    sy.dma_start(out=qtile[p][0:64, :],
                                 in_=qoz[lo:lo + 64, fcq * SL:(fcq + 1) * SL]
                                 ).then_inc(kse.h, 16)
                    kse.n += 64
                q_kv = kse.n

                tA0, vA0 = tsem.n, vsem.n

                @block.tensor
                def _(t, p=p, q_kv=q_kv, tA0=tA0, vA0=vA0,
                      v_h0=v_h0, a_h0=a_h0, kse=kse):
                    t.wait_ge(kse.h, q_kv)
                    t.wait_ge(vsem.h, v_h0)
                    t.wait_ge(asem.h, a_h0)
                    for qc in range(NQC):
                        if qc >= 1:
                            t.wait_ge(vsem.h, vA0 + 3 * qc)
                        q_sl = qtile[p][0:64, qc * 128:(qc + 1) * 128]
                        for g4 in range(4):
                            t.matmul(bank(pAa, g4), q_sl,
                                     khb[p][0:64, g4 * T:(g4 + 1) * T],
                                     start=True, stop=True)
                        for g4 in range(4):
                            ins = t.matmul(bank(pBa, g4), q_sl,
                                           khb[p][0:64, 2048 + g4 * T:][:, 0:T],
                                           start=True, stop=True)
                        tsem.inc(ins)
                        t.wait_ge(vsem.h, vA0 + 3 * qc + 2)
                        tsem.inc(t.transpose(pAa[0:1, 0:128], gcol[:, 2:3], idf[:, :]))

                @block.vector
                def _(v, tA0=tA0):
                    for qc in range(NQC):
                        v.wait_ge(tsem.h, tA0 + 2 * qc + 1)
                        v.reduce_max(gcol[:, 0:1], pAa[:, :], axis=AX.X, negate=True)
                        vsem.inc(v.reduce_max(gcol[:, 1:2], pBa[:, :], axis=AX.X, negate=True))
                        v.wait_ge(vsem.h, vsem.n)
                        vsem.inc(v.tensor_tensor(gcol[:, 2:3], gcol[:, 0:1], gcol[:, 1:2],
                                                 ALU.min))
                        v.wait_ge(tsem.h, tA0 + 2 * qc + 2)
                        vsem.inc(v.tensor_copy(mrow[0:1, qc * 128:(qc + 1) * 128],
                                               pAa[0:1, 0:128]))
                vA_end = vsem.n

                @block.sync
                def _(sy, p=p, vA_end=vA_end):
                    sy.wait_ge(vsem.h, vA_end)
                    sy.dma_start(out=qtile[p][64:65, :], in_=mrow[0:1, :]).then_inc(qsem.h, 16)
                    qsem.n += 16
                q_mrow = qsem.n

                for qb in range(NTT):
                    tB0, vB0, aB0 = tsem.n, vsem.n, asem.n

                    def t_sTq(g, tB0=tB0):
                        return tB0 + 2 * g + 1

                    def t_AVq(g, tB0=tB0):
                        return tB0 + 2 * g + 2

                    t_rsb = tB0 + 17

                    def s_exp(g, aB0=aB0):
                        return aB0 + g + 1

                    @block.tensor
                    def _(t, h=h, p=p, qb=qb, tB0=tB0, vB0=vB0, aB0=aB0,
                          t_sTq=t_sTq, t_AVq=t_AVq, s_exp=s_exp, t_rsb=t_rsb,
                          q_mrow=q_mrow):
                        t.wait_ge(vsem.h, vB0)
                        t.wait_ge(qsem.h, q_mrow)
                        if qb > 0:
                            t.wait_ge(asem.h, aB0)
                        q_all = qtile[p][0:65, qb * T:(qb + 1) * T]

                        def sTq(g):
                            ins = None
                            for j in range(4):
                                kc = 4 * g + j
                                ins = t.matmul(bank(pAa, j),
                                               khb[p][0:65, kc * 128:(kc + 1) * 128],
                                               q_all, start=True, stop=True)
                            tsem.inc(ins)
                        sTq(0)
                        for g in range(8):
                            t.wait_ge(asem.h, s_exp(g))
                            ins = None
                            for j in range(4):
                                kc = 4 * g + j
                                ins = t.matmul(pBa[0:65, 0:512],
                                               vh[p][:, kc * 65:(kc + 1) * 65],
                                               att[g % 2][:, j * 512:(j + 1) * 512],
                                               start=(kc == 0), stop=(kc == NKC - 1))
                            tsem.inc(ins)
                            if g + 1 < 8:
                                sTq(g + 1)
                        t.wait_ge(vsem.h, vB0 + 1)
                        tsem.inc(t.matmul(pBa[:, 512:1024], onesf[64:65, 0:128],
                                          srow[64:65, :], start=True, stop=True))
                        assert tsem.n == t_rsb

                    @block.scalar
                    def _(s, qb=qb, tB0=tB0, t_sTq=t_sTq, t_AVq=t_AVq, t_rsb=t_rsb):
                        for g in range(8):
                            s.wait_ge(tsem.h, t_sTq(g))
                            if g >= 2:
                                s.wait_ge(tsem.h, t_AVq(g - 2))
                            asem.inc(s.activation(att[g % 2][:, :], pAa[:, 0:2048], AF.Exp))
                        s.wait_ge(tsem.h, t_rsb)
                        asem.inc(s.copy(rsb[0:64, :], pBa[0:64, 512:1024]))

                    @block.vector
                    def _(v, lo=lo, fcq=fcq, qb=qb, tB0=tB0, aB0=aB0,
                          t_AVq=t_AVq, q_h0=q_h0):
                        v.wait_ge(tsem.h, t_AVq(7))
                        vsem.inc(v.reciprocal(srow[64:65, :], pBa[64:65, 0:512]))
                        v.wait_ge(asem.h, aB0 + 9)
                        if lo == 0:
                            vsem.inc(v.tensor_mul(qoz[0:64, fcq * SL + qb * T:][:, 0:T],
                                                  pBa[0:64, 0:512], rsb[0:64, :]))
                        else:
                            if qb > 0:
                                v.wait_ge(qsem.h, q_h0 + 16 + 16 * qb)
                            vsem.inc(v.tensor_mul(obuf[:, :], pBa[0:64, 0:512],
                                                  rsb[0:64, :]))
                    v_oscale = vsem.n

                    if lo == 64:
                        @block.sync
                        def _(sy, fcq=fcq, qb=qb, v_oscale=v_oscale):
                            sy.wait_ge(vsem.h, v_oscale)
                            sy.dma_start(out=qoz[64:128, fcq * SL + qb * T:][:, 0:T],
                                         in_=obuf[:, :]).then_inc(qsem.h, 16)
                            qsem.n += 16
                head_t_end[h] = tsem.n

            t_heads = tsem.n

            @block.gpsimd
            def _(g, t_heads=t_heads):
                g.wait_ge(tsem.h, t_heads)
                g.dma_start(out=wta[:, :].rearrange("p (dc f) -> p dc f", f=D),
                            in_=woT.rearrange("(dc p) f -> p dc f", p=128)
                            ).then_inc(dsem.h, 16)
                dsem.n += 16
            d_w_wo = dsem.n
            v_wo0, q_wo0 = vsem.n, qsem.n

            @block.tensor
            def _(t, v_wo0=v_wo0, q_wo0=q_wo0, d_w_wo=d_w_wo):
                t.wait_ge(vsem.h, v_wo0)
                t.wait_ge(qsem.h, q_wo0)
                t.wait_ge(dsem.h, d_w_wo)
                for dc in range(NDC):
                    for tt in range(NTT):
                        k = dc * NTT + tt
                        if k >= 2:
                            t.wait_ge(vsem.h, v_wo0 + k - 1)
                        for fc in range(NDC):
                            ins = t.matmul(bank(pAa, k % 2),
                                           wta[:, fc * D + dc * 128:][:, 0:128],
                                           qoz[:, fc * SL + tt * T:][:, 0:T],
                                           start=(fc == 0), stop=(fc == NDC - 1))
                        tsem.inc(ins)
            t_wo0 = tsem.n - NDC * NTT

            @block.vector
            def _(v, t_wo0=t_wo0):
                for dc in range(NDC):
                    for tt in range(NTT):
                        k = dc * NTT + tt
                        v.wait_ge(tsem.h, t_wo0 + k + 1)
                        x_sl = xfm[:, dc * SL + tt * T:][:, 0:T]
                        vsem.inc(v.scalar_tensor_tensor(
                            x_sl, bank(pAa, k % 2), battn[:, 3 * NDC + dc:][:, 0:1],
                            x_sl, ALU.add, ALU.add))

        if stop_after != "p0":
            layernorm(fold_x=True, with_gate=True)
        if stop_after in ("u1", "c1"):
            t_d, v_d, a_d = tsem.n, vsem.n, asem.n

            @block.gpsimd
            def _(g):
                g.wait_ge(tsem.h, t_d)
                g.wait_ge(vsem.h, v_d)
                g.wait_ge(asem.h, a_d)
                if stop_after == "u1":
                    for dc in range(NDC):
                        g.dma_start(out=outT[dc * 128:(dc + 1) * 128, :],
                                    in_=ub[:, dc * SL:(dc + 1) * SL]).then_inc(dsem.h, 16)
                        dsem.n += 16
                else:
                    g.dma_start(out=outT[0:8, :], in_=cT[0:8, :]).then_inc(dsem.h, 16)
                    dsem.n += 16
                g.wait_ge(dsem.h, dsem.n)
        if stop_after not in ("p0", "ln1", "u1", "c1"):
            ffn_experts(0)
        if stop_after not in ("p0", "ln1", "u1", "c1", "ffn1"):
            layernorm(fold_x=False, with_gate=False)
            attention("qkv" if stop_after == "qkv" else "all")
        if stop_after not in ("p0", "ln1", "u1", "c1", "ffn1", "qkv", "attn"):
            layernorm(fold_x=True, with_gate=True)
            ffn_experts(1)

        t_fin, v_fin, a_fin = tsem.n, vsem.n, asem.n
        skip_out = stop_after in ("u1", "c1")

        @block.sync
        def _(sy):
            if skip_out:
                sy.wait_ge(dsem.h, dsem.n)
                return
            sy.wait_ge(vsem.h, v_fin)
            sy.wait_ge(tsem.h, t_fin)
            sy.wait_ge(asem.h, a_fin)
            for dc in range(NDC):
                sy.dma_start(out=outT[dc * 128:(dc + 1) * 128, :],
                             in_=xfm[:, dc * SL:(dc + 1) * SL]).then_inc(qsem.h, 16)
            qsem.n += NDC * 16
            sy.wait_ge(qsem.h, qsem.n)

    return nc


def kernel(L0_attn_x, L0_attn_Wq, L0_attn_bq, L0_attn_Wk, L0_attn_bk,
           L0_attn_Wv, L0_attn_bv, L0_attn_Wo, L0_attn_bo,
           L0_ffn_Wg, L0_ffn_bg, L0_ffn_W1, L0_ffn_W2, L0_ffn_W3):
    x = np.ascontiguousarray(np.asarray(L0_attn_x, dtype=np.float32)).reshape(B * S, D)
    com = dict(
        wqT=np.ascontiguousarray(np.asarray(L0_attn_Wq, np.float32).T),
        wkT=np.ascontiguousarray(np.asarray(L0_attn_Wk, np.float32).T),
        wvT=np.ascontiguousarray(np.asarray(L0_attn_Wv, np.float32).T),
        woT=np.ascontiguousarray(np.asarray(L0_attn_Wo, np.float32).T),
        bq=np.asarray(L0_attn_bq, np.float32), bk=np.asarray(L0_attn_bk, np.float32),
        bv=np.asarray(L0_attn_bv, np.float32), bo=np.asarray(L0_attn_bo, np.float32),
        wg=np.asarray(L0_ffn_Wg, np.float32), bg=np.asarray(L0_ffn_bg, np.float32),
        w1=np.ascontiguousarray(np.asarray(L0_ffn_W1, np.float32)),
        w2=np.ascontiguousarray(np.asarray(L0_ffn_W2, np.float32)),
        w3=np.ascontiguousarray(np.asarray(L0_ffn_W3, np.float32)),
    )
    in_maps = []
    for c in range(NCORE):
        xs = np.ascontiguousarray(x[c * SL:(c + 1) * SL, :].T)
        in_maps.append(dict(xT=xs, **com))
    nc = build_program()
    res = run_bass_kernel_spmd(nc, in_maps, core_ids=list(range(NCORE)))
    outs = [np.asarray(r["outT"]).T for r in res.results]
    return np.concatenate(outs, axis=0).reshape(B, S, D)


# version 04
# speedup vs baseline: 1.1413x; 1.1264x over previous
import numpy as np
import concourse.bass as bass
import concourse.mybir as mybir
from concourse.bass_utils import run_bass_kernel_spmd

F32 = mybir.dt.float32
BF16 = mybir.dt.bfloat16
AF = mybir.ActivationFunctionType
ALU = mybir.AluOpType
AX = mybir.AxisListType

NCORE = 8
B, S, D = 4, 4096, 1024
SL = 2048
NH, HD = 16, 64
E, H = 8, 4096
NDC = D // 128
T = 512
NTT = SL // T
HC = 512
NHC = H // HC
SF = 4096
NKC = SF // 128
NQC = SL // 128
SCALE = 0.125
EPS = 1e-5


class Sem:
    def __init__(self, nc, ctx, name):
        self.h = ctx.enter_context(nc.semaphore(name))
        self.n = 0

    def inc(self, inst, k=1):
        inst.then_inc(self.h, k)
        self.n += k
        return self.n


def build_program(stop_after="all"):
    nc = bass.Bass("TRN2", target_bir_lowering=False, detect_race_conditions=False)

    xT = nc.dram_tensor("xT", [D, SL], F32, kind="ExternalInput").ap()
    wqT = nc.dram_tensor("wqT", [D, D], F32, kind="ExternalInput").ap()
    wkT = nc.dram_tensor("wkT", [D, D], F32, kind="ExternalInput").ap()
    wvT = nc.dram_tensor("wvT", [D, D], F32, kind="ExternalInput").ap()
    woT = nc.dram_tensor("woT", [D, D], F32, kind="ExternalInput").ap()
    bqd = nc.dram_tensor("bq", [D], F32, kind="ExternalInput").ap()
    bkd = nc.dram_tensor("bk", [D], F32, kind="ExternalInput").ap()
    bvd = nc.dram_tensor("bv", [D], F32, kind="ExternalInput").ap()
    bod = nc.dram_tensor("bo", [D], F32, kind="ExternalInput").ap()
    wgd = nc.dram_tensor("wg", [D, E], F32, kind="ExternalInput").ap()
    bgd = nc.dram_tensor("bg", [E], F32, kind="ExternalInput").ap()
    w1d = nc.dram_tensor("w1", [E, D, H], F32, kind="ExternalInput").ap()
    w2d = nc.dram_tensor("w2", [E, H, D], F32, kind="ExternalInput").ap()
    w3d = nc.dram_tensor("w3", [E, D, H], F32, kind="ExternalInput").ap()
    outT = nc.dram_tensor("outT", [D, SL], F32, kind="ExternalOutput").ap()

    kexp = nc.dram_tensor("kexp", [D, SL], BF16).ap()
    vexp = nc.dram_tensor("vexp", [SL, D], BF16).ap()
    kg = nc.dram_tensor("kg", [2 * D, SL], BF16).ap()
    vg = nc.dram_tensor("vg", [SF, D], BF16).ap()
    cdram = nc.dram_tensor("cdram", [E, SL], BF16).ap()
    mstore = nc.dram_tensor("mstore", [NH, SL], BF16).ap()

    from contextlib import ExitStack
    with ExitStack() as ctx:
        BASE = 16512
        off = [BASE]

        def At(name, shape, dt_, offset=None):
            nbytes = int(np.prod(shape[1:])) * mybir.dt.size(dt_)
            if offset is None:
                offset = off[0]
                off[0] += (nbytes + 31) // 32 * 32
            return nc.alloc_sbuf_tensor_at(name, list(shape), dt_, offset=offset).ap()

        xfm = At("xfm", [128, NDC * SL], F32)
        ub = At("ub", [128, NDC * SL], BF16)
        qoz = At("qoz", [128, NDC * SL], BF16)
        wreg = off[0]
        w1t = [At(f"w1t{p}", [128, NDC * HC], BF16, wreg + p * 24576) for p in range(2)]
        w3t = [At(f"w3t{p}", [128, NDC * HC], BF16, wreg + p * 24576 + 8192) for p in range(2)]
        w2t = [At(f"w2t{p}", [128, 4 * D], BF16, wreg + p * 24576 + 16384) for p in range(2)]
        wta = At("wta", [128, NDC * D], BF16, wreg)
        qtile = [At(f"qtile{p}", [65, SL], BF16, wreg + p * 4096) for p in range(2)]
        khb = [At(f"khb{p}", [65, SF], BF16, wreg + 8192 + p * 8192) for p in range(2)]
        khA = At("khA", [128, SF], BF16, wreg + 24576)
        vh = [At(f"vh{p}", [128, NKC * 65], BF16, wreg + 32768 + p * 4160) for p in range(2)]
        kvx = At("kvx", [128, SL], BF16, wreg + 16384)
        bvrowb = At("bvrowb", [1, D], BF16, wreg + 20480)
        off[0] = wreg + 49152
        screg = off[0]
        sq = [At(f"sq{p}", [128, T], F32, screg + p * 2048) for p in range(2)]
        uf32 = [At(f"uf32{p}", [128, T], F32, screg + 4096 + p * 2048) for p in range(2)]
        hbuf = At("hbuf", [128, 4 * T], BF16, screg + 8192)
        s1 = [At(f"s1{p}", [128, T], F32, screg + 12288 + p * 2048) for p in range(2)]
        mrow = At("mrow", [1, SL], BF16, screg)
        rsb = At("rsb", [128, 512], F32, screg + 5120)
        srow = At("srow", [128, 512], F32, screg + 7168)
        obuf2 = [At(f"obuf{p}", [64, 512], BF16, screg + 9216 + p * 1024) for p in range(2)]
        att = [At("att0", [128, 4 * 512], BF16, wreg + 41088),
               At("att1", [128, 4 * 512], BF16, screg + 10240)]
        off[0] = screg + 16384
        rowbuf = At("rowbuf", [1, 4 * T], F32)
        ctoff = off[0]
        cT = At("cT", [8, SL], BF16)
        crow = At("crow", [1, SL], BF16, offset=ctoff)
        battn = At("battn", [128, 4 * NDC], F32)
        onesf = At("onesf", [128, 128], F32)
        onesb = At("onesb", [1, 128], BF16)
        idf = At("idf", [128, 128], F32)
        idb = At("idb", [128, 128], BF16)
        wg_sb = At("wg_sb", [128, NDC * E], F32)
        g1b = At("g1b", [128, E], F32)
        bgb = At("bgb", [128, E], F32)
        grow = At("grow", [1, 2 * E], F32)
        gsc = At("gsc", [128, 4 * E], F32)
        gcol = At("gcol", [128, 8], F32)
        assert off[0] <= 229344, off[0]

        ROW_TMP, ROW_R, ROW_MR, ROW_NM = (rowbuf[0:1, k * T:(k + 1) * T] for k in range(4))

        pA = ctx.enter_context(nc.psum_tensor("pA", [128, 2048], F32))
        pB = ctx.enter_context(nc.psum_tensor("pB", [128, 2048], F32))
        pAa, pBa = pA[:, :], pB[:, :]
        pAb = pA[:, :].bitcast(BF16)

        def bank(ap_, i):
            return ap_[:, i * 512:(i + 1) * 512]

        names = ["qsem", "dsem", "tsem", "vsem", "asem", "ck", "cv", "esem",
                 "ws0", "ws1", "ks0", "ks1", "kb0", "kb1"]
        SM = {nm: Sem(nc, ctx, nm) for nm in names}
        qsem, dsem, tsem, vsem, asem, ck, cv, esem, ws0, ws1, ks0, ks1, kb0, kb1 = \
            (SM[n] for n in names)
        wsp = [ws0, ws1]
        ksp = [ks0, ks1]
        kbp = [kb0, kb1]
        block = ctx.enter_context(nc.Block())

        @block.gpsimd
        def _(g):
            g.memset(onesf[:, :], 1.0)
            g.memset(onesb[:, :], 1.0)
            g.memset(idf[:, :], 1.0)
            esem.inc(g.memset(idb[:, :], 1.0))
            g.wait_ge(esem.h, 1)
            g.affine_select(idf[:, :], idf[:, :], [[1, 128]], ALU.is_equal, 0.0,
                            base=0, channel_multiplier=-1)
            esem.inc(g.affine_select(idb[:, :], idb[:, :], [[1, 128]], ALU.is_equal, 0.0,
                                     base=0, channel_multiplier=-1))
            with nc.allow_non_contiguous_dma("tiny one-time bias loads"):
                for j, bb in enumerate([bqd, bkd, bvd, bod]):
                    g.dma_start(out=battn[:, j * NDC:(j + 1) * NDC],
                                in_=bb.rearrange("(dc p) -> p dc", p=128),
                                allow_slow_non_contiguous=True).then_inc(dsem.h, 16)
            g.dma_start(out=wg_sb[:, :].rearrange("p (dc e) -> p dc e", e=E),
                        in_=wgd.rearrange("(dc p) e -> p dc e", p=128)).then_inc(dsem.h, 16)
            g.dma_start(out=grow[0:1, E:2 * E], in_=bgd[None, :]).then_inc(dsem.h, 16)
            dsem.n += 6 * 16

        @block.sync
        def _(sy):
            for dc in range(NDC):
                sy.dma_start(out=xfm[:, dc * SL:(dc + 1) * SL],
                             in_=xT[dc * 128:(dc + 1) * 128, :]).then_inc(qsem.h, 16)
            qsem.n += NDC * 16
        q_x = qsem.n

        @block.tensor
        def _(t):
            t.wait_ge(dsem.h, dsem.n)
            t.wait_ge(esem.h, 2)
            t.wait_ge(qsem.h, q_x)
            for dc in range(NDC):
                ins = t.matmul(pBa[0:1, 0:E], onesf[:, 0:1], wg_sb[:, dc * E:(dc + 1) * E],
                               start=(dc == 0), stop=(dc == NDC - 1))
            tsem.inc(ins)
        ev0 = tsem.n

        @block.vector
        def _(v):
            v.wait_ge(tsem.h, ev0)
            vsem.inc(v.tensor_copy(grow[0:1, 0:E], pBa[0:1, 0:E]))
        ev1 = vsem.n

        @block.tensor
        def _(t):
            t.wait_ge(vsem.h, ev1)
            tsem.inc(t.matmul(pAa[:, 0:2 * E], onesf[0:1, 0:128], grow[0:1, :],
                              start=True, stop=True))
        ev2 = tsem.n

        @block.vector
        def _(v):
            v.wait_ge(tsem.h, ev2)
            v.tensor_copy(g1b[:, :], pAa[:, 0:E])
            v.tensor_scalar_mul(battn[:, 0:NDC], battn[:, 0:NDC], SCALE)
            vsem.inc(v.tensor_copy(bgb[:, :], pAa[:, E:2 * E]))

        def gate(tt, v_rows):
            for tc in range(4):
                vg0 = vsem.n
                tok0 = tt * T + tc * 128

                @block.tensor
                def _(t, tt=tt, tc=tc, tok0=tok0, vg0=vg0, v_rows=v_rows):
                    if tc == 0:
                        t.wait_ge(vsem.h, v_rows)
                    else:
                        t.wait_ge(vsem.h, vg0)
                    for dc in range(NDC):
                        ins = t.matmul(pBa[0:128, 0:E], xfm[:, dc * SL + tok0:][:, 0:128],
                                       wg_sb[:, dc * E:(dc + 1) * E],
                                       start=(dc == 0), stop=(dc == NDC - 1))
                    tsem.inc(ins)
                    tsem.inc(t.transpose(pAa[0:128, 1536:1537],
                                         ROW_NM[0:1, tc * 128:(tc + 1) * 128], onesf[0:1, 0:1]))
                    tsem.inc(t.transpose(pAa[0:128, 1537:1538],
                                         ROW_R[0:1, tc * 128:(tc + 1) * 128], onesf[0:1, 0:1]))
                t_g = tsem.n

                @block.vector
                def _(v, t_g=t_g):
                    def vf(ins):
                        vsem.inc(ins)
                        v.wait_ge(vsem.h, vsem.n)
                    v.wait_ge(tsem.h, t_g)
                    vf(v.tensor_copy(gcol[:, 6:8], pAa[:, 1536:1538]))
                    vf(v.scalar_tensor_tensor(gsc[:, 0:E], g1b[:, :], gcol[:, 6:7],
                                              pBa[0:128, 0:E], ALU.mult, ALU.add))
                    vf(v.tensor_scalar(gsc[:, 0:E], gsc[:, 0:E], gcol[:, 7:8], None, ALU.mult))
                    vf(v.tensor_add(gsc[:, 0:E], gsc[:, 0:E], bgb[:, :]))
                    vf(v.reduce_max(gcol[:, 0:1], gsc[:, 0:E], axis=AX.X))
                    vf(v.tensor_scalar_mul(gcol[:, 1:2], gcol[:, 0:1], -1.0))
                    vf(v.tensor_scalar(gsc[:, E:2 * E], gsc[:, 0:E], gcol[:, 0:1], None, ALU.is_ge))
                    vf(v.scalar_tensor_tensor(gsc[:, 2 * E:3 * E], gsc[:, E:2 * E], -1e30,
                                              gsc[:, 0:E], ALU.mult, ALU.add))
                    vsem.inc(v.reduce_max(gcol[:, 2:3], gsc[:, 2 * E:3 * E], axis=AX.X))
                v_g1 = vsem.n

                @block.scalar
                def _(s, v_g1=v_g1):
                    s.wait_ge(vsem.h, v_g1)
                    s.activation(gsc[:, 2 * E:3 * E], gsc[:, 0:E], AF.Exp, bias=gcol[:, 1:2])
                    asem.inc(s.activation(gcol[:, 3:4], gcol[:, 2:3], AF.Exp, bias=gcol[:, 1:2]))
                a_g = asem.n

                @block.vector
                def _(v, a_g=a_g):
                    def vf(ins):
                        vsem.inc(ins)
                        v.wait_ge(vsem.h, vsem.n)
                    v.wait_ge(asem.h, a_g)
                    vf(v.tensor_scalar_add(gcol[:, 4:5], gcol[:, 3:4], 1.0))
                    vf(v.reciprocal(gcol[:, 5:6], gcol[:, 4:5]))
                    vf(v.tensor_scalar(gsc[:, E:2 * E], gsc[:, 0:E], gcol[:, 2:3], None, ALU.is_ge))
                    vf(v.tensor_mul(gsc[:, 2 * E:3 * E], gsc[:, 2 * E:3 * E],
                                    gsc[:, E:2 * E]))
                    vsem.inc(v.tensor_scalar(gsc[:, 3 * E:4 * E].bitcast(BF16)[:, 0:E],
                                             gsc[:, 2 * E:3 * E], gcol[:, 5:6], None, ALU.mult))
                v_g2 = vsem.n

                @block.tensor
                def _(t, v_g2=v_g2):
                    t.wait_ge(vsem.h, v_g2)
                    tsem.inc(t.transpose(pAb[0:8, 3072:3200],
                                         gsc[:, 3 * E:4 * E].bitcast(BF16)[:, 0:E], idb[:, :]))
                t_c = tsem.n

                @block.vector
                def _(v, t_c=t_c, tok0=tok0):
                    v.wait_ge(tsem.h, t_c)
                    vsem.inc(v.tensor_copy(cT[0:8, tok0:tok0 + 128], pAb[0:8, 3072:3200]))

        def layernorm(fold_x, with_gate):
            v_enter = vsem.n
            prev_vu = [v_enter]
            for tt in range(NTT):
                av0 = asem.n
                tv0 = tsem.n

                @block.scalar
                def _(s, tt=tt, tv0=tv0, v_enter=v_enter):
                    if tt == 0:
                        s.wait_ge(vsem.h, v_enter)
                        s.wait_ge(qsem.h, q_x)
                    for dc in range(NDC):
                        if dc >= 2:
                            s.wait_ge(tsem.h, tv0 + dc - 1)
                        asem.inc(s.activation(sq[dc % 2][:, :],
                                              xfm[:, dc * SL + tt * T:][:, 0:T], AF.Square))

                @block.tensor
                def _(t, tt=tt, av0=av0, wait_v=prev_vu[0]):
                    t.wait_ge(vsem.h, wait_v)
                    for dc in range(NDC):
                        x_sl = xfm[:, dc * SL + tt * T:][:, 0:T]
                        t.matmul(pBa[0:1, 1024:1536], onesf[:, 0:1], x_sl,
                                 start=(dc == 0), stop=(dc == NDC - 1))
                        t.wait_ge(asem.h, av0 + dc + 1)
                        tsem.inc(t.matmul(pBa[0:1, 1536:2048], onesf[:, 0:1], sq[dc % 2][:, :],
                                          start=(dc == 0), stop=(dc == NDC - 1)))
                t_sums = tsem.n

                @block.vector
                def _(v, t_sums=t_sums):
                    v.wait_ge(tsem.h, t_sums)
                    v.tensor_scalar_mul(pBa[0:1, 1024:1536], pBa[0:1, 1024:1536], 1.0 / D)
                    vsem.inc(v.tensor_scalar_mul(pBa[0:1, 1536:2048], pBa[0:1, 1536:2048], 1.0 / D))
                v_mean = vsem.n

                @block.scalar
                def _(s, v_mean=v_mean):
                    s.wait_ge(vsem.h, v_mean)
                    asem.inc(s.activation(ROW_TMP, pBa[0:1, 1024:1536], AF.Square))
                a_msq = asem.n

                @block.vector
                def _(v, a_msq=a_msq):
                    v.wait_ge(asem.h, a_msq)
                    v.tensor_sub(pBa[0:1, 1536:2048], pBa[0:1, 1536:2048], ROW_TMP)
                    vsem.inc(v.tensor_scalar_add(pBa[0:1, 1536:2048], pBa[0:1, 1536:2048], EPS))
                v_var = vsem.n

                @block.scalar
                def _(s, v_var=v_var):
                    s.wait_ge(vsem.h, v_var)
                    asem.inc(s.activation(ROW_TMP, pBa[0:1, 1536:2048], AF.Ln))
                a_ln = asem.n

                @block.vector
                def _(v, a_ln=a_ln):
                    v.wait_ge(asem.h, a_ln)
                    vsem.inc(v.tensor_scalar_mul(ROW_TMP, ROW_TMP, -0.5))
                v_ln = vsem.n

                @block.scalar
                def _(s, v_ln=v_ln):
                    s.wait_ge(vsem.h, v_ln)
                    asem.inc(s.activation(ROW_R, ROW_TMP, AF.Exp))
                a_r = asem.n

                @block.vector
                def _(v, a_r=a_r, with_gate=with_gate):
                    v.wait_ge(asem.h, a_r)
                    v.tensor_mul(ROW_MR, pBa[0:1, 1024:1536], ROW_R)
                    if with_gate:
                        vsem.inc(v.tensor_scalar_mul(ROW_NM, pBa[0:1, 1024:1536], -1.0))
                    else:
                        vsem.inc(v.engine_nop())
                v_rows = vsem.n

                if with_gate:
                    gate(tt, v_rows)

                @block.tensor
                def _(t, v_rows=v_rows):
                    t.wait_ge(vsem.h, v_rows)
                    tsem.inc(t.matmul(pBa[:, 1024:1536], onesf[0:1, 0:128], ROW_R,
                                      start=True, stop=True))
                    tsem.inc(t.matmul(pBa[:, 1536:2048], onesf[0:1, 0:128], ROW_MR,
                                      start=True, stop=True))
                t_bc = tsem.n
                vu0 = vsem.n
                au0 = asem.n

                @block.vector
                def _(v, tt=tt, t_bc=t_bc, fold_x=fold_x, au0=au0):
                    v.wait_ge(tsem.h, t_bc)
                    for dc in range(NDC):
                        x_sl = xfm[:, dc * SL + tt * T:][:, 0:T]
                        p = dc % 2
                        if dc >= 2:
                            v.wait_ge(asem.h, au0 + dc - 1)
                        v.tensor_mul(uf32[p][:, :], x_sl, pBa[:, 1024:1536])
                        vsem.inc(v.tensor_sub(uf32[p][:, :], uf32[p][:, :], pBa[:, 1536:2048]))
                        if fold_x:
                            vsem.inc(v.scalar_tensor_tensor(x_sl, uf32[p][:, :], 0.5, x_sl,
                                                            ALU.mult, ALU.add))
                        else:
                            vsem.inc(v.engine_nop())

                @block.scalar
                def _(s, tt=tt, vu0=vu0):
                    for dc in range(NDC):
                        u_sl = ub[:, dc * SL + tt * T:][:, 0:T]
                        s.wait_ge(vsem.h, vu0 + 2 * dc + 1)
                        asem.inc(s.copy(u_sl, uf32[dc % 2][:, :]))
                prev_vu[0] = vsem.n

        def ffn_experts(call):
            t0 = tsem.n
            v0 = vsem.n
            a0 = asem.n
            a_enter = asem.n
            v_pre = vsem.n
            d0w = [wsp[0].n, wsp[1].n]

            @block.sync
            def _(sy, v_pre=v_pre):
                sy.wait_ge(vsem.h, v_pre)
                sy.dma_start(out=cdram[:, :], in_=cT[0:8, :]).then_inc(qsem.h, 16)
                qsem.n += 16
            q_cflat = qsem.n

            def t_base(e):
                return t0 + e * (4 + 32 * 16)

            def t_cb(e, tt):
                return t_base(e) + tt + 1

            def t_blk(e, hc, tt):
                return t_base(e) + 4 + (hc * NTT + tt) * 16

            def t_h1(e, hc, tt, i):
                return t_blk(e, hc, tt) + 2 * i + 1

            def t_h3(e, hc, tt, i):
                return t_blk(e, hc, tt) + 2 * i + 2

            def t_y(e, hc, tt, dc):
                return t_blk(e, hc, tt) + 8 + dc + 1

            def v_base(e):
                return v0 + e * (4 + 32 * 12)

            def v_z(e, tt):
                return v_base(e) + tt + 1

            def v_blkbase(e, hc, tt):
                return v_base(e) + 4 + (hc * NTT + tt) * 12

            def v_hmul(e, hc, tt, i):
                return v_blkbase(e, hc, tt) + i + 1

            def v_yadd(e, hc, tt, dc):
                return v_blkbase(e, hc, tt) + 4 + dc + 1

            def lin_h(e, hc, tt, i):
                return ((e * NHC + hc) * NTT + tt) * 4 + i

            def lin_y(e, hc, tt, dc):
                return ((e * NHC + hc) * NTT + tt) * 8 + dc

            def unlin_h(k):
                return (k // (4 * NTT * NHC), (k // (4 * NTT)) % NHC,
                        (k // 4) % NTT, k % 4)

            def unlin_y(k):
                return (k // (8 * NTT * NHC), (k // (8 * NTT)) % NHC,
                        (k // 8) % NTT, k % 8)

            def a_silu(e, hc, tt, i):
                return a0 + ((e * NHC + hc) * NTT + tt) * 4 + i + 1

            @block.gpsimd
            def _(g, call=call, t0=t0):
                for c in range(E * NHC):
                    e, hc = divmod(c, NHC)
                    p = c % 2
                    if c < 2:
                        g.wait_ge(tsem.h, t0)
                    else:
                        ee, hh = divmod(c - 2, NHC)
                        g.wait_ge(tsem.h, t_y(ee, hh, NTT - 1, 7))
                    g.dma_start(out=w1t[p].rearrange("p (dc h) -> p dc h", h=HC),
                                in_=w1d[e][:, hc * HC:(hc + 1) * HC]
                                .rearrange("(dc p) h -> p dc h", p=128)).then_inc(wsp[p].h, 16)
                    g.dma_start(out=w3t[p].rearrange("p (dc h) -> p dc h", h=HC),
                                in_=w3d[e][:, hc * HC:(hc + 1) * HC]
                                .rearrange("(dc p) h -> p dc h", p=128)).then_inc(wsp[p].h, 16)
                    g.dma_start(out=w2t[p].rearrange("p (hh d) -> p hh d", d=D),
                                in_=w2d[e][hc * HC:(hc + 1) * HC, :]
                                .rearrange("(hh p) d -> p hh d", p=128)).then_inc(wsp[p].h, 16)
                    wsp[p].n += 48

            @block.sync
            def _(sy, q_cflat=q_cflat):
                for e in range(E):
                    if e == 0:
                        sy.wait_ge(qsem.h, q_cflat)
                    else:
                        sy.wait_ge(tsem.h, t_cb(e - 1, NTT - 1))
                    sy.dma_start(out=crow[0:1, :], in_=cdram[e:e + 1, :]).then_inc(qsem.h, 16)
                    qsem.n += 16
            q_crow0 = q_cflat

            @block.tensor
            def _(t, call=call, q_crow0=q_crow0, d0w=d0w):
                for e in range(E):
                    for tt in range(NTT):
                        t.wait_ge(qsem.h, q_crow0 + 16 * (e + 1))
                        if tt > 0:
                            t.wait_ge(vsem.h, v_z(e, tt - 1))
                        if e > 0 and tt == 0:
                            t.wait_ge(vsem.h, v_z(e - 1, NTT - 1))
                        tsem.inc(t.matmul(pBa[:, 1024:1536], onesb[:, 0:128],
                                          crow[0:1, tt * T:(tt + 1) * T], start=True, stop=True))
                        assert tsem.n == t_cb(e, tt)
                    for hc in range(NHC):
                        c = e * NHC + hc
                        p = c % 2
                        t.wait_ge(wsp[p].h, d0w[p] + 48 * (c // 2 + 1))
                        for tt in range(NTT):
                            t.wait_ge(vsem.h, v_z(e, tt))
                            for i in range(4):
                                kk = lin_h(e, hc, tt, i)
                                if kk >= 2:
                                    t.wait_ge(vsem.h, v_hmul(*unlin_h(kk - 2)))
                                pp = kk % 2
                                for dc in range(NDC):
                                    ins = t.matmul(bank(pAa, pp),
                                                   w1t[p][:, dc * HC + i * 128:][:, 0:128],
                                                   qoz[:, dc * SL + tt * T:][:, 0:T],
                                                   start=(dc == 0), stop=(dc == NDC - 1))
                                tsem.inc(ins)
                                for dc in range(NDC):
                                    ins = t.matmul(bank(pAa, 2 + pp),
                                                   w3t[p][:, dc * HC + i * 128:][:, 0:128],
                                                   qoz[:, dc * SL + tt * T:][:, 0:T],
                                                   start=(dc == 0), stop=(dc == NDC - 1))
                                tsem.inc(ins)
                                assert tsem.n == t_h3(e, hc, tt, i)
                            t.wait_ge(vsem.h, v_hmul(e, hc, tt, 3))
                            for dc in range(NDC):
                                kk = lin_y(e, hc, tt, dc)
                                if kk >= 2:
                                    t.wait_ge(vsem.h, v_yadd(*unlin_y(kk - 2)))
                                pp = kk % 2
                                for i in range(4):
                                    ins = t.matmul(bank(pBa, pp),
                                                   w2t[p][:, i * D + dc * 128:][:, 0:128],
                                                   hbuf[:, i * T:(i + 1) * T],
                                                   start=(i == 0), stop=(i == 3))
                                tsem.inc(ins)
                                assert tsem.n == t_y(e, hc, tt, dc)

            @block.vector
            def _(v, call=call, a_enter=a_enter):
                for e in range(E):
                    for tt in range(NTT):
                        if e == 0 and tt == 0:
                            v.wait_ge(asem.h, a_enter)
                        v.wait_ge(tsem.h, t_cb(e, tt))
                        if e > 0:
                            v.wait_ge(tsem.h, t_h3(e - 1, NHC - 1, tt, 3))
                        ins = None
                        for dc in range(NDC):
                            ins = v.tensor_mul(qoz[:, dc * SL + tt * T:][:, 0:T],
                                               ub[:, dc * SL + tt * T:][:, 0:T],
                                               pBa[:, 1024:1536])
                        vsem.inc(ins)
                        assert vsem.n == v_z(e, tt)
                    for hc in range(NHC):
                        for tt in range(NTT):
                            for i in range(4):
                                kk = lin_h(e, hc, tt, i)
                                pp = kk % 2
                                v.wait_ge(asem.h, a_silu(e, hc, tt, i))
                                v.wait_ge(tsem.h, t_h3(e, hc, tt, i))
                                vsem.inc(v.tensor_mul(hbuf[:, i * T:(i + 1) * T],
                                                      s1[pp][:, :], bank(pAa, 2 + pp)))
                                assert vsem.n == v_hmul(e, hc, tt, i)
                            for dc in range(NDC):
                                kk = lin_y(e, hc, tt, dc)
                                pp = kk % 2
                                v.wait_ge(tsem.h, t_y(e, hc, tt, dc))
                                x_sl = xfm[:, dc * SL + tt * T:][:, 0:T]
                                vsem.inc(v.scalar_tensor_tensor(x_sl, bank(pBa, pp), 0.5,
                                                                x_sl, ALU.mult, ALU.add))
                                assert vsem.n == v_yadd(e, hc, tt, dc)

            @block.scalar
            def _(s, call=call):
                for e in range(E):
                    for hc in range(NHC):
                        for tt in range(NTT):
                            for i in range(4):
                                kk = lin_h(e, hc, tt, i)
                                pp = kk % 2
                                s.wait_ge(tsem.h, t_h1(e, hc, tt, i))
                                if kk >= 2:
                                    s.wait_ge(vsem.h, v_hmul(*unlin_h(kk - 2)))
                                asem.inc(s.activation(s1[pp][:, :], bank(pAa, pp), AF.Silu))
                                assert asem.n == a_silu(e, hc, tt, i)

        def attention(stop_after="all"):
            a_enter = asem.n
            for wi, wsrc_ in enumerate([wqT, wkT, wvT]):
                t_pre = tsem.n

                @block.gpsimd
                def _(g, wsrc_=wsrc_, t_pre=t_pre, wi=wi):
                    g.wait_ge(tsem.h, t_pre)
                    if wi == 2:
                        g.dma_start(out=bvrowb[:, :], in_=bvd[None, :]).then_inc(dsem.h, 16)
                        dsem.n += 16
                    g.dma_start(out=wta[:, :].rearrange("p (dc f) -> p dc f", f=D),
                                in_=wsrc_.rearrange("(dc p) f -> p dc f", p=128)
                                ).then_inc(dsem.h, 16)
                    dsem.n += 16
                d_w = dsem.n

                if wi == 0:
                    a_pre = asem.n

                    @block.tensor
                    def _(t, d_w=d_w, a_pre=a_pre, a_enter=a_enter):
                        t.wait_ge(dsem.h, d_w)
                        t.wait_ge(asem.h, a_enter)
                        for fc in range(NDC):
                            for tt in range(NTT):
                                k = fc * NTT + tt
                                if k >= 2:
                                    t.wait_ge(asem.h, a_pre + k - 1)
                                for dc in range(NDC):
                                    ins = t.matmul(bank(pAa, k % 2),
                                                   wta[:, dc * D + fc * 128:][:, 0:128],
                                                   ub[:, dc * SL + tt * T:][:, 0:T],
                                                   start=(dc == 0), stop=(dc == NDC - 1))
                                tsem.inc(ins)

                    @block.scalar
                    def _(s, t_pre=t_pre):
                        for fc in range(NDC):
                            for tt in range(NTT):
                                k = fc * NTT + tt
                                s.wait_ge(tsem.h, t_pre + k + 1)
                                asem.inc(s.activation(qoz[:, fc * SL + tt * T:][:, 0:T],
                                                      bank(pAa, k % 2), AF.Identity,
                                                      bias=battn[:, 0 * NDC + fc:][:, 0:1],
                                                      scale=SCALE))
                elif wi == 1:
                    a_pre = asem.n
                    q_pre = qsem.n

                    @block.tensor
                    def _(t, d_w=d_w, a_pre=a_pre):
                        t.wait_ge(dsem.h, d_w)
                        t.wait_ge(asem.h, a_pre)
                        for fc in range(NDC):
                            for tt in range(NTT):
                                k = fc * NTT + tt
                                if k >= 2:
                                    t.wait_ge(asem.h, a_pre + k - 1)
                                for dc in range(NDC):
                                    ins = t.matmul(bank(pAa, k % 2),
                                                   wta[:, dc * D + fc * 128:][:, 0:128],
                                                   ub[:, dc * SL + tt * T:][:, 0:T],
                                                   start=(dc == 0), stop=(dc == NDC - 1))
                                tsem.inc(ins)

                    @block.scalar
                    def _(s, t_pre=t_pre, q_pre=q_pre, a_pre=a_pre):
                        for fc in range(NDC):
                            for tt in range(NTT):
                                k = fc * NTT + tt
                                s.wait_ge(tsem.h, t_pre + k + 1)
                                if fc >= 1 and tt == 0:
                                    s.wait_ge(qsem.h, q_pre + 16 * fc)
                                asem.inc(s.activation(kvx[:, tt * T:(tt + 1) * T],
                                                      bank(pAa, k % 2), AF.Identity,
                                                      bias=battn[:, 1 * NDC + fc:][:, 0:1]))

                    @block.sync
                    def _(sy, a_pre=a_pre):
                        for fc in range(NDC):
                            sy.wait_ge(asem.h, a_pre + NTT * (fc + 1))
                            sy.dma_start(out=kexp[fc * 128:(fc + 1) * 128, :],
                                         in_=kvx[:, :]).then_inc(qsem.h, 16)
                            qsem.n += 16
                    q_kexp = qsem.n

                    @block.gpsimd
                    def _(g, q_kexp=q_kexp):
                        g.wait_ge(qsem.h, q_kexp)
                        g.collective_compute(
                            "AllGather", ALU.bypass,
                            replica_groups=[[0, 1], [2, 3], [4, 5], [6, 7]],
                            ins=[kexp.opt()], outs=[kg.opt()]).then_inc(ck.h, 1)
                        ck.n = 1
                elif wi == 2:
                    v_pre = vsem.n
                    q_pre = qsem.n

                    @block.tensor
                    def _(t, d_w=d_w, v_pre=v_pre):
                        t.wait_ge(dsem.h, d_w)
                        t.wait_ge(vsem.h, v_pre)
                        for tc in range(NQC):
                            for fb in range(2):
                                k = tc * 2 + fb
                                if k >= 4:
                                    t.wait_ge(vsem.h, v_pre + k - 3)
                                for dc in range(NDC):
                                    t.matmul(bank(pBa, k % 4),
                                             ub[:, dc * SL + tc * 128:][:, 0:128],
                                             wta[:, dc * D + fb * T:][:, 0:T],
                                             start=(dc == 0), stop=False)
                                ins = t.matmul(bank(pBa, k % 4), onesb[:, 0:128],
                                               bvrowb[0:1, fb * T:(fb + 1) * T],
                                               start=False, stop=True)
                                tsem.inc(ins)

                    @block.vector
                    def _(v, t_pre=t_pre, q_pre=q_pre):
                        for tc in range(NQC):
                            for fb in range(2):
                                k = tc * 2 + fb
                                v.wait_ge(tsem.h, t_pre + k + 1)
                                if tc >= 1 and fb == 0:
                                    v.wait_ge(qsem.h, q_pre + 16 * tc)
                                vsem.inc(v.tensor_copy(kvx[:, fb * T:(fb + 1) * T],
                                                       bank(pBa, k % 4)))

                    @block.sync
                    def _(sy, v_pre=v_pre):
                        for tc in range(NQC):
                            sy.wait_ge(vsem.h, v_pre + 2 * (tc + 1))
                            sy.dma_start(out=vexp[tc * 128:(tc + 1) * 128, :],
                                         in_=kvx[:, 0:1024]).then_inc(qsem.h, 16)
                            qsem.n += 16
                    q_vexp = qsem.n

                    @block.gpsimd
                    def _(g, q_vexp=q_vexp):
                        g.wait_ge(qsem.h, q_vexp)
                        g.collective_compute(
                            "AllGather", ALU.bypass,
                            replica_groups=[[0, 1], [2, 3], [4, 5], [6, 7]],
                            ins=[vexp.opt()], outs=[vg.opt()]).then_inc(cv.h, 1)
                        cv.n = 1

            t_postv = tsem.n

            @block.gpsimd
            def _(g, t_postv=t_postv):
                g.wait_ge(tsem.h, t_postv)
                for p in range(2):
                    vv = vh[p].rearrange("p (k c) -> p k c", c=65)
                    g.memset(vv[:, :, 64:65], 1.0)
                    g.memset(khb[p][64:65, :], 1.0)
                g.memset(srow[:, :], 1.0).then_inc(esem.h, 1)
                esem.n += 1

            if stop_after == "qkv":
                return
            NF = 8 * NQC
            tfill = {}
            ttp = {}
            tBsT = {}
            tBAV = {}
            tBrs = {}
            tBend = {}
            tAend = {}
            vbaseA = {}
            sbaseB = {}
            q_mst = {}
            prev_osc = [0]
            prev_vred_tail = [None, None]

            for s in range(NH + 1):
                hA = s if s < NH else None
                hB = s - 1 if s >= 1 else None
                pA_ = None if hA is None else hA % 2
                pB_ = None if hB is None else hB % 2

                if hA is not None:
                    kse = ksp[pA_]
                    loA = 64 * (hA % 2)

                    @block.sync
                    def _(sy, hA=hA, loA=loA, kse=kse, s=s):
                        if s == 0:
                            sy.wait_ge(ck.h, 1)
                            sy.wait_ge(cv.h, 1)
                            sy.wait_ge(esem.h, esem.n)
                        if hA >= 2:
                            sy.wait_ge(tsem.h, tAend[hA - 2])
                        sy.dma_start(out=khA[loA:loA + 64, 0:SL],
                                     in_=kg[64 * hA:64 * hA + 64, :]).then_inc(kse.h, 16)
                        sy.dma_start(out=khA[loA:loA + 64, SL:2 * SL],
                                     in_=kg[D + 64 * hA:D + 64 * hA + 64, :]).then_inc(kse.h, 16)
                        kse.n += 32
                    q_khA = ksp[pA_].n
                if hB is not None:
                    kbe = kbp[pB_]
                    loB = 64 * (hB % 2)
                    fcqB = hB // 2

                    @block.sync
                    def _(sy, hB=hB, loB=loB, fcqB=fcqB, kbe=kbe, pB_=pB_):
                        if hB >= 2:
                            sy.wait_ge(tsem.h, tBend[hB - 2])
                        sy.dma_start(out=khb[pB_][0:64, 0:SL],
                                     in_=kg[64 * hB:64 * hB + 64, :]).then_inc(kbe.h, 16)
                        sy.dma_start(out=khb[pB_][0:64, SL:2 * SL],
                                     in_=kg[D + 64 * hB:D + 64 * hB + 64, :]).then_inc(kbe.h, 16)
                        vv = vh[pB_].rearrange("p (k c) -> p k c", c=65)
                        sy.dma_start(out=vv[:, :, 0:64],
                                     in_=vg.rearrange("(k p) f -> p k f", p=128)
                                     [:, :, 64 * hB:64 * hB + 64]).then_inc(kbe.h, 16)
                        sy.dma_start(out=qtile[pB_][0:64, :],
                                     in_=qoz[loB:loB + 64, fcqB * SL:(fcqB + 1) * SL]
                                     ).then_inc(kbe.h, 16)
                        sy.wait_ge(qsem.h, q_mst[hB])
                        sy.dma_start(out=qtile[pB_][64:65, :],
                                     in_=mstore[hB:hB + 1, :]).then_inc(kbe.h, 16)
                        kbe.n += 80
                    q_kB = kbp[pB_].n

                vbase = vsem.n
                if hA is not None:
                    vbaseA[hA] = vbase

                def v_red(qc, f, vbase=vbase, hB=hB):
                    ex = 2 * (qc // 4) if hB is not None else 0
                    return vbase + 10 * qc + ex + f + 1

                def v_rfin(qc, vbase=vbase, hB=hB):
                    ex = 2 * (qc // 4) if hB is not None else 0
                    return vbase + 10 * qc + ex + 9

                def v_copy(qc, vbase=vbase, hB=hB):
                    ex = 2 * (qc // 4) if hB is not None else 0
                    return vbase + 10 * qc + ex + 10

                def v_recip(qb, vbase=vbase, hA=hA):
                    return vbase + (42 * qb + 41 if hA is not None else 2 * qb + 1)

                def v_osc(qb, vbase=vbase, hA=hA):
                    return vbase + (42 * qb + 42 if hA is not None else 2 * qb + 2)

                sbase = asem.n
                if hB is not None:
                    sbaseB[hB] = sbase

                def s_exp(qb, g, sbase=sbase):
                    return sbase + 9 * qb + g + 1

                def s_rsb(qb, sbase=sbase):
                    return sbase + 9 * qb + 9

                qpred = [None]

                @block.tensor
                def _(t, s=s, hA=hA, hB=hB, pA_=pA_, pB_=pB_,
                      v_red=v_red, v_rfin=v_rfin, v_copy=v_copy,
                      v_recip=v_recip, v_osc=v_osc, s_exp=s_exp, s_rsb=s_rsb,
                      vbase=vbase, sbase=sbase):
                    items = []
                    if hA is not None:
                        loA = 64 * (hA % 2)
                        fcqA = hA // 2

                        def mk_fill(qc, f, hA=hA, loA=loA, fcqA=fcqA):
                            def emit():
                                FF = 8 * qc + f
                                if FF == 0:
                                    t.wait_ge(ksp[hA % 2].h, q_khA)
                                    if hA == 0:
                                        pass
                                if FF < 2:
                                    if prev_vred_tail[f] is not None:
                                        t.wait_ge(vsem.h, prev_vred_tail[f])
                                elif f == 0 and FF % 8 == 0:
                                    t.wait_ge(vsem.h, v_copy(qc - 1))
                                else:
                                    t.wait_ge(vsem.h, v_red((FF - 2) // 8, (FF - 2) % 8))
                                q_sl = qoz[loA:loA + 64, fcqA * SL + qc * 128:][:, 0:128]
                                ins = t.matmul(pBa[:, (2 + f % 2) * 512:(3 + f % 2) * 512],
                                               q_sl, khA[loA:loA + 64, f * 512:(f + 1) * 512],
                                               start=True, stop=True)
                                tsem.inc(ins)
                                tfill[(hA, FF)] = tsem.n
                            return emit

                        def mk_tp(qc, hA=hA):
                            def emit():
                                t.wait_ge(vsem.h, v_rfin(qc))
                                tsem.inc(t.transpose(pBa[0:1, 1024:1152],
                                                     gsc[:, 0:1], idf[:, :]))
                                ttp[(hA, qc)] = tsem.n
                            return emit

                        for qc in range(NQC):
                            for f in range(8):
                                items.append(mk_fill(qc, f))
                            items.append(mk_tp(qc))
                    it = iter(items)

                    def inject(n):
                        for _i in range(n):
                            em = next(it, None)
                            if em is None:
                                return
                            em()

                    if hB is not None:
                        loB = 64 * (hB % 2)
                        kbe = kbp[pB_]
                        t.wait_ge(kbe.h, q_kB)

                        def sTq(qb, g, hB=hB, pB_=pB_):
                            ins = None
                            for j in range(4):
                                kc = 4 * g + j
                                ins = t.matmul(bank(pAa, j),
                                               khb[pB_][0:65, kc * 128:(kc + 1) * 128],
                                               qtile[pB_][0:65, qb * T:(qb + 1) * T],
                                               start=True, stop=True)
                            tsem.inc(ins)
                            tBsT[(hB, qb, g)] = tsem.n
                        for qb in range(NTT):
                            if qb == 0:
                                t.wait_ge(asem.h, sbase)
                            else:
                                t.wait_ge(asem.h, s_exp(qb - 1, 7))
                            sTq(qb, 0)
                            for g in range(8):
                                inject((5 if g < 4 else 4) if hA is not None else 0)
                                t.wait_ge(asem.h, s_exp(qb, g))
                                if g == 0:
                                    t.wait_ge(vsem.h,
                                              v_osc(qb - 1) if qb > 0 else prev_osc[0])
                                ins = None
                                for j in range(4):
                                    kc = 4 * g + j
                                    ins = t.matmul(pBa[0:65, 0:512],
                                                   vh[pB_][:, kc * 65:(kc + 1) * 65],
                                                   att[g % 2][:, j * 512:(j + 1) * 512],
                                                   start=(kc == 0), stop=(kc == NKC - 1))
                                tsem.inc(ins)
                                tBAV[(hB, qb, g)] = tsem.n
                                if g + 1 < 8:
                                    sTq(qb, g + 1)
                            t.wait_ge(vsem.h, v_recip(qb))
                            tsem.inc(t.matmul(pBa[:, 512:1024], onesf[64:65, 0:128],
                                              srow[64:65, :], start=True, stop=True))
                            tBrs[(hB, qb)] = tsem.n
                    inject(10 ** 9)
                    if hA is not None:
                        tAend[hA] = tsem.n
                    if hB is not None:
                        tBend[hB] = tsem.n

                if hB is not None:
                    @block.scalar
                    def _(s_, hB=hB, v_red=v_red):
                        for qb in range(NTT):
                            for g in range(8):
                                s_.wait_ge(tsem.h, tBsT[(hB, qb, g)])
                                if g >= 2:
                                    s_.wait_ge(tsem.h, tBAV[(hB, qb, g - 2)])
                                elif qb > 0:
                                    s_.wait_ge(tsem.h, tBAV[(hB, qb - 1, 6 + g)])
                                asem.inc(s_.activation(att[g % 2][:, :], pAa[:, 0:2048],
                                                       AF.Exp))
                            s_.wait_ge(tsem.h, tBrs[(hB, qb)])
                            asem.inc(s_.copy(rsb[0:64, :], pBa[0:64, 512:1024]))

                qpred[0] = qsem.n

                @block.vector
                def _(v, s=s, hA=hA, hB=hB, pB_=pB_, vbase=vbase, sbase=sbase,
                      v_red=v_red, v_rfin=v_rfin, v_copy=v_copy,
                      v_recip=v_recip, v_osc=v_osc, s_rsb=s_rsb, qpred=qpred):
                    def a_block(qc, hA=hA):
                        if qc == 0 and hA is not None and hA >= 1:
                            v.wait_ge(qsem.h, q_mst[hA - 1])
                        for f in range(8):
                            v.wait_ge(tsem.h, tfill[(hA, 8 * qc + f)])
                            vsem.inc(v.reduce_max(gcol[:, f:f + 1],
                                                  pBa[:, (2 + f % 2) * 512:(3 + f % 2) * 512],
                                                  axis=AX.X))
                        v.wait_ge(vsem.h, vsem.n)
                        vsem.inc(v.reduce_max(gsc[:, 0:1], gcol[:, 0:8], axis=AX.X,
                                              negate=True))
                        v.wait_ge(tsem.h, ttp[(hA, qc)])
                        vsem.inc(v.tensor_copy(mrow[0:1, qc * 128:(qc + 1) * 128],
                                               pBa[0:1, 1024:1152]))

                    def b_block(qb, hB=hB, pB_=pB_):
                        loB = 64 * (hB % 2)
                        fcqB = hB // 2
                        v.wait_ge(tsem.h, tBAV[(hB, qb, 7)])
                        vsem.inc(v.reciprocal(srow[64:65, :], pBa[64:65, 0:512]))
                        v.wait_ge(asem.h, s_rsb(qb))
                        if loB == 0:
                            vsem.inc(v.tensor_mul(
                                qoz[0:64, fcqB * SL + qb * T:][:, 0:T],
                                pBa[0:64, 0:512], rsb[0:64, :]))
                        else:
                            if qb >= 2:
                                v.wait_ge(qsem.h, qpred[0] + 16 * (qb - 1))
                            vsem.inc(v.tensor_mul(obuf2[qb % 2][:, :],
                                                  pBa[0:64, 0:512], rsb[0:64, :]))

                    if hA is not None and hB is not None:
                        for qb in range(NTT):
                            for qc in range(4 * qb, 4 * qb + 4):
                                a_block(qc)
                            b_block(qb)
                    elif hA is not None:
                        for qc in range(NQC):
                            a_block(qc)
                    else:
                        for qb in range(NTT):
                            b_block(qb)
                    if hB is not None:
                        assert vsem.n == v_osc(NTT - 1), (vsem.n, v_osc(NTT - 1))
                    else:
                        assert vsem.n == v_copy(NQC - 1), (vsem.n, v_copy(NQC - 1))

                if hA is not None:
                    prev_vred_tail[0] = v_copy(NQC - 1)
                    prev_vred_tail[1] = v_red(NQC - 1, 7)
                if hB is not None:
                    prev_osc[0] = v_osc(NTT - 1)

                @block.sync
                def _(sy, hA=hA, hB=hB, v_copy=v_copy, v_osc=v_osc):
                    if hB is not None and hB % 2 == 1:
                        fcqB = hB // 2
                        for qb in range(NTT):
                            sy.wait_ge(vsem.h, v_osc(qb))
                            sy.dma_start(out=qoz[64:128, fcqB * SL + qb * T:][:, 0:T],
                                         in_=obuf2[qb % 2][:, :]).then_inc(qsem.h, 16)
                            qsem.n += 16
                    if hA is not None:
                        sy.wait_ge(vsem.h, v_copy(NQC - 1))
                        sy.dma_start(out=mstore[hA:hA + 1, :],
                                     in_=mrow[0:1, :]).then_inc(qsem.h, 16)
                        qsem.n += 16
                if hA is not None:
                    q_mst[hA] = qsem.n

            t_heads = tsem.n

            @block.gpsimd
            def _(g, t_heads=t_heads):
                g.wait_ge(tsem.h, t_heads)
                g.dma_start(out=wta[:, :].rearrange("p (dc f) -> p dc f", f=D),
                            in_=woT.rearrange("(dc p) f -> p dc f", p=128)
                            ).then_inc(dsem.h, 16)
                dsem.n += 16
            d_w_wo = dsem.n
            v_wo0, q_wo0 = vsem.n, qsem.n

            @block.tensor
            def _(t, v_wo0=v_wo0, q_wo0=q_wo0, d_w_wo=d_w_wo):
                t.wait_ge(vsem.h, v_wo0)
                t.wait_ge(qsem.h, q_wo0)
                t.wait_ge(dsem.h, d_w_wo)
                for dc in range(NDC):
                    for tt in range(NTT):
                        k = dc * NTT + tt
                        if k >= 2:
                            t.wait_ge(vsem.h, v_wo0 + k - 1)
                        for fc in range(NDC):
                            ins = t.matmul(bank(pAa, k % 2),
                                           wta[:, fc * D + dc * 128:][:, 0:128],
                                           qoz[:, fc * SL + tt * T:][:, 0:T],
                                           start=(fc == 0), stop=(fc == NDC - 1))
                        tsem.inc(ins)
            t_wo0 = tsem.n - NDC * NTT

            @block.vector
            def _(v, t_wo0=t_wo0):
                for dc in range(NDC):
                    for tt in range(NTT):
                        k = dc * NTT + tt
                        v.wait_ge(tsem.h, t_wo0 + k + 1)
                        x_sl = xfm[:, dc * SL + tt * T:][:, 0:T]
                        vsem.inc(v.scalar_tensor_tensor(
                            x_sl, bank(pAa, k % 2), battn[:, 3 * NDC + dc:][:, 0:1],
                            x_sl, ALU.add, ALU.add))

        if stop_after != "p0":
            layernorm(fold_x=True, with_gate=True)
        if stop_after in ("u1", "c1"):
            t_d, v_d, a_d = tsem.n, vsem.n, asem.n

            @block.gpsimd
            def _(g):
                g.wait_ge(tsem.h, t_d)
                g.wait_ge(vsem.h, v_d)
                g.wait_ge(asem.h, a_d)
                if stop_after == "u1":
                    for dc in range(NDC):
                        g.dma_start(out=outT[dc * 128:(dc + 1) * 128, :],
                                    in_=ub[:, dc * SL:(dc + 1) * SL]).then_inc(dsem.h, 16)
                        dsem.n += 16
                else:
                    g.dma_start(out=outT[0:8, :], in_=cT[0:8, :]).then_inc(dsem.h, 16)
                    dsem.n += 16
                g.wait_ge(dsem.h, dsem.n)
        if stop_after not in ("p0", "ln1", "u1", "c1"):
            ffn_experts(0)
        if stop_after not in ("p0", "ln1", "u1", "c1", "ffn1"):
            layernorm(fold_x=False, with_gate=False)
            attention("qkv" if stop_after == "qkv" else "all")
        if stop_after not in ("p0", "ln1", "u1", "c1", "ffn1", "qkv", "attn"):
            layernorm(fold_x=True, with_gate=True)
            ffn_experts(1)

        t_fin, v_fin, a_fin = tsem.n, vsem.n, asem.n
        skip_out = stop_after in ("u1", "c1")

        @block.sync
        def _(sy):
            if skip_out:
                sy.wait_ge(dsem.h, dsem.n)
                return
            sy.wait_ge(vsem.h, v_fin)
            sy.wait_ge(tsem.h, t_fin)
            sy.wait_ge(asem.h, a_fin)
            for dc in range(NDC):
                sy.dma_start(out=outT[dc * 128:(dc + 1) * 128, :],
                             in_=xfm[:, dc * SL:(dc + 1) * SL]).then_inc(qsem.h, 16)
            qsem.n += NDC * 16
            sy.wait_ge(qsem.h, qsem.n)

    return nc


def kernel(L0_attn_x, L0_attn_Wq, L0_attn_bq, L0_attn_Wk, L0_attn_bk,
           L0_attn_Wv, L0_attn_bv, L0_attn_Wo, L0_attn_bo,
           L0_ffn_Wg, L0_ffn_bg, L0_ffn_W1, L0_ffn_W2, L0_ffn_W3):
    x = np.ascontiguousarray(np.asarray(L0_attn_x, dtype=np.float32)).reshape(B * S, D)
    com = dict(
        wqT=np.ascontiguousarray(np.asarray(L0_attn_Wq, np.float32).T),
        wkT=np.ascontiguousarray(np.asarray(L0_attn_Wk, np.float32).T),
        wvT=np.ascontiguousarray(np.asarray(L0_attn_Wv, np.float32).T),
        woT=np.ascontiguousarray(np.asarray(L0_attn_Wo, np.float32).T),
        bq=np.asarray(L0_attn_bq, np.float32), bk=np.asarray(L0_attn_bk, np.float32),
        bv=np.asarray(L0_attn_bv, np.float32), bo=np.asarray(L0_attn_bo, np.float32),
        wg=np.asarray(L0_ffn_Wg, np.float32), bg=np.asarray(L0_ffn_bg, np.float32),
        w1=np.ascontiguousarray(np.asarray(L0_ffn_W1, np.float32)),
        w2=np.ascontiguousarray(np.asarray(L0_ffn_W2, np.float32)),
        w3=np.ascontiguousarray(np.asarray(L0_ffn_W3, np.float32)),
    )
    in_maps = []
    for c in range(NCORE):
        xs = np.ascontiguousarray(x[c * SL:(c + 1) * SL, :].T)
        in_maps.append(dict(xT=xs, **com))
    nc = build_program()
    res = run_bass_kernel_spmd(nc, in_maps, core_ids=list(range(NCORE)))
    outs = [np.asarray(r["outT"]).T for r in res.results]
    return np.concatenate(outs, axis=0).reshape(B, S, D)


# version 05
# speedup vs baseline: 1.1637x; 1.0196x over previous
import numpy as np
import concourse.bass as bass
import concourse.mybir as mybir
from concourse.bass_utils import run_bass_kernel_spmd

F32 = mybir.dt.float32
BF16 = mybir.dt.bfloat16
AF = mybir.ActivationFunctionType
ALU = mybir.AluOpType
AX = mybir.AxisListType

NCORE = 8
B, S, D = 4, 4096, 1024
SL = 2048
NH, HD = 16, 64
E, H = 8, 4096
NDC = D // 128
T = 512
NTT = SL // T
HC = 512
NHC = H // HC
SF = 4096
NKC = SF // 128
NQC = SL // 128
SCALE = 0.125
EPS = 1e-5


class Sem:
    def __init__(self, nc, ctx, name):
        self.h = ctx.enter_context(nc.semaphore(name))
        self.n = 0

    def inc(self, inst, k=1):
        inst.then_inc(self.h, k)
        self.n += k
        return self.n


def build_program(stop_after="all"):
    nc = bass.Bass("TRN2", target_bir_lowering=False, detect_race_conditions=False)

    xT = nc.dram_tensor("xT", [D, SL], F32, kind="ExternalInput").ap()
    wqT = nc.dram_tensor("wqT", [D, D], F32, kind="ExternalInput").ap()
    wkT = nc.dram_tensor("wkT", [D, D], F32, kind="ExternalInput").ap()
    wvT = nc.dram_tensor("wvT", [D, D], F32, kind="ExternalInput").ap()
    woT = nc.dram_tensor("woT", [D, D], F32, kind="ExternalInput").ap()
    bqd = nc.dram_tensor("bq", [D], F32, kind="ExternalInput").ap()
    bkd = nc.dram_tensor("bk", [D], F32, kind="ExternalInput").ap()
    bvd = nc.dram_tensor("bv", [D], F32, kind="ExternalInput").ap()
    bod = nc.dram_tensor("bo", [D], F32, kind="ExternalInput").ap()
    wgd = nc.dram_tensor("wg", [D, E], F32, kind="ExternalInput").ap()
    bgd = nc.dram_tensor("bg", [E], F32, kind="ExternalInput").ap()
    w1d = nc.dram_tensor("w1", [E, D, H], F32, kind="ExternalInput").ap()
    w2d = nc.dram_tensor("w2", [E, H, D], F32, kind="ExternalInput").ap()
    w3d = nc.dram_tensor("w3", [E, D, H], F32, kind="ExternalInput").ap()
    outT = nc.dram_tensor("outT", [D, SL], F32, kind="ExternalOutput").ap()

    kexp = nc.dram_tensor("kexp", [D, SL], BF16).ap()
    vexp = nc.dram_tensor("vexp", [SL, D], BF16).ap()
    kg = nc.dram_tensor("kg", [2 * D, SL], BF16).ap()
    vg = nc.dram_tensor("vg", [SF, D], BF16).ap()
    cdram = nc.dram_tensor("cdram", [E, SL], BF16).ap()
    mstore = nc.dram_tensor("mstore", [NH, SL], BF16).ap()

    from contextlib import ExitStack
    with ExitStack() as ctx:
        BASE = 16512
        off = [BASE]

        def At(name, shape, dt_, offset=None):
            nbytes = int(np.prod(shape[1:])) * mybir.dt.size(dt_)
            if offset is None:
                offset = off[0]
                off[0] += (nbytes + 31) // 32 * 32
            return nc.alloc_sbuf_tensor_at(name, list(shape), dt_, offset=offset).ap()

        xfm = At("xfm", [128, NDC * SL], F32)
        ub = At("ub", [128, NDC * SL], BF16)
        qoz = At("qoz", [128, NDC * SL], BF16)
        wreg = off[0]
        w1t = [At(f"w1t{p}", [128, NDC * HC], BF16, wreg + p * 24576) for p in range(2)]
        w3t = [At(f"w3t{p}", [128, NDC * HC], BF16, wreg + p * 24576 + 8192) for p in range(2)]
        w2t = [At(f"w2t{p}", [128, 4 * D], BF16, wreg + p * 24576 + 16384) for p in range(2)]
        wta = At("wta", [128, NDC * D], BF16, wreg)
        qtile = [At(f"qtile{p}", [65, SL], BF16, wreg + p * 4096) for p in range(2)]
        khb = [At(f"khb{p}", [65, SF], BF16, wreg + 8192 + p * 8192) for p in range(2)]
        khA = At("khA", [128, SF], BF16, wreg + 24576)
        vh = [At(f"vh{p}", [128, NKC * 65], BF16, wreg + 32768 + p * 4160) for p in range(2)]
        kvx = At("kvx", [128, SL], BF16, wreg + 16384)
        bvrowb = At("bvrowb", [1, D], BF16, wreg + 20480)
        off[0] = wreg + 49152
        screg = off[0]
        sq = [At(f"sq{p}", [128, T], F32, screg + p * 2048) for p in range(2)]
        uf32 = [At(f"uf32{p}", [128, T], F32, screg + 4096 + p * 2048) for p in range(2)]
        hbuf = At("hbuf", [128, 4 * T], BF16, screg + 8192)
        s1 = [At(f"s1{p}", [128, T], F32, screg + 12288 + p * 2048) for p in range(2)]
        mrow = At("mrow", [1, SL], BF16, screg)
        rsb = At("rsb", [128, 512], F32, screg + 5120)
        srow = At("srow", [128, 512], F32, screg + 7168)
        obuf2 = [At(f"obuf{p}", [64, 512], BF16, screg + 9216 + p * 1024) for p in range(2)]
        att = [At("att0", [128, 4 * 512], BF16, wreg + 41088),
               At("att1", [128, 4 * 512], BF16, screg + 10240)]
        off[0] = screg + 16384
        rowbuf = At("rowbuf", [1, 4 * T], F32)
        ctoff = off[0]
        cT = At("cT", [8, SL], BF16)
        crow = At("crow", [1, SL], BF16, offset=ctoff)
        battn = At("battn", [128, 4 * NDC], F32)
        onesf = At("onesf", [128, 128], F32)
        onesb = At("onesb", [1, 128], BF16)
        idf = At("idf", [128, 128], F32)
        idb = At("idb", [128, 128], BF16)
        wg_sb = At("wg_sb", [128, NDC * E], F32)
        g1b = At("g1b", [128, E], F32)
        bgb = At("bgb", [128, E], F32)
        grow = At("grow", [1, 2 * E], F32)
        gsc = At("gsc", [128, 4 * E], F32)
        gcol = At("gcol", [128, 8], F32)
        assert off[0] <= 229344, off[0]

        ROW_TMP, ROW_R, ROW_MR, ROW_NM = (rowbuf[0:1, k * T:(k + 1) * T] for k in range(4))

        pA = ctx.enter_context(nc.psum_tensor("pA", [128, 2048], F32))
        pB = ctx.enter_context(nc.psum_tensor("pB", [128, 2048], F32))
        pAa, pBa = pA[:, :], pB[:, :]
        pAb = pA[:, :].bitcast(BF16)

        def bank(ap_, i):
            return ap_[:, i * 512:(i + 1) * 512]

        names = ["qsem", "dsem", "tsem", "vsem", "asem", "ck", "cv", "esem",
                 "ws0", "ws1", "ks0", "ks1", "kb0", "kb1"]
        SM = {nm: Sem(nc, ctx, nm) for nm in names}
        qsem, dsem, tsem, vsem, asem, ck, cv, esem, ws0, ws1, ks0, ks1, kb0, kb1 = \
            (SM[n] for n in names)
        wsp = [ws0, ws1]
        ksp = [ks0, ks1]
        kbp = [kb0, kb1]
        block = ctx.enter_context(nc.Block())

        @block.gpsimd
        def _(g):
            g.memset(onesf[:, :], 1.0)
            g.memset(onesb[:, :], 1.0)
            g.memset(idf[:, :], 1.0)
            esem.inc(g.memset(idb[:, :], 1.0))
            g.wait_ge(esem.h, 1)
            g.affine_select(idf[:, :], idf[:, :], [[1, 128]], ALU.is_equal, 0.0,
                            base=0, channel_multiplier=-1)
            esem.inc(g.affine_select(idb[:, :], idb[:, :], [[1, 128]], ALU.is_equal, 0.0,
                                     base=0, channel_multiplier=-1))
            with nc.allow_non_contiguous_dma("tiny one-time bias loads"):
                for j, bb in enumerate([bqd, bkd, bvd, bod]):
                    g.dma_start(out=battn[:, j * NDC:(j + 1) * NDC],
                                in_=bb.rearrange("(dc p) -> p dc", p=128),
                                allow_slow_non_contiguous=True).then_inc(dsem.h, 16)
            g.dma_start(out=wg_sb[:, :].rearrange("p (dc e) -> p dc e", e=E),
                        in_=wgd.rearrange("(dc p) e -> p dc e", p=128)).then_inc(dsem.h, 16)
            g.dma_start(out=grow[0:1, E:2 * E], in_=bgd[None, :]).then_inc(dsem.h, 16)
            dsem.n += 6 * 16

        @block.sync
        def _(sy):
            for dc in range(NDC):
                sy.dma_start(out=xfm[:, dc * SL:(dc + 1) * SL],
                             in_=xT[dc * 128:(dc + 1) * 128, :]).then_inc(qsem.h, 16)
            qsem.n += NDC * 16
        q_x = qsem.n

        @block.tensor
        def _(t):
            t.wait_ge(dsem.h, dsem.n)
            t.wait_ge(esem.h, 2)
            t.wait_ge(qsem.h, q_x)
            for dc in range(NDC):
                ins = t.matmul(pBa[0:1, 0:E], onesf[:, 0:1], wg_sb[:, dc * E:(dc + 1) * E],
                               start=(dc == 0), stop=(dc == NDC - 1))
            tsem.inc(ins)
        ev0 = tsem.n

        @block.vector
        def _(v):
            v.wait_ge(tsem.h, ev0)
            vsem.inc(v.tensor_copy(grow[0:1, 0:E], pBa[0:1, 0:E]))
        ev1 = vsem.n

        @block.tensor
        def _(t):
            t.wait_ge(vsem.h, ev1)
            tsem.inc(t.matmul(pAa[:, 0:2 * E], onesf[0:1, 0:128], grow[0:1, :],
                              start=True, stop=True))
        ev2 = tsem.n

        @block.vector
        def _(v):
            v.wait_ge(tsem.h, ev2)
            v.tensor_copy(g1b[:, :], pAa[:, 0:E])
            v.tensor_scalar_mul(battn[:, 0:NDC], battn[:, 0:NDC], SCALE)
            vsem.inc(v.tensor_copy(bgb[:, :], pAa[:, E:2 * E]))

        def gate(tt, v_rows):
            for tc in range(4):
                vg0 = vsem.n
                tok0 = tt * T + tc * 128

                @block.tensor
                def _(t, tt=tt, tc=tc, tok0=tok0, vg0=vg0, v_rows=v_rows):
                    if tc == 0:
                        t.wait_ge(vsem.h, v_rows)
                    else:
                        t.wait_ge(vsem.h, vg0)
                    for dc in range(NDC):
                        ins = t.matmul(pBa[0:128, 0:E], xfm[:, dc * SL + tok0:][:, 0:128],
                                       wg_sb[:, dc * E:(dc + 1) * E],
                                       start=(dc == 0), stop=(dc == NDC - 1))
                    tsem.inc(ins)
                    tsem.inc(t.transpose(pAa[0:128, 1536:1537],
                                         ROW_NM[0:1, tc * 128:(tc + 1) * 128], onesf[0:1, 0:1]))
                    tsem.inc(t.transpose(pAa[0:128, 1537:1538],
                                         ROW_R[0:1, tc * 128:(tc + 1) * 128], onesf[0:1, 0:1]))
                t_g = tsem.n

                @block.vector
                def _(v, t_g=t_g):
                    def vf(ins):
                        vsem.inc(ins)
                        v.wait_ge(vsem.h, vsem.n)
                    v.wait_ge(tsem.h, t_g)
                    vf(v.tensor_copy(gcol[:, 6:8], pAa[:, 1536:1538]))
                    vf(v.scalar_tensor_tensor(gsc[:, 0:E], g1b[:, :], gcol[:, 6:7],
                                              pBa[0:128, 0:E], ALU.mult, ALU.add))
                    vf(v.tensor_scalar(gsc[:, 0:E], gsc[:, 0:E], gcol[:, 7:8], None, ALU.mult))
                    vf(v.tensor_add(gsc[:, 0:E], gsc[:, 0:E], bgb[:, :]))
                    vf(v.reduce_max(gcol[:, 0:1], gsc[:, 0:E], axis=AX.X))
                    vf(v.tensor_scalar_mul(gcol[:, 1:2], gcol[:, 0:1], -1.0))
                    vf(v.tensor_scalar(gsc[:, E:2 * E], gsc[:, 0:E], gcol[:, 0:1], None, ALU.is_ge))
                    vf(v.scalar_tensor_tensor(gsc[:, 2 * E:3 * E], gsc[:, E:2 * E], -1e30,
                                              gsc[:, 0:E], ALU.mult, ALU.add))
                    vsem.inc(v.reduce_max(gcol[:, 2:3], gsc[:, 2 * E:3 * E], axis=AX.X))
                v_g1 = vsem.n

                @block.scalar
                def _(s, v_g1=v_g1):
                    s.wait_ge(vsem.h, v_g1)
                    s.activation(gsc[:, 2 * E:3 * E], gsc[:, 0:E], AF.Exp, bias=gcol[:, 1:2])
                    asem.inc(s.activation(gcol[:, 3:4], gcol[:, 2:3], AF.Exp, bias=gcol[:, 1:2]))
                a_g = asem.n

                @block.vector
                def _(v, a_g=a_g):
                    def vf(ins):
                        vsem.inc(ins)
                        v.wait_ge(vsem.h, vsem.n)
                    v.wait_ge(asem.h, a_g)
                    vf(v.tensor_scalar_add(gcol[:, 4:5], gcol[:, 3:4], 1.0))
                    vf(v.reciprocal(gcol[:, 5:6], gcol[:, 4:5]))
                    vf(v.tensor_scalar(gsc[:, E:2 * E], gsc[:, 0:E], gcol[:, 2:3], None, ALU.is_ge))
                    vf(v.tensor_mul(gsc[:, 2 * E:3 * E], gsc[:, 2 * E:3 * E],
                                    gsc[:, E:2 * E]))
                    vsem.inc(v.tensor_scalar(gsc[:, 3 * E:4 * E].bitcast(BF16)[:, 0:E],
                                             gsc[:, 2 * E:3 * E], gcol[:, 5:6], None, ALU.mult))
                v_g2 = vsem.n

                @block.tensor
                def _(t, v_g2=v_g2):
                    t.wait_ge(vsem.h, v_g2)
                    tsem.inc(t.transpose(pAb[0:8, 3072:3200],
                                         gsc[:, 3 * E:4 * E].bitcast(BF16)[:, 0:E], idb[:, :]))
                t_c = tsem.n

                @block.vector
                def _(v, t_c=t_c, tok0=tok0):
                    v.wait_ge(tsem.h, t_c)
                    vsem.inc(v.tensor_copy(cT[0:8, tok0:tok0 + 128], pAb[0:8, 3072:3200]))

        def layernorm(fold_x, with_gate):
            v_enter = vsem.n
            prev_vu = [v_enter]
            for tt in range(NTT):
                av0 = asem.n
                tv0 = tsem.n

                @block.scalar
                def _(s, tt=tt, tv0=tv0, v_enter=v_enter):
                    if tt == 0:
                        s.wait_ge(vsem.h, v_enter)
                        s.wait_ge(qsem.h, q_x)
                    for dc in range(NDC):
                        if dc >= 2:
                            s.wait_ge(tsem.h, tv0 + dc - 1)
                        asem.inc(s.activation(sq[dc % 2][:, :],
                                              xfm[:, dc * SL + tt * T:][:, 0:T], AF.Square))

                @block.tensor
                def _(t, tt=tt, av0=av0, wait_v=prev_vu[0]):
                    t.wait_ge(vsem.h, wait_v)
                    for dc in range(NDC):
                        x_sl = xfm[:, dc * SL + tt * T:][:, 0:T]
                        t.matmul(pBa[0:1, 1024:1536], onesf[:, 0:1], x_sl,
                                 start=(dc == 0), stop=(dc == NDC - 1))
                        t.wait_ge(asem.h, av0 + dc + 1)
                        tsem.inc(t.matmul(pBa[0:1, 1536:2048], onesf[:, 0:1], sq[dc % 2][:, :],
                                          start=(dc == 0), stop=(dc == NDC - 1)))
                t_sums = tsem.n

                @block.vector
                def _(v, t_sums=t_sums):
                    v.wait_ge(tsem.h, t_sums)
                    v.tensor_scalar_mul(pBa[0:1, 1024:1536], pBa[0:1, 1024:1536], 1.0 / D)
                    vsem.inc(v.tensor_scalar_mul(pBa[0:1, 1536:2048], pBa[0:1, 1536:2048], 1.0 / D))
                v_mean = vsem.n

                @block.scalar
                def _(s, v_mean=v_mean):
                    s.wait_ge(vsem.h, v_mean)
                    asem.inc(s.activation(ROW_TMP, pBa[0:1, 1024:1536], AF.Square))
                a_msq = asem.n

                @block.vector
                def _(v, a_msq=a_msq):
                    v.wait_ge(asem.h, a_msq)
                    v.tensor_sub(pBa[0:1, 1536:2048], pBa[0:1, 1536:2048], ROW_TMP)
                    vsem.inc(v.tensor_scalar_add(pBa[0:1, 1536:2048], pBa[0:1, 1536:2048], EPS))
                v_var = vsem.n

                @block.scalar
                def _(s, v_var=v_var):
                    s.wait_ge(vsem.h, v_var)
                    asem.inc(s.activation(ROW_TMP, pBa[0:1, 1536:2048], AF.Ln))
                a_ln = asem.n

                @block.vector
                def _(v, a_ln=a_ln):
                    v.wait_ge(asem.h, a_ln)
                    vsem.inc(v.tensor_scalar_mul(ROW_TMP, ROW_TMP, -0.5))
                v_ln = vsem.n

                @block.scalar
                def _(s, v_ln=v_ln):
                    s.wait_ge(vsem.h, v_ln)
                    asem.inc(s.activation(ROW_R, ROW_TMP, AF.Exp))
                a_r = asem.n

                @block.vector
                def _(v, a_r=a_r, with_gate=with_gate):
                    v.wait_ge(asem.h, a_r)
                    v.tensor_mul(ROW_MR, pBa[0:1, 1024:1536], ROW_R)
                    if with_gate:
                        vsem.inc(v.tensor_scalar_mul(ROW_NM, pBa[0:1, 1024:1536], -1.0))
                    else:
                        vsem.inc(v.engine_nop())
                v_rows = vsem.n

                if with_gate:
                    gate(tt, v_rows)

                @block.tensor
                def _(t, v_rows=v_rows):
                    t.wait_ge(vsem.h, v_rows)
                    tsem.inc(t.matmul(pBa[:, 1024:1536], onesf[0:1, 0:128], ROW_R,
                                      start=True, stop=True))
                    tsem.inc(t.matmul(pBa[:, 1536:2048], onesf[0:1, 0:128], ROW_MR,
                                      start=True, stop=True))
                t_bc = tsem.n
                vu0 = vsem.n
                au0 = asem.n

                @block.vector
                def _(v, tt=tt, t_bc=t_bc, fold_x=fold_x, au0=au0):
                    v.wait_ge(tsem.h, t_bc)
                    for dc in range(NDC):
                        x_sl = xfm[:, dc * SL + tt * T:][:, 0:T]
                        p = dc % 2
                        if dc >= 2:
                            v.wait_ge(asem.h, au0 + dc - 1)
                        v.tensor_mul(uf32[p][:, :], x_sl, pBa[:, 1024:1536])
                        vsem.inc(v.tensor_sub(uf32[p][:, :], uf32[p][:, :], pBa[:, 1536:2048]))
                        if fold_x:
                            vsem.inc(v.scalar_tensor_tensor(x_sl, uf32[p][:, :], 0.5, x_sl,
                                                            ALU.mult, ALU.add))
                        else:
                            vsem.inc(v.engine_nop())

                @block.scalar
                def _(s, tt=tt, vu0=vu0):
                    for dc in range(NDC):
                        u_sl = ub[:, dc * SL + tt * T:][:, 0:T]
                        s.wait_ge(vsem.h, vu0 + 2 * dc + 1)
                        asem.inc(s.copy(u_sl, uf32[dc % 2][:, :]))
                prev_vu[0] = vsem.n

        def ffn_experts(call):
            t0 = tsem.n
            v0 = vsem.n
            a0 = asem.n
            a_enter = asem.n
            v_pre = vsem.n
            d0w = [wsp[0].n, wsp[1].n]

            @block.sync
            def _(sy, v_pre=v_pre):
                sy.wait_ge(vsem.h, v_pre)
                sy.dma_start(out=cdram[:, :], in_=cT[0:8, :]).then_inc(qsem.h, 16)
                qsem.n += 16
            q_cflat = qsem.n

            def t_base(e):
                return t0 + e * (4 + 32 * 16)

            def t_cb(e, tt):
                return t_base(e) + tt + 1

            def t_blk(e, hc, tt):
                return t_base(e) + 4 + (hc * NTT + tt) * 16

            def t_h1(e, hc, tt, i):
                return t_blk(e, hc, tt) + 2 * i + 1

            def t_h3(e, hc, tt, i):
                return t_blk(e, hc, tt) + 2 * i + 2

            def t_y(e, hc, tt, dc):
                return t_blk(e, hc, tt) + 8 + dc + 1

            def v_base(e):
                return v0 + e * (4 + 32 * 12)

            def v_z(e, tt):
                return v_base(e) + tt + 1

            def v_blkbase(e, hc, tt):
                return v_base(e) + 4 + (hc * NTT + tt) * 12

            def v_hmul(e, hc, tt, i):
                return v_blkbase(e, hc, tt) + i + 1

            def v_yadd(e, hc, tt, dc):
                return v_blkbase(e, hc, tt) + 4 + dc + 1

            def lin_h(e, hc, tt, i):
                return ((e * NHC + hc) * NTT + tt) * 4 + i

            def lin_y(e, hc, tt, dc):
                return ((e * NHC + hc) * NTT + tt) * 8 + dc

            def unlin_h(k):
                return (k // (4 * NTT * NHC), (k // (4 * NTT)) % NHC,
                        (k // 4) % NTT, k % 4)

            def unlin_y(k):
                return (k // (8 * NTT * NHC), (k // (8 * NTT)) % NHC,
                        (k // 8) % NTT, k % 8)

            def a_silu(e, hc, tt, i):
                return a0 + ((e * NHC + hc) * NTT + tt) * 4 + i + 1

            @block.gpsimd
            def _(g, call=call, t0=t0):
                for c in range(E * NHC):
                    e, hc = divmod(c, NHC)
                    p = c % 2
                    if c < 2:
                        g.wait_ge(tsem.h, t0)
                    else:
                        ee, hh = divmod(c - 2, NHC)
                        g.wait_ge(tsem.h, t_y(ee, hh, NTT - 1, 7))
                    g.dma_start(out=w1t[p].rearrange("p (dc h) -> p dc h", h=HC),
                                in_=w1d[e][:, hc * HC:(hc + 1) * HC]
                                .rearrange("(dc p) h -> p dc h", p=128)).then_inc(wsp[p].h, 16)
                    g.dma_start(out=w3t[p].rearrange("p (dc h) -> p dc h", h=HC),
                                in_=w3d[e][:, hc * HC:(hc + 1) * HC]
                                .rearrange("(dc p) h -> p dc h", p=128)).then_inc(wsp[p].h, 16)
                    g.dma_start(out=w2t[p].rearrange("p (hh d) -> p hh d", d=D),
                                in_=w2d[e][hc * HC:(hc + 1) * HC, :]
                                .rearrange("(hh p) d -> p hh d", p=128)).then_inc(wsp[p].h, 16)
                    wsp[p].n += 48

            @block.sync
            def _(sy, q_cflat=q_cflat):
                for e in range(E):
                    if e == 0:
                        sy.wait_ge(qsem.h, q_cflat)
                    else:
                        sy.wait_ge(tsem.h, t_cb(e - 1, NTT - 1))
                    sy.dma_start(out=crow[0:1, :], in_=cdram[e:e + 1, :]).then_inc(qsem.h, 16)
                    qsem.n += 16
            q_crow0 = q_cflat

            @block.tensor
            def _(t, call=call, q_crow0=q_crow0, d0w=d0w):
                for e in range(E):
                    for tt in range(NTT):
                        t.wait_ge(qsem.h, q_crow0 + 16 * (e + 1))
                        if tt > 0:
                            t.wait_ge(vsem.h, v_z(e, tt - 1))
                        if e > 0 and tt == 0:
                            t.wait_ge(vsem.h, v_z(e - 1, NTT - 1))
                        tsem.inc(t.matmul(pBa[:, 1024:1536], onesb[:, 0:128],
                                          crow[0:1, tt * T:(tt + 1) * T], start=True, stop=True))
                        assert tsem.n == t_cb(e, tt)
                    for hc in range(NHC):
                        c = e * NHC + hc
                        p = c % 2
                        t.wait_ge(wsp[p].h, d0w[p] + 48 * (c // 2 + 1))
                        for tt in range(NTT):
                            t.wait_ge(vsem.h, v_z(e, tt))
                            for i in range(4):
                                kk = lin_h(e, hc, tt, i)
                                if kk >= 2:
                                    t.wait_ge(vsem.h, v_hmul(*unlin_h(kk - 2)))
                                pp = kk % 2
                                for dc in range(NDC):
                                    ins = t.matmul(bank(pAa, pp),
                                                   w1t[p][:, dc * HC + i * 128:][:, 0:128],
                                                   qoz[:, dc * SL + tt * T:][:, 0:T],
                                                   start=(dc == 0), stop=(dc == NDC - 1))
                                tsem.inc(ins)
                                for dc in range(NDC):
                                    ins = t.matmul(bank(pAa, 2 + pp),
                                                   w3t[p][:, dc * HC + i * 128:][:, 0:128],
                                                   qoz[:, dc * SL + tt * T:][:, 0:T],
                                                   start=(dc == 0), stop=(dc == NDC - 1))
                                tsem.inc(ins)
                                assert tsem.n == t_h3(e, hc, tt, i)
                            t.wait_ge(vsem.h, v_hmul(e, hc, tt, 3))
                            for dc in range(NDC):
                                kk = lin_y(e, hc, tt, dc)
                                if kk >= 2:
                                    t.wait_ge(vsem.h, v_yadd(*unlin_y(kk - 2)))
                                pp = kk % 2
                                for i in range(4):
                                    ins = t.matmul(bank(pBa, pp),
                                                   w2t[p][:, i * D + dc * 128:][:, 0:128],
                                                   hbuf[:, i * T:(i + 1) * T],
                                                   start=(i == 0), stop=(i == 3))
                                tsem.inc(ins)
                                assert tsem.n == t_y(e, hc, tt, dc)

            @block.vector
            def _(v, call=call, a_enter=a_enter):
                for e in range(E):
                    for tt in range(NTT):
                        if e == 0 and tt == 0:
                            v.wait_ge(asem.h, a_enter)
                        v.wait_ge(tsem.h, t_cb(e, tt))
                        if e > 0:
                            v.wait_ge(tsem.h, t_h3(e - 1, NHC - 1, tt, 3))
                        ins = None
                        for dc in range(NDC):
                            ins = v.tensor_mul(qoz[:, dc * SL + tt * T:][:, 0:T],
                                               ub[:, dc * SL + tt * T:][:, 0:T],
                                               pBa[:, 1024:1536])
                        vsem.inc(ins)
                        assert vsem.n == v_z(e, tt)
                    for hc in range(NHC):
                        for tt in range(NTT):
                            for i in range(4):
                                kk = lin_h(e, hc, tt, i)
                                pp = kk % 2
                                v.wait_ge(asem.h, a_silu(e, hc, tt, i))
                                v.wait_ge(tsem.h, t_h3(e, hc, tt, i))
                                vsem.inc(v.tensor_mul(hbuf[:, i * T:(i + 1) * T],
                                                      s1[pp][:, :], bank(pAa, 2 + pp)))
                                assert vsem.n == v_hmul(e, hc, tt, i)
                            for dc in range(NDC):
                                kk = lin_y(e, hc, tt, dc)
                                pp = kk % 2
                                v.wait_ge(tsem.h, t_y(e, hc, tt, dc))
                                x_sl = xfm[:, dc * SL + tt * T:][:, 0:T]
                                vsem.inc(v.scalar_tensor_tensor(x_sl, bank(pBa, pp), 0.5,
                                                                x_sl, ALU.mult, ALU.add))
                                assert vsem.n == v_yadd(e, hc, tt, dc)

            @block.scalar
            def _(s, call=call):
                for e in range(E):
                    for hc in range(NHC):
                        for tt in range(NTT):
                            for i in range(4):
                                kk = lin_h(e, hc, tt, i)
                                pp = kk % 2
                                s.wait_ge(tsem.h, t_h1(e, hc, tt, i))
                                if kk >= 2:
                                    s.wait_ge(vsem.h, v_hmul(*unlin_h(kk - 2)))
                                asem.inc(s.activation(s1[pp][:, :], bank(pAa, pp), AF.Silu))
                                assert asem.n == a_silu(e, hc, tt, i)

        def attention(stop_after="all"):
            a_enter = asem.n
            for wi, wsrc_ in enumerate([wqT, wkT, wvT]):
                t_pre = tsem.n

                @block.gpsimd
                def _(g, wsrc_=wsrc_, t_pre=t_pre, wi=wi):
                    g.wait_ge(tsem.h, t_pre)
                    if wi == 2:
                        g.dma_start(out=bvrowb[:, :], in_=bvd[None, :]).then_inc(dsem.h, 16)
                        dsem.n += 16
                    g.dma_start(out=wta[:, :].rearrange("p (dc f) -> p dc f", f=D),
                                in_=wsrc_.rearrange("(dc p) f -> p dc f", p=128)
                                ).then_inc(dsem.h, 16)
                    dsem.n += 16
                d_w = dsem.n

                if wi == 0:
                    a_pre = asem.n

                    @block.tensor
                    def _(t, d_w=d_w, a_pre=a_pre, a_enter=a_enter):
                        t.wait_ge(dsem.h, d_w)
                        t.wait_ge(asem.h, a_enter)
                        for fc in range(NDC):
                            for tt in range(NTT):
                                k = fc * NTT + tt
                                if k >= 2:
                                    t.wait_ge(asem.h, a_pre + k - 1)
                                for dc in range(NDC):
                                    ins = t.matmul(bank(pAa, k % 2),
                                                   wta[:, dc * D + fc * 128:][:, 0:128],
                                                   ub[:, dc * SL + tt * T:][:, 0:T],
                                                   start=(dc == 0), stop=(dc == NDC - 1))
                                tsem.inc(ins)

                    @block.scalar
                    def _(s, t_pre=t_pre):
                        for fc in range(NDC):
                            for tt in range(NTT):
                                k = fc * NTT + tt
                                s.wait_ge(tsem.h, t_pre + k + 1)
                                asem.inc(s.activation(qoz[:, fc * SL + tt * T:][:, 0:T],
                                                      bank(pAa, k % 2), AF.Identity,
                                                      bias=battn[:, 0 * NDC + fc:][:, 0:1],
                                                      scale=SCALE))
                elif wi == 1:
                    a_pre = asem.n
                    q_pre = qsem.n

                    @block.tensor
                    def _(t, d_w=d_w, a_pre=a_pre):
                        t.wait_ge(dsem.h, d_w)
                        t.wait_ge(asem.h, a_pre)
                        for fc in range(NDC):
                            for tt in range(NTT):
                                k = fc * NTT + tt
                                if k >= 2:
                                    t.wait_ge(asem.h, a_pre + k - 1)
                                for dc in range(NDC):
                                    ins = t.matmul(bank(pAa, k % 2),
                                                   wta[:, dc * D + fc * 128:][:, 0:128],
                                                   ub[:, dc * SL + tt * T:][:, 0:T],
                                                   start=(dc == 0), stop=(dc == NDC - 1))
                                tsem.inc(ins)

                    @block.scalar
                    def _(s, t_pre=t_pre, q_pre=q_pre, a_pre=a_pre):
                        for fc in range(NDC):
                            for tt in range(NTT):
                                k = fc * NTT + tt
                                s.wait_ge(tsem.h, t_pre + k + 1)
                                if fc >= 1 and tt == 0:
                                    s.wait_ge(qsem.h, q_pre + 16 * fc)
                                asem.inc(s.activation(kvx[:, tt * T:(tt + 1) * T],
                                                      bank(pAa, k % 2), AF.Identity,
                                                      bias=battn[:, 1 * NDC + fc:][:, 0:1]))

                    @block.sync
                    def _(sy, a_pre=a_pre):
                        for fc in range(NDC):
                            sy.wait_ge(asem.h, a_pre + NTT * (fc + 1))
                            sy.dma_start(out=kexp[fc * 128:(fc + 1) * 128, :],
                                         in_=kvx[:, :]).then_inc(qsem.h, 16)
                            qsem.n += 16
                    q_kexp = qsem.n

                    @block.gpsimd
                    def _(g, q_kexp=q_kexp):
                        g.wait_ge(qsem.h, q_kexp)
                        g.collective_compute(
                            "AllGather", ALU.bypass,
                            replica_groups=[[0, 1], [2, 3], [4, 5], [6, 7]],
                            ins=[kexp.opt()], outs=[kg.opt()]).then_inc(ck.h, 1)
                        ck.n = 1
                elif wi == 2:
                    v_pre = vsem.n
                    q_pre = qsem.n

                    @block.tensor
                    def _(t, d_w=d_w, v_pre=v_pre):
                        t.wait_ge(dsem.h, d_w)
                        t.wait_ge(vsem.h, v_pre)
                        for tc in range(NQC):
                            for fb in range(2):
                                k = tc * 2 + fb
                                if k >= 4:
                                    t.wait_ge(vsem.h, v_pre + k - 3)
                                for dc in range(NDC):
                                    t.matmul(bank(pBa, k % 4),
                                             ub[:, dc * SL + tc * 128:][:, 0:128],
                                             wta[:, dc * D + fb * T:][:, 0:T],
                                             start=(dc == 0), stop=False)
                                ins = t.matmul(bank(pBa, k % 4), onesb[:, 0:128],
                                               bvrowb[0:1, fb * T:(fb + 1) * T],
                                               start=False, stop=True)
                                tsem.inc(ins)

                    @block.vector
                    def _(v, t_pre=t_pre, q_pre=q_pre):
                        for tc in range(NQC):
                            for fb in range(2):
                                k = tc * 2 + fb
                                v.wait_ge(tsem.h, t_pre + k + 1)
                                if tc >= 1 and fb == 0:
                                    v.wait_ge(qsem.h, q_pre + 16 * tc)
                                vsem.inc(v.tensor_copy(kvx[:, fb * T:(fb + 1) * T],
                                                       bank(pBa, k % 4)))

                    @block.sync
                    def _(sy, v_pre=v_pre):
                        for tc in range(NQC):
                            sy.wait_ge(vsem.h, v_pre + 2 * (tc + 1))
                            sy.dma_start(out=vexp[tc * 128:(tc + 1) * 128, :],
                                         in_=kvx[:, 0:1024]).then_inc(qsem.h, 16)
                            qsem.n += 16
                    q_vexp = qsem.n

                    @block.gpsimd
                    def _(g, q_vexp=q_vexp):
                        g.wait_ge(qsem.h, q_vexp)
                        g.collective_compute(
                            "AllGather", ALU.bypass,
                            replica_groups=[[0, 1], [2, 3], [4, 5], [6, 7]],
                            ins=[vexp.opt()], outs=[vg.opt()]).then_inc(cv.h, 1)
                        cv.n = 1

            t_postv = tsem.n

            @block.gpsimd
            def _(g, t_postv=t_postv):
                g.wait_ge(tsem.h, t_postv)
                for p in range(2):
                    vv = vh[p].rearrange("p (k c) -> p k c", c=65)
                    g.memset(vv[:, :, 64:65], 1.0)
                    g.memset(khb[p][64:65, :], 1.0)
                g.memset(srow[:, :], 1.0).then_inc(esem.h, 1)
                esem.n += 1

            if stop_after == "qkv":
                return
            NF = 8 * NQC
            tfill = {}
            ttp = {}
            tBsT = {}
            tBAV = {}
            tBrs = {}
            tBend = {}
            tAend = {}
            vbaseA = {}
            sbaseB = {}
            q_mst = {}
            prev_osc = [0]
            prev_vred_tail = [None, None]

            for s in range(NH + 1):
                hA = s if s < NH else None
                hB = s - 1 if s >= 1 else None
                pA_ = None if hA is None else hA % 2
                pB_ = None if hB is None else hB % 2

                if hA is not None:
                    kse = ksp[pA_]
                    loA = 64 * (hA % 2)

                    @block.sync
                    def _(sy, hA=hA, loA=loA, kse=kse, s=s):
                        if s == 0:
                            sy.wait_ge(ck.h, 1)
                            sy.wait_ge(cv.h, 1)
                            sy.wait_ge(esem.h, esem.n)
                        if hA >= 2:
                            sy.wait_ge(tsem.h, tAend[hA - 2])
                        sy.dma_start(out=khA[loA:loA + 64, 0:SL],
                                     in_=kg[64 * hA:64 * hA + 64, :]).then_inc(kse.h, 16)
                        sy.dma_start(out=khA[loA:loA + 64, SL:2 * SL],
                                     in_=kg[D + 64 * hA:D + 64 * hA + 64, :]).then_inc(kse.h, 16)
                        kse.n += 32
                    q_khA = ksp[pA_].n
                if hB is not None:
                    kbe = kbp[pB_]
                    loB = 64 * (hB % 2)
                    fcqB = hB // 2

                    @block.sync
                    def _(sy, hB=hB, loB=loB, fcqB=fcqB, kbe=kbe, pB_=pB_):
                        if hB >= 2:
                            sy.wait_ge(tsem.h, tBend[hB - 2])
                        sy.dma_start(out=khb[pB_][0:64, 0:SL],
                                     in_=kg[64 * hB:64 * hB + 64, :]).then_inc(kbe.h, 16)
                        sy.dma_start(out=khb[pB_][0:64, SL:2 * SL],
                                     in_=kg[D + 64 * hB:D + 64 * hB + 64, :]).then_inc(kbe.h, 16)
                        vv = vh[pB_].rearrange("p (k c) -> p k c", c=65)
                        sy.dma_start(out=vv[:, :, 0:64],
                                     in_=vg.rearrange("(k p) f -> p k f", p=128)
                                     [:, :, 64 * hB:64 * hB + 64]).then_inc(kbe.h, 16)
                        sy.dma_start(out=qtile[pB_][0:64, :],
                                     in_=qoz[loB:loB + 64, fcqB * SL:(fcqB + 1) * SL]
                                     ).then_inc(kbe.h, 16)
                        sy.wait_ge(qsem.h, q_mst[hB])
                        sy.dma_start(out=qtile[pB_][64:65, :],
                                     in_=mstore[hB:hB + 1, :]).then_inc(kbe.h, 16)
                        kbe.n += 80
                    q_kB = kbp[pB_].n

                vbase = vsem.n
                if hA is not None:
                    vbaseA[hA] = vbase

                def v_red(qc, f, vbase=vbase, hB=hB):
                    ex = 2 * (qc // 4) if hB is not None else 0
                    return vbase + 10 * qc + ex + f + 1

                def v_rfin(qc, vbase=vbase, hB=hB):
                    ex = 2 * (qc // 4) if hB is not None else 0
                    return vbase + 10 * qc + ex + 9

                def v_copy(qc, vbase=vbase, hB=hB):
                    ex = 2 * (qc // 4) if hB is not None else 0
                    return vbase + 10 * qc + ex + 10

                def v_recip(qb, vbase=vbase, hA=hA):
                    return vbase + (42 * qb + 41 if hA is not None else 2 * qb + 1)

                def v_osc(qb, vbase=vbase, hA=hA):
                    return vbase + (42 * qb + 42 if hA is not None else 2 * qb + 2)

                sbase = asem.n
                if hB is not None:
                    sbaseB[hB] = sbase

                def s_exp(qb, g, sbase=sbase):
                    return sbase + 17 * qb + g + 1

                def s_rsb(qb, sbase=sbase):
                    return sbase + 17 * qb + 17

                qpred = [None]

                @block.tensor
                def _(t, s=s, hA=hA, hB=hB, pA_=pA_, pB_=pB_,
                      v_red=v_red, v_rfin=v_rfin, v_copy=v_copy,
                      v_recip=v_recip, v_osc=v_osc, s_exp=s_exp, s_rsb=s_rsb,
                      vbase=vbase, sbase=sbase):
                    items = []
                    if hA is not None:
                        loA = 64 * (hA % 2)
                        fcqA = hA // 2

                        def mk_fill(qc, f, hA=hA, loA=loA, fcqA=fcqA):
                            def emit():
                                FF = 8 * qc + f
                                if FF == 0:
                                    t.wait_ge(ksp[hA % 2].h, q_khA)
                                    if hA == 0:
                                        pass
                                if FF < 2:
                                    if prev_vred_tail[f] is not None:
                                        t.wait_ge(vsem.h, prev_vred_tail[f])
                                elif f == 0 and FF % 8 == 0:
                                    t.wait_ge(vsem.h, v_copy(qc - 1))
                                else:
                                    t.wait_ge(vsem.h, v_red((FF - 2) // 8, (FF - 2) % 8))
                                q_sl = qoz[loA:loA + 64, fcqA * SL + qc * 128:][:, 0:128]
                                ins = t.matmul(pBa[:, (2 + f % 2) * 512:(3 + f % 2) * 512],
                                               q_sl, khA[loA:loA + 64, f * 512:(f + 1) * 512],
                                               start=True, stop=True)
                                tsem.inc(ins)
                                tfill[(hA, FF)] = tsem.n
                            return emit

                        def mk_tp(qc, hA=hA):
                            def emit():
                                t.wait_ge(vsem.h, v_rfin(qc))
                                tsem.inc(t.transpose(pBa[0:1, 1024:1152],
                                                     gsc[:, 0:1], idf[:, :]))
                                ttp[(hA, qc)] = tsem.n
                            return emit

                        for qc in range(NQC):
                            for f in range(8):
                                items.append(mk_fill(qc, f))
                            items.append(mk_tp(qc))
                    it = iter(items)

                    def inject(n):
                        for _i in range(n):
                            em = next(it, None)
                            if em is None:
                                return
                            em()

                    if hB is not None:
                        loB = 64 * (hB % 2)
                        kbe = kbp[pB_]
                        t.wait_ge(kbe.h, q_kB)

                        def attsl(g):
                            sl = g % 4
                            return att[sl // 2][:, (sl % 2) * 1024:(sl % 2 + 1) * 1024]

                        def sTq(qb, g, hB=hB, pB_=pB_):
                            ins = None
                            for j in range(2):
                                kc = 2 * g + j
                                ins = t.matmul(pAa[:, ((g % 2) * 2 + j) * 512:][:, 0:512],
                                               khb[pB_][0:65, kc * 128:(kc + 1) * 128],
                                               qtile[pB_][0:65, qb * T:(qb + 1) * T],
                                               start=True, stop=True)
                            tsem.inc(ins)
                            tBsT[(hB, qb, g)] = tsem.n
                        for qb in range(NTT):
                            if qb == 0:
                                t.wait_ge(asem.h, sbase)
                            else:
                                t.wait_ge(asem.h, s_exp(qb - 1, 14))
                            sTq(qb, 0)
                            if qb > 0:
                                t.wait_ge(asem.h, s_exp(qb - 1, 15))
                            sTq(qb, 1)
                            for g in range(16):
                                inject((3 if g < 4 else 2) if hA is not None else 0)
                                t.wait_ge(asem.h, s_exp(qb, g))
                                if g == 0:
                                    t.wait_ge(vsem.h,
                                              v_osc(qb - 1) if qb > 0 else prev_osc[0])
                                ins = None
                                for j in range(2):
                                    kc = 2 * g + j
                                    ins = t.matmul(pBa[0:65, 0:512],
                                                   vh[pB_][:, kc * 65:(kc + 1) * 65],
                                                   attsl(g)[:, j * 512:(j + 1) * 512],
                                                   start=(kc == 0), stop=(kc == NKC - 1))
                                tsem.inc(ins)
                                tBAV[(hB, qb, g)] = tsem.n
                                if g + 2 < 16:
                                    sTq(qb, g + 2)
                            t.wait_ge(vsem.h, v_recip(qb))
                            tsem.inc(t.matmul(pBa[:, 512:1024], onesf[64:65, 0:128],
                                              srow[64:65, :], start=True, stop=True))
                            tBrs[(hB, qb)] = tsem.n
                    inject(10 ** 9)
                    if hA is not None:
                        tAend[hA] = tsem.n
                    if hB is not None:
                        tBend[hB] = tsem.n

                if hB is not None:
                    @block.scalar
                    def _(s_, hB=hB):
                        def attsl(g):
                            sl = g % 4
                            return att[sl // 2][:, (sl % 2) * 1024:(sl % 2 + 1) * 1024]
                        for qb in range(NTT):
                            for g in range(16):
                                s_.wait_ge(tsem.h, tBsT[(hB, qb, g)])
                                if g >= 4:
                                    s_.wait_ge(tsem.h, tBAV[(hB, qb, g - 4)])
                                elif qb > 0:
                                    s_.wait_ge(tsem.h, tBAV[(hB, qb - 1, 12 + g)])
                                asem.inc(s_.activation(attsl(g),
                                                       pAa[:, (g % 2) * 1024:(g % 2 + 1) * 1024],
                                                       AF.Exp))
                            s_.wait_ge(tsem.h, tBrs[(hB, qb)])
                            asem.inc(s_.copy(rsb[0:64, :], pBa[0:64, 512:1024]))

                qpred[0] = qsem.n

                @block.vector
                def _(v, s=s, hA=hA, hB=hB, pB_=pB_, vbase=vbase, sbase=sbase,
                      v_red=v_red, v_rfin=v_rfin, v_copy=v_copy,
                      v_recip=v_recip, v_osc=v_osc, s_rsb=s_rsb, qpred=qpred):
                    def a_block(qc, hA=hA):
                        if qc == 0 and hA is not None and hA >= 1:
                            v.wait_ge(qsem.h, q_mst[hA - 1])
                        for f in range(8):
                            v.wait_ge(tsem.h, tfill[(hA, 8 * qc + f)])
                            vsem.inc(v.reduce_max(gcol[:, f:f + 1],
                                                  pBa[:, (2 + f % 2) * 512:(3 + f % 2) * 512],
                                                  axis=AX.X))
                        v.wait_ge(vsem.h, vsem.n)
                        vsem.inc(v.reduce_max(gsc[:, 0:1], gcol[:, 0:8], axis=AX.X,
                                              negate=True))
                        v.wait_ge(tsem.h, ttp[(hA, qc)])
                        vsem.inc(v.tensor_copy(mrow[0:1, qc * 128:(qc + 1) * 128],
                                               pBa[0:1, 1024:1152]))

                    def b_block(qb, hB=hB, pB_=pB_):
                        loB = 64 * (hB % 2)
                        fcqB = hB // 2
                        v.wait_ge(tsem.h, tBAV[(hB, qb, 15)])
                        vsem.inc(v.reciprocal(srow[64:65, :], pBa[64:65, 0:512]))
                        v.wait_ge(asem.h, s_rsb(qb))
                        if loB == 0:
                            vsem.inc(v.tensor_mul(
                                qoz[0:64, fcqB * SL + qb * T:][:, 0:T],
                                pBa[0:64, 0:512], rsb[0:64, :]))
                        else:
                            if qb >= 2:
                                v.wait_ge(qsem.h, qpred[0] + 16 * (qb - 1))
                            vsem.inc(v.tensor_mul(obuf2[qb % 2][:, :],
                                                  pBa[0:64, 0:512], rsb[0:64, :]))

                    if hA is not None and hB is not None:
                        for qb in range(NTT):
                            for qc in range(4 * qb, 4 * qb + 4):
                                a_block(qc)
                            b_block(qb)
                    elif hA is not None:
                        for qc in range(NQC):
                            a_block(qc)
                    else:
                        for qb in range(NTT):
                            b_block(qb)
                    if hB is not None:
                        assert vsem.n == v_osc(NTT - 1), (vsem.n, v_osc(NTT - 1))
                    else:
                        assert vsem.n == v_copy(NQC - 1), (vsem.n, v_copy(NQC - 1))

                if hA is not None:
                    prev_vred_tail[0] = v_copy(NQC - 1)
                    prev_vred_tail[1] = v_red(NQC - 1, 7)
                if hB is not None:
                    prev_osc[0] = v_osc(NTT - 1)

                @block.sync
                def _(sy, hA=hA, hB=hB, v_copy=v_copy, v_osc=v_osc):
                    if hB is not None and hB % 2 == 1:
                        fcqB = hB // 2
                        for qb in range(NTT):
                            sy.wait_ge(vsem.h, v_osc(qb))
                            sy.dma_start(out=qoz[64:128, fcqB * SL + qb * T:][:, 0:T],
                                         in_=obuf2[qb % 2][:, :]).then_inc(qsem.h, 16)
                            qsem.n += 16
                    if hA is not None:
                        sy.wait_ge(vsem.h, v_copy(NQC - 1))
                        sy.dma_start(out=mstore[hA:hA + 1, :],
                                     in_=mrow[0:1, :]).then_inc(qsem.h, 16)
                        qsem.n += 16
                if hA is not None:
                    q_mst[hA] = qsem.n

            t_heads = tsem.n

            @block.gpsimd
            def _(g, t_heads=t_heads):
                g.wait_ge(tsem.h, t_heads)
                g.dma_start(out=wta[:, :].rearrange("p (dc f) -> p dc f", f=D),
                            in_=woT.rearrange("(dc p) f -> p dc f", p=128)
                            ).then_inc(dsem.h, 16)
                dsem.n += 16
            d_w_wo = dsem.n
            v_wo0, q_wo0 = vsem.n, qsem.n

            @block.tensor
            def _(t, v_wo0=v_wo0, q_wo0=q_wo0, d_w_wo=d_w_wo):
                t.wait_ge(vsem.h, v_wo0)
                t.wait_ge(qsem.h, q_wo0)
                t.wait_ge(dsem.h, d_w_wo)
                for dc in range(NDC):
                    for tt in range(NTT):
                        k = dc * NTT + tt
                        if k >= 2:
                            t.wait_ge(vsem.h, v_wo0 + k - 1)
                        for fc in range(NDC):
                            ins = t.matmul(bank(pAa, k % 2),
                                           wta[:, fc * D + dc * 128:][:, 0:128],
                                           qoz[:, fc * SL + tt * T:][:, 0:T],
                                           start=(fc == 0), stop=(fc == NDC - 1))
                        tsem.inc(ins)
            t_wo0 = tsem.n - NDC * NTT

            @block.vector
            def _(v, t_wo0=t_wo0):
                for dc in range(NDC):
                    for tt in range(NTT):
                        k = dc * NTT + tt
                        v.wait_ge(tsem.h, t_wo0 + k + 1)
                        x_sl = xfm[:, dc * SL + tt * T:][:, 0:T]
                        vsem.inc(v.scalar_tensor_tensor(
                            x_sl, bank(pAa, k % 2), battn[:, 3 * NDC + dc:][:, 0:1],
                            x_sl, ALU.add, ALU.add))

        if stop_after != "p0":
            layernorm(fold_x=True, with_gate=True)
        if stop_after in ("u1", "c1"):
            t_d, v_d, a_d = tsem.n, vsem.n, asem.n

            @block.gpsimd
            def _(g):
                g.wait_ge(tsem.h, t_d)
                g.wait_ge(vsem.h, v_d)
                g.wait_ge(asem.h, a_d)
                if stop_after == "u1":
                    for dc in range(NDC):
                        g.dma_start(out=outT[dc * 128:(dc + 1) * 128, :],
                                    in_=ub[:, dc * SL:(dc + 1) * SL]).then_inc(dsem.h, 16)
                        dsem.n += 16
                else:
                    g.dma_start(out=outT[0:8, :], in_=cT[0:8, :]).then_inc(dsem.h, 16)
                    dsem.n += 16
                g.wait_ge(dsem.h, dsem.n)
        if stop_after not in ("p0", "ln1", "u1", "c1"):
            ffn_experts(0)
        if stop_after not in ("p0", "ln1", "u1", "c1", "ffn1"):
            layernorm(fold_x=False, with_gate=False)
            attention("qkv" if stop_after == "qkv" else "all")
        if stop_after not in ("p0", "ln1", "u1", "c1", "ffn1", "qkv", "attn"):
            layernorm(fold_x=True, with_gate=True)
            ffn_experts(1)

        t_fin, v_fin, a_fin = tsem.n, vsem.n, asem.n
        skip_out = stop_after in ("u1", "c1")

        @block.sync
        def _(sy):
            if skip_out:
                sy.wait_ge(dsem.h, dsem.n)
                return
            sy.wait_ge(vsem.h, v_fin)
            sy.wait_ge(tsem.h, t_fin)
            sy.wait_ge(asem.h, a_fin)
            for dc in range(NDC):
                sy.dma_start(out=outT[dc * 128:(dc + 1) * 128, :],
                             in_=xfm[:, dc * SL:(dc + 1) * SL]).then_inc(qsem.h, 16)
            qsem.n += NDC * 16
            sy.wait_ge(qsem.h, qsem.n)

    return nc


def kernel(L0_attn_x, L0_attn_Wq, L0_attn_bq, L0_attn_Wk, L0_attn_bk,
           L0_attn_Wv, L0_attn_bv, L0_attn_Wo, L0_attn_bo,
           L0_ffn_Wg, L0_ffn_bg, L0_ffn_W1, L0_ffn_W2, L0_ffn_W3):
    x = np.ascontiguousarray(np.asarray(L0_attn_x, dtype=np.float32)).reshape(B * S, D)
    com = dict(
        wqT=np.ascontiguousarray(np.asarray(L0_attn_Wq, np.float32).T),
        wkT=np.ascontiguousarray(np.asarray(L0_attn_Wk, np.float32).T),
        wvT=np.ascontiguousarray(np.asarray(L0_attn_Wv, np.float32).T),
        woT=np.ascontiguousarray(np.asarray(L0_attn_Wo, np.float32).T),
        bq=np.asarray(L0_attn_bq, np.float32), bk=np.asarray(L0_attn_bk, np.float32),
        bv=np.asarray(L0_attn_bv, np.float32), bo=np.asarray(L0_attn_bo, np.float32),
        wg=np.asarray(L0_ffn_Wg, np.float32), bg=np.asarray(L0_ffn_bg, np.float32),
        w1=np.ascontiguousarray(np.asarray(L0_ffn_W1, np.float32)),
        w2=np.ascontiguousarray(np.asarray(L0_ffn_W2, np.float32)),
        w3=np.ascontiguousarray(np.asarray(L0_ffn_W3, np.float32)),
    )
    in_maps = []
    for c in range(NCORE):
        xs = np.ascontiguousarray(x[c * SL:(c + 1) * SL, :].T)
        in_maps.append(dict(xT=xs, **com))
    nc = build_program()
    res = run_bass_kernel_spmd(nc, in_maps, core_ids=list(range(NCORE)))
    outs = [np.asarray(r["outT"]).T for r in res.results]
    return np.concatenate(outs, axis=0).reshape(B, S, D)
